# version 06
# baseline (speedup 1.0000x reference)
import numpy as np
from contextlib import ExitStack

import concourse.bass as bass
import concourse.mybir as mybir
from concourse.bass_utils import run_bass_kernel_spmd

F32 = mybir.dt.float32
BF16 = mybir.dt.bfloat16
AF = mybir.ActivationFunctionType
ALU = mybir.AluOpType
AX = mybir.AxisListType

T = 2048
TH = 1024
D = 2048
NDCH = 16
H = 16
DK = DV = M = 128
C = 128
NCH = TH // C
NG = 4
NGRP = H // NG
E_MOE = 16
I_MOE = 1408
I_SH = 5632
NEXP = 20
SCALE = float(DK) ** -0.5
LN_EPS = 1e-5
RMS_EPS = 1e-5
L2_EPS = 1e-6

W_NAMES = ["L0_attn_Wq", "L0_attn_Wk", "L0_attn_Wv", "L0_attn_Ww", "L0_attn_cq",
           "L0_attn_ck", "L0_attn_cv", "L0_attn_Wf1", "L0_attn_Wf2", "L0_attn_Wb",
           "L0_attn_Wg1", "L0_attn_Wg2", "L0_attn_bg2", "L0_attn_norm_w",
           "L0_attn_Wo", "L0_ffn_gate_w", "L0_ffn_sgate_w", "L0_ffn_w13",
           "L0_ffn_w2", "L0_ffn_shared_gate_up_w", "L0_ffn_shared_down_w"]


class KB:
    ENGS = ("T", "V", "S", "G", "Y")
    NYS = 10

    def __init__(self, nc, ctx):
        self.nc = nc
        E = ctx.enter_context
        self.ops = {e: [] for e in self.ENGS}
        self.sem = {}
        self.cnt = {}
        names = (["DT", "DV", "DS", "DG", "QG"]
                 + [f"QY{i}" for i in range(self.NYS)]
                 + [f"QS{i}" for i in range(self.NYS)]
                 + [f"PS{i}" for i in range(8)])
        for name in names:
            self.sem[name] = E(nc.semaphore(name))
            self.cnt[name] = 0
        self.done_of = {"T": "DT", "V": "DV", "S": "DS", "G": "DG"}
        self.q_of = {"G": "QG"}
        self.ynext = 0
        self.snext = 0
        self.trace = []

    def op(self, eng, f, inc=False, wait=None, drain=None, bank=None):
        semname = self.done_of.get(eng)
        if inc:
            self.cnt[semname] += 1
        sem_obj = self.sem.get(semname) if inc else None
        bank_sem = None
        if bank is not None:
            bn = f"PS{bank}"
            self.cnt[bn] += 1
            bank_sem = self.sem[bn]
        wait = self._flat(wait)
        waits = [(self.sem[s], v) for (s, v) in wait]
        self.trace.append((eng, wait, semname if inc else None, 1))
        if drain is None:
            drain = eng in ("V", "S", "G")

        if bank_sem is not None:
            drain = True

        def run(e, f=f, waits=waits, sem_obj=sem_obj, drain=drain, bank_sem=bank_sem):
            for s, v in waits:
                e.wait_ge(s, v)
            inst = f(e)
            if sem_obj is not None:
                inst.then_inc(sem_obj, 1)
            if drain:
                d = e.drain()
                if bank_sem is not None:
                    d.then_inc(bank_sem, 1)
        self.ops[eng].append(run)
        return self.cnt[semname] if inc else None

    @staticmethod
    def _flat(wait):
        out = []
        for w in (wait or []):
            if isinstance(w, tuple) and len(w) == 2 and isinstance(w[0], str):
                out.append(w)
            else:
                out.extend(w)
        return [(s, v) for (s, v) in out if v > 0]

    def dma(self, eng, out, in_, wait=None, **kw):
        wait = self._flat(wait)
        if eng == "Y":
            q = f"QY{self.ynext}"
            self.ynext = (self.ynext + 1) % self.NYS
        elif eng == "S":
            q = f"QS{self.snext}"
            self.snext = (self.snext + 1) % self.NYS
        else:
            q = self.q_of[eng]
        if self.cnt[q] > 0:
            wait = wait + [(q, self.cnt[q])]
        self.cnt[q] += 16
        sem_obj = self.sem[q]
        waits = [(self.sem[s], v) for (s, v) in wait]
        self.trace.append((eng, wait, q, 16))

        def run(e, out=out, in_=in_, kw=kw, waits=waits, sem_obj=sem_obj):
            for s, v in waits:
                e.wait_ge(s, v)
            e.dma_start(out=out, in_=in_, **kw).then_inc(sem_obj, 16)
        self.ops[eng].append(run)
        if eng == "Y":
            return tuple((f"QY{i}", self.cnt[f"QY{i}"]) for i in range(self.NYS))
        if eng == "S":
            return tuple((f"QS{i}", self.cnt[f"QS{i}"]) for i in range(self.NYS))
        return ((q, self.cnt[q]),)

    def wait(self, eng, sem_name, value):
        if value > 0:
            self.trace.append((eng, [(sem_name, value)], None, 0))
            self.ops[eng].append(
                lambda e, s=self.sem[sem_name], v=value: e.wait_ge(s, v))

    def waitt(self, eng, token):
        for (sname, v) in token:
            self.wait(eng, sname, v)

    def c(self, sem_name):
        return self.cnt[sem_name]


class PsumPool:

    def __init__(self, kb, tiles):
        self.kb = kb
        self.tiles = tiles
        self.rr = 0

    def acquire(self):
        b = self.rr
        self.rr = (self.rr + 1) % 8
        wait = [(f"PS{b}", self.kb.cnt[f"PS{b}"])]
        return b, self.tiles[b], wait


def build_program(stop_after=None, n_half=2, do_scratch=True):
    nc = bass.Bass("TRN2", target_bir_lowering=False)

    x_d = nc.dram_tensor("x", [T, D], F32, kind="ExternalInput").ap()
    shapes = {
        "L0_attn_Wq": [D, D], "L0_attn_Wk": [D, D], "L0_attn_Wv": [D, D],
        "L0_attn_Ww": [D, D], "L0_attn_cq": [D, 4], "L0_attn_ck": [D, 4],
        "L0_attn_cv": [D, 4], "L0_attn_Wf1": [D, 128], "L0_attn_Wf2": [128, D],
        "L0_attn_Wb": [D, 16], "L0_attn_Wg1": [D, 128], "L0_attn_Wg2": [128, D],
        "L0_attn_bg2": [D], "L0_attn_norm_w": [128], "L0_attn_Wo": [D, D],
        "L0_ffn_gate_w": [16, D], "L0_ffn_sgate_w": [1, D],
        "L0_ffn_w13": [E_MOE, 2 * I_MOE, D], "L0_ffn_w2": [E_MOE, D, I_MOE],
        "L0_ffn_shared_gate_up_w": [2 * I_SH, D], "L0_ffn_shared_down_w": [D, I_SH],
    }
    w_d = {n: nc.dram_tensor(n, s, F32, kind="ExternalInput").ap()
           for n, s in shapes.items()}
    out_d = nc.dram_tensor("out", [T, D], F32, kind="ExternalOutput").ap()

    scr_shapes = {
        "wq": [D, D], "wk": [D, D], "wv": [D, D], "ww": [D, D],
        "wf1": [D, 128], "wf2": [128, D], "wb": [D, 16],
        "wg1": [D, 128], "wg2": [128, D], "wo": [D, D],
        "gatew": [16, D],
        "w13": [E_MOE, 2 * I_MOE, D], "w2": [E_MOE, D, I_MOE],
        "sgu": [2 * I_SH, D], "sdw": [D, I_SH],
    }
    scr = {n: nc.dram_tensor("scr_" + n, s, BF16, kind="Internal").ap()
           for n, s in scr_shapes.items()}
    scrT_shapes = {
        "w13T": [E_MOE, D, 2 * I_MOE], "w2T": [E_MOE, I_MOE, D],
        "sguT": [D, 2 * I_SH], "sdwT": [I_SH, D],
    }
    scrT = {n: nc.dram_tensor("scrT_" + n, s, BF16, kind="Internal").ap()
            for n, s in scrT_shapes.items()}

    dbg_taps = []

    with ExitStack() as ctx:
        E = ctx.enter_context
        kb = KB(nc, ctx)

        xnT = E(nc.sbuf_tensor("xnT", [128, NDCH * TH], BF16))
        acc = E(nc.sbuf_tensor("acc", [128, NDCH * TH], F32))
        big = E(nc.sbuf_tensor("big", [128, 20480], BF16))
        qf = big[:, 0:4096]
        kf = big[:, 4096:8192]
        vf = big[:, 8192:12288]
        wff = big[:, 12288:16384]
        o_grp = big[:, 16384:20480]
        hs = big[:, 0:11264]
        sgtmp = big[:, 11264:13312]

        slab = [E(nc.sbuf_tensor(f"slab{i}", [128, 4096], BF16)) for i in range(2)]
        wp = [slab[0][:, 0:2048], slab[1][:, 0:2048]]
        upslab = [slab[0], slab[1]]
        dnslab = [slab[0][:, 0:2816], slab[1][:, 0:2816]]
        wp_free = up_free = dn_free = slab_free = [0, 0]

        x_tok = acc[:, 0:2048]
        x_tok_e = big[:, 0:4096].bitcast(F32)
        xn_tok = E(nc.sbuf_tensor("xn_tok", [128, D], BF16))
        raw = E(nc.sbuf_tensor("raw", [128, 528], F32))
        conv_acc = E(nc.sbuf_tensor("conv_acc", [128, 512], F32))
        tail3 = E(nc.sbuf_tensor("tail3", [128, 4], F32))
        tails = E(nc.sbuf_tensor("tails", [128, 4 * 16 * 3], F32))
        p_sb = E(nc.sbuf_tensor("p_sb", [128, 512], F32))
        lam_c = E(nc.sbuf_tensor("lam_c", [128, 512], F32))
        lam_f = E(nc.sbuf_tensor("lam_f", [128, 512], F32))
        eu = E(nc.sbuf_tensor("eu", [128, 512], BF16))
        rfac = E(nc.sbuf_tensor("rfac", [128, 512], BF16))
        elam_row = E(nc.sbuf_tensor("elam_row", [1, 512], BF16))
        elam_col = E(nc.sbuf_tensor("elam_col", [128, 16], F32))
        gpre1 = E(nc.sbuf_tensor("gpre1", [128, TH], BF16))
        gpre2 = E(nc.sbuf_tensor("gpre2", [128, TH], BF16))
        Sk = E(nc.sbuf_tensor("Sk", [128, H * M], BF16))
        Sv = E(nc.sbuf_tensor("Sv", [128, H * DV], BF16))
        beta_all = E(nc.sbuf_tensor("beta_all", [128, NCH * 16], F32))
        coefT = E(nc.sbuf_tensor("coefT", [32, TH], BF16))
        coef_bc = E(nc.sbuf_tensor("coef_bc", [128, TH], BF16))
        st1 = E(nc.sbuf_tensor("st1", [128, 128], F32))
        wfg1 = E(nc.sbuf_tensor("wfg1", [128, 2048], BF16))
        ktok4 = wfg1[:, 0:512]
        vtok4 = wfg1[:, 512:1024]
        wn4 = wfg1[:, 1024:1536]
        u4 = wfg1[:, 1536:2048]
        r4 = E(nc.sbuf_tensor("r4", [128, 512], BF16))
        am4 = E(nc.sbuf_tensor("am4", [128, 512], BF16))
        st14 = E(nc.sbuf_tensor("st14", [128, 512], F32))
        st24 = E(nc.sbuf_tensor("st24", [128, 512], F32))
        coef_row = st24[:].bitcast(BF16)[0:1, 0:TH]
        ac4 = E(nc.sbuf_tensor("ac4", [128, 512], BF16))
        af4 = E(nc.sbuf_tensor("af4", [128, 512], BF16))
        o4 = af4
        o24 = E(nc.sbuf_tensor("o24", [128, 512], BF16))
        sktmp4 = ac4
        wtok4 = wn4
        pm4 = am4
        acT4 = ac4
        afT4 = af4
        uT4 = u4
        eg4 = p_sb[:].bitcast(BF16)[:, 0:512]
        cols2 = E(nc.sbuf_tensor("cols2", [128, 64], F32))
        wsum4 = cols2[:, 0:16]
        wrs4 = cols2[:, 16:32]
        nmx4 = cols2[:, 32:48]
        smr4 = cols2[:, 48:64]
        ebg = E(nc.sbuf_tensor("ebg", [128, 16], F32))
        cols = E(nc.sbuf_tensor("cols", [128, 20], F32))
        wsum = cols[:, 0:1]
        wrs = cols[:, 1:2]
        nmx = cols[:, 2:3]
        smr = cols[:, 3:4]
        sx = cols[:, 4:6]
        meanc = cols[:, 6:7]
        varc = cols[:, 7:8]
        rstdc = cols[:, 8:9]
        thr_t = cols[:, 9:10]
        m8_t = cols[:, 10:18]
        rrow = am4[0:1, 0:512]
        lrow = st14[0:1, 0:512]
        rw_t = E(nc.sbuf_tensor("rw_t", [128, 16], F32))
        cf_t = E(nc.sbuf_tensor("cf_t", [128, 17], F32))
        ctaps = E(nc.sbuf_tensor("ctaps", [128, 3 * 64], F32))
        nbg2 = E(nc.sbuf_tensor("nbg2", [128, 16], F32))
        normw = E(nc.sbuf_tensor("normw", [128, 1], F32))
        wb_sb = wfg1[:, 0:256]
        gw_sb = wfg1[:, 256:512]
        sgw_sb = wfg1[:, 512:528]
        wf2g = E(nc.sbuf_tensor("wf2g", [128, 512], BF16))
        wg2g = E(nc.sbuf_tensor("wg2g", [128, 512], BF16))
        ident_b = E(nc.sbuf_tensor("ident_b", [128, 128], BF16))
        ident_f = E(nc.sbuf_tensor("ident_f", [128, 128], F32))
        mask01 = E(nc.sbuf_tensor("mask01", [128, 128], BF16))
        Lhalf = E(nc.sbuf_tensor("Lhalf", [128, 128], F32))
        Lfull = E(nc.sbuf_tensor("Lfull", [128, 128], F32))
        Ustrict = E(nc.sbuf_tensor("Ustrict", [128, 128], F32))
        ones_cb = E(nc.sbuf_tensor("ones_cb", [128, 1], BF16))
        ones_cf = E(nc.sbuf_tensor("ones_cf", [128, 1], F32))
        ones_1b = E(nc.sbuf_tensor("ones_1b", [1, 128], BF16))
        cst = E(nc.sbuf_tensor("cst", [128, 4], F32))
        b_lneps = cst[:, 0:1]
        b_l2eps = cst[:, 1:2]
        b_rmse = cst[:, 2:3]
        b_hl128 = cst[:, 3:4]

        psum = [E(nc.psum_tensor(f"ps{i}", [128, 512], F32)) for i in range(8)]
        pool = PsumPool(kb, psum)

        pump_nat = big[:, 13312:16128]
        pump_st = [big[:, 16128:18176], big[:, 18176:20224]]

        class Pump:
            def __init__(self):
                self.tiles = []
                self.ti = 0
                self.done_tok = {}
                self.load_tok = None
                self.tr_done = 0
                self.cur = None
                self.par = 0
                self.st_free = [(), ()]

            def add(self, tag, src, dstfn, C):
                self.tiles.append((tag, src, dstfn, C))

            def finished(self):
                return self.ti >= len(self.tiles) and self.cur is None

            def step(self):
                if self.cur is None:
                    if self.ti >= len(self.tiles):
                        return False
                    tag, src, dstfn, C = self.tiles[self.ti]
                    self.ti += 1
                    self.load_tok = kb.dma("S", pump_nat[:, 0:C], src,
                                           wait=[scrw[tag.split("@")[0]],
                                                 ("DT", self.tr_done)])
                    self.cur = (tag, dstfn, C, 0)
                    return True
                tag, dstfn, C, c0 = self.cur
                w = min(2048, C - c0)
                nch = (w + 127) // 128
                st = pump_st[self.par]
                banks = []
                for cc in range(nch):
                    if cc % 4 == 0:
                        bb, pp, w8 = pool.acquire()
                        banks.append((bb, pp, w8))
                    bb, pp, w8 = banks[-1]
                    nsl = pump_nat[:, c0 + cc * 128: c0 + (cc + 1) * 128]
                    kb.op("T", lambda t, pp=pp, cc=cc, nsl=nsl: t.matmul(
                        pp[:, (cc % 4) * 128:((cc % 4) + 1) * 128], nsl, ident_b[:],
                        start=True, stop=True),
                        wait=(list(w8) + [self.load_tok]) if cc % 4 == 0 else None)
                self.tr_done = kb.op("T", lambda t: t.drain(), inc=True)
                for bi_, (bb, pp, w8) in enumerate(banks):
                    wdt = min(512, w - bi_ * 512)
                    dstv = st[:, bi_ * 512: bi_ * 512 + wdt]
                    kb.op("S", lambda sc, dstv=dstv, pp=pp, wdt=wdt: sc.activation(
                        dstv, pp[:, 0:wdt], AF.Copy),
                        wait=[("DT", self.tr_done), self.st_free[self.par]],
                        inc=True, bank=bb)
                ev = kb.c("DS")
                dst = dstfn(c0, w)
                srcv = st[:].rearrange("p (j c) -> p j c", c=128)[:, 0:nch, :]
                tok = kb.dma("S", dst, srcv, wait=[("DS", ev)])
                self.st_free[self.par] = tok
                self.done_tok[tag] = tok
                self.par ^= 1
                c0 += w
                self.cur = None if c0 >= C else (tag, dstfn, C, c0)
                return True

            def emit(self, n):
                for _ in range(n):
                    if not self.step():
                        return

            def flush(self):
                while self.step():
                    pass

        pump = Pump()
        def dstfn_factory(tgt, rows128, c_off):
            def f(c0, w, tgt=tgt, c_off=c_off):
                return tgt[c_off + c0: c_off + c0 + w,
                           rows128 * 128:(rows128 + 1) * 128].rearrange(
                    "(j p) c -> p j c", p=128)
            return f
        for e in range(E_MOE):
            for r in range(22):
                pump.add(f"w13_{e}@t", scr["w13"][e][r * 128:(r + 1) * 128, :],
                         dstfn_factory(scrT["w13T"][e], r, 0), D)
            for r in range(16):
                pump.add(f"w2_{e}@t", scr["w2"][e][r * 128:(r + 1) * 128, :],
                         dstfn_factory(scrT["w2T"][e], r, 0), I_MOE)
        for r in range(88):
            pump.add("sgu@t", scr["sgu"][r * 128:(r + 1) * 128, :],
                     dstfn_factory(scrT["sguT"], r, 0), D)
        for r in range(16):
            for hh in range(2):
                pump.add("sdw@t", scr["sdw"][r * 128:(r + 1) * 128,
                                            hh * 2816:(hh + 1) * 2816],
                         dstfn_factory(scrT["sdwT"], r, hh * 2816), 2816)

        def consts(g):
            for t_, fill, cmp in ((ident_b, 1.0, ALU.not_equal),
                                  (ident_f, 1.0, ALU.not_equal)):
                g.memset(t_[:], 0.0)
                g.drain()
                g.affine_select(out=t_[:], in_=t_[:], compare_op=cmp, fill=fill,
                                base=0, pattern=[[-1, 128]], channel_multiplier=1)
                g.drain()
            for t_, keepval, fillval in ((mask01, 1.0, 0.0), (Lfull, 1.0, 0.0),
                                         (Lhalf, 0.5, -0.5)):
                g.memset(t_[:], keepval)
                g.drain()
                g.affine_select(out=t_[:], in_=t_[:], compare_op=ALU.is_ge,
                                fill=fillval, base=0, pattern=[[1, 128]],
                                channel_multiplier=-1)
                g.drain()
            g.memset(Ustrict[:], 1.0)
            g.drain()
            g.affine_select(out=Ustrict[:], in_=Ustrict[:], compare_op=ALU.is_gt,
                            fill=0.0, base=0, pattern=[[-1, 128]], channel_multiplier=1)
            g.drain()
            g.memset(ones_cb[:], 1.0)
            g.memset(ones_cf[:], 1.0)
            g.memset(ones_1b[:], 1.0)
            g.memset(cst[:, 0:1], LN_EPS)
            g.memset(cst[:, 1:2], L2_EPS)
            g.memset(cst[:, 2:3], 128.0 * RMS_EPS)
            g.memset(cst[:, 3:4], 0.5 * float(np.log(128.0)))
            g.memset(tails[:], 0.0)
            g.memset(cols2[:], 1.0)
            g.memset(elam_col[:], 1.0)
            g.memset(Sk[:], 0.0)
            return g.memset(Sv[:], 0.0)
        g_consts = kb.op("G", consts, inc=True)

        for pi, wn in enumerate(["L0_attn_cq", "L0_attn_ck", "L0_attn_cv"]):
            src = w_d[wn].rearrange("(j p) k -> p j k", p=128)
            dst = ctaps[:, pi * 64:(pi + 1) * 64].rearrange("p (j k) -> p j k", k=4)
            kb.dma("G", dst, src)
        g_bg2 = kb.dma("G", nbg2[:], w_d["L0_attn_bg2"].rearrange("(h p) -> p h", p=128), allow_slow_non_contiguous=True)
        g_nw = kb.dma("G", normw[:], w_d["L0_attn_norm_w"].rearrange("(p one) -> p one", p=128))
        v_nbg = kb.op("V", lambda v: v.tensor_scalar_mul(nbg2[:], nbg2[:], -1.0),
              wait=[g_bg2], inc=True)
        kb.op("S", lambda sc: sc.activation(ebg[:], nbg2[:], AF.Exp),
              wait=[("DV", v_nbg)])

        scrw = {}
        def scratch_write(tag, dst, src):
            if not do_scratch:
                scrw[tag] = ()
                return
            scrw[tag] = kb.dma("G", dst, src)
        for tag, wn in [("wq", "L0_attn_Wq"), ("wk", "L0_attn_Wk"),
                        ("wv", "L0_attn_Wv"), ("ww", "L0_attn_Ww"),
                        ("wf1", "L0_attn_Wf1"), ("wf2", "L0_attn_Wf2"),
                        ("wb", "L0_attn_Wb"), ("wg1", "L0_attn_Wg1"),
                        ("wg2", "L0_attn_Wg2"), ("gatew", "L0_ffn_gate_w"),
                        ("wo", "L0_attn_Wo")]:
            scratch_write(tag, scr[tag], w_d[wn])
        for e in range(E_MOE):
            scratch_write(f"w13_{e}", scr["w13"][e], w_d["L0_ffn_w13"][e])
            scratch_write(f"w2_{e}", scr["w2"][e], w_d["L0_ffn_w2"][e])
        scratch_write("sgu", scr["sgu"], w_d["L0_ffn_shared_gate_up_w"])
        scratch_write("sdw", scr["sdw"], w_d["L0_ffn_shared_down_w"])

        def chunked_rows(ap):
            return ap.rearrange("(j p) c -> p j c", p=128)


        if stop_after == "consts":
            dbg_taps.extend([ident_b, mask01, Lhalf, Lfull, Ustrict])
        st_tok = ()
        for half in range(0 if stop_after == "consts" else n_half):
            th0 = half * TH

            prev_vxn = 0
            for tc in range(NCH):
                xb = x_tok
                yx = kb.dma("Y", xb[:], x_d[th0 + tc * 128: th0 + (tc + 1) * 128, :],
                            wait=([("DV", kb.c("DV")), ("DS", kb.c("DS")), ("DT", kb.c("DT"))]
                                  + [st_tok]) if (tc >= 1 or half > 0) else None)
                kb.waitt("S", yx)
                kb.wait("S", "DV", prev_vxn)
                s_sums = kb.op("S", lambda s, xb=xb: s.activation(
                    xn_tok[:], xb[:], AF.Square, accum_out=sx[:, 1:2]), inc=True)
                kb.waitt("V", yx)
                kb.op("V", lambda v, xb=xb: v.reduce_sum(sx[:, 0:1], xb[:], axis=AX.X))
                kb.op("V", lambda v: v.drain())
                if stop_after == "ln_sums":
                    dbg_taps.append(cols)
                    break
                kb.wait("V", "DS", s_sums)
                kb.op("V", lambda v: v.tensor_scalar_mul(meanc[:], sx[:, 0:1], 1.0 / D))
                kb.op("V", lambda v: v.tensor_scalar_mul(varc[:], sx[:, 1:2], 1.0 / D))
                kb.op("V", lambda v: v.drain())
                kb.op("V", lambda v: v.tensor_tensor(st1[:, 0:1], meanc[:], meanc[:], ALU.mult))
                kb.op("V", lambda v: v.drain())
                v_var = kb.op("V", lambda v: v.tensor_tensor(
                    varc[:], varc[:], st1[:, 0:1], ALU.subtract), inc=True)
                kb.wait("S", "DV", v_var)
                kb.op("S", lambda s: s.activation(rstdc[:], varc[:], AF.Ln, bias=b_lneps))
                s_rstd = kb.op("S", lambda s: s.activation(
                    rstdc[:], rstdc[:], AF.Exp, scale=-0.5), inc=True)
                kb.wait("V", "DS", s_rstd)
                v_xn = kb.op("V", lambda v, xb=xb: v.tensor_scalar(
                    xn_tok[:], xb[:], meanc[:], rstdc[:], ALU.subtract, ALU.mult), inc=True)
                prev_vxn = v_xn
                if stop_after == "ln":
                    dbg_taps.extend([xn_tok, cols])
                    break
                for bq in range(4):
                    b, ps, w8 = pool.acquire()
                    kb.wait("T", "DG", g_consts)
                    kb.wait("T", "DV", v_xn)
                    for jj in range(4):
                        j = bq * 4 + jj
                        src_sl = xn_tok[:, j * 128:(j + 1) * 128]
                        kb.op("T", lambda t, ps=ps, jj=jj, src_sl=src_sl: t.matmul(
                            ps[:, jj * 128:(jj + 1) * 128], src_sl, ident_b[:],
                            start=True, stop=True), wait=w8 if jj == 0 else None)
                    t_tr = kb.op("T", lambda t: t.drain(), inc=True)
                    dst = xnT[:].rearrange("p (j t) -> p j t", j=NDCH)[
                        :, bq * 4:(bq + 1) * 4, tc * 128:(tc + 1) * 128]
                    src = ps[:].rearrange("p (j t) -> p j t", j=4)
                    s_ev = kb.op("S", lambda s, dst=dst, src=src: s.activation(
                        dst, src, AF.Copy), wait=[("DT", t_tr)], inc=True, bank=b)

            if stop_after in ("ln", "ln_sums"):
                break
            xn_done = kb.c("DS")
            if stop_after == "xn":
                dbg_taps.append(xnT)
                break

            y_wb = kb.dma("Y", wb_sb.rearrange("p (j c) -> p j c", j=16),
                          chunked_rows(scr["wb"]),
                          wait=[scrw["wb"], ("DT", kb.c("DT"))])
            y_gw = None
            for j in range(16):
                y_gw = kb.dma("Y", gw_sb[:, j * 16:(j + 1) * 16],
                              scr["gatew"][:, j * 128:(j + 1) * 128],
                              wait=[scrw["gatew"]] if j == 0 else None,
                              transpose=True)
            g_sgw = kb.dma("G", sgw_sb.rearrange("p (one j) -> p one j", one=1),
                           w_d["L0_ffn_sgate_w"].rearrange("one (j p) -> p one j", p=128),
                           allow_slow_non_contiguous=True,
                           wait=[("DT", kb.c("DT"))])
            prev_vct = 0
            for tc in range(NCH):
                b, ps, w8 = pool.acquire()
                kb.wait("T", "DS", xn_done)
                kb.waitt("T", y_wb)
                kb.waitt("T", y_gw)
                kb.waitt("T", g_sgw)
                for j in range(NDCH):
                    xsl = xnT[:, j * TH + tc * 128: j * TH + (tc + 1) * 128]
                    kb.op("T", lambda t, ps=ps, j=j, xsl=xsl: t.matmul(
                        ps[:, 0:16], xsl, wb_sb[:, j * 16:(j + 1) * 16],
                        start=(j == 0), stop=(j == NDCH - 1)), wait=w8 if j == 0 else None)
                for j in range(NDCH):
                    xsl = xnT[:, j * TH + tc * 128: j * TH + (tc + 1) * 128]
                    kb.op("T", lambda t, ps=ps, j=j, xsl=xsl: t.matmul(
                        ps[:, 16:32], xsl, gw_sb[:, j * 16:(j + 1) * 16],
                        start=(j == 0), stop=(j == NDCH - 1)))
                for j in range(NDCH):
                    xsl = xnT[:, j * TH + tc * 128: j * TH + (tc + 1) * 128]
                    kb.op("T", lambda t, ps=ps, j=j, xsl=xsl: t.matmul(
                        ps[:, 32:33], xsl, sgw_sb[:, j:j + 1],
                        start=(j == 0), stop=(j == NDCH - 1)))
                t_b = kb.op("T", lambda t: t.drain(), inc=True)
                kb.wait("S", "DT", t_b)
                kb.wait("S", "DV", prev_vct)
                s_eb = kb.op("S", lambda s, ps=ps: s.activation(
                    rw_t[:], ps[:, 0:16], AF.Exp, scale=-1.0), inc=True, bank=b)
                kb.wait("V", "DS", s_eb)
                kb.op("V", lambda v: v.tensor_scalar_add(rw_t[:], rw_t[:], 1.0))
                kb.op("V", lambda v: v.drain())
                dstb = beta_all[:, tc * 16:(tc + 1) * 16]
                kb.op("V", lambda v, dstb=dstb: v.reciprocal(dstb, rw_t[:]))
                s_er = kb.op("S", lambda s, ps=ps: s.activation(
                    cf_t[:, 0:16], ps[:, 16:32], AF.Exp, accum_out=thr_t[:]), inc=True, bank=b)
                kb.wait("V", "DS", s_er)
                kb.op("V", lambda v: v.reciprocal(smr[:], thr_t[:]))
                kb.op("V", lambda v: v.drain())
                kb.op("V", lambda v: v.tensor_scalar_mul(rw_t[:], cf_t[:, 0:16], smr[:]))
                kb.op("V", lambda v: v.drain())
                kb.op("V", lambda v: v.max(out=m8_t[:], in_=rw_t[:]))
                kb.op("V", lambda v: v.drain())
                kb.op("V", lambda v: v.tensor_single_scalar(thr_t[:], m8_t[:, 3:4], 1.0, ALU.mult))
                kb.op("V", lambda v: v.drain())
                kb.op("V", lambda v: v.tensor_single_scalar(cf_t[:, 0:16], rw_t[:], thr_t[:], ALU.is_ge))
                kb.op("V", lambda v: v.drain())
                kb.op("V", lambda v: v.tensor_tensor(cf_t[:, 0:16], cf_t[:, 0:16], rw_t[:], ALU.mult))
                s_eg = kb.op("S", lambda s, ps=ps: s.activation(
                    st1[:, 0:1], ps[:, 32:33], AF.Exp, scale=-1.0), inc=True, bank=b)
                kb.wait("V", "DS", s_eg)
                kb.op("V", lambda v: v.tensor_scalar_add(st1[:, 0:1], st1[:, 0:1], 1.0))
                kb.op("V", lambda v: v.drain())
                v_sgv = kb.op("V", lambda v: v.reciprocal(cf_t[:, 16:17], st1[:, 0:1]), inc=True)
                b2, ps2, w82 = pool.acquire()
                kb.wait("T", "DV", v_sgv)
                kb.op("T", lambda t, ps2=ps2: t.matmul(
                    ps2[0:17, 0:128], cf_t[:], ident_f[:], start=True, stop=True), wait=w82)
                t_ct = kb.op("T", lambda t: t.drain(), inc=True)
                dstc = coefT[0:17, tc * 128:(tc + 1) * 128]
                v_ct = kb.op("V", lambda v, dstc=dstc, ps2=ps2: v.tensor_copy(
                    dstc, ps2[0:17, 0:128]), wait=[("DT", t_ct)], inc=True, bank=b2)
                prev_vct = v_ct

            for (stag, gdst) in (("wf1", gpre1), ("wg1", gpre2)):
                y_g1 = kb.dma("Y", wfg1[:].rearrange("p (j c) -> p j c", j=16),
                              chunked_rows(scr[stag]),
                              wait=[scrw[stag], ("DT", kb.c("DT"))])
                for tt in range(2):
                    b, ps, w8 = pool.acquire()
                    kb.waitt("T", y_g1)
                    for j in range(NDCH):
                        xsl = xnT[:, j * TH + tt * 512: j * TH + (tt + 1) * 512]
                        kb.op("T", lambda t, ps=ps, j=j, xsl=xsl: t.matmul(
                            ps[:], wfg1[:, j * 128:(j + 1) * 128], xsl,
                            start=(j == 0), stop=(j == NDCH - 1)), wait=w8 if j == 0 else None)
                    t_g = kb.op("T", lambda t: t.drain(), inc=True)
                    dstsl = gdst[:, tt * 512:(tt + 1) * 512]
                    s_ev = kb.op("S", lambda s, dstsl=dstsl, ps=ps: s.activation(
                        dstsl, ps[:], AF.Copy), wait=[("DT", t_g)], inc=True, bank=b)

            phaseB_V = kb.c("DV")
            phaseB_S = kb.c("DS")
            if stop_after == "phaseB":
                dbg_taps.extend([beta_all, coefT[0:17, :], gpre1, gpre2])
                break

            for grp in range(NGRP):
                wsel = {0: "wq", 1: "wk", 2: "wv", 3: "ww"}
                dsel = {0: qf, 1: kf, 2: vf, 3: wff}
                for pi in range(4):
                    for hin in range(NG):
                        head = grp * NG + hin
                        bi = (pi * NG + hin) % 2
                        wbuf = wp[bi]
                        stag = wsel[pi]
                        src = scr[stag].rearrange("(j p) c -> p j c", p=128)[
                            :, :, head * 128:(head + 1) * 128]
                        yw = kb.dma("Y" if bi == 0 else "S", wbuf[:], src,
                                    wait=[scrw[stag], ("DT", wp_free[bi])])
                        tsl = tails[:, (pi * 16 + head) * 3:(pi * 16 + head) * 3 + 3]
                        cbase = {0: 0, 1: 64, 2: 128, 3: 128}[pi]
                        for tt in range(2):
                            b, ps, w8 = pool.acquire()
                            kb.waitt("T", yw)
                            for j in range(NDCH):
                                xsl = xnT[:, j * TH + tt * 512: j * TH + (tt + 1) * 512]
                                kb.op("T", lambda t, ps=ps, wbuf=wbuf, j=j, xsl=xsl: t.matmul(
                                    ps[:], wbuf[:, j * 128:(j + 1) * 128], xsl,
                                    start=(j == 0), stop=(j == NDCH - 1)),
                                    wait=w8 if j == 0 else None)
                            t_p = kb.op("T", lambda t: t.drain(), inc=True)
                            if tt == 1:
                                wp_free[bi] = t_p
                            v_ev = kb.op("V", lambda v, ps=ps: v.tensor_copy(
                                raw[:, 16:528], ps[:]), wait=[("DT", t_p)], inc=True, bank=b)
                            if tt == 0:
                                kb.op("V", lambda v, tsl=tsl: v.tensor_copy(raw[:, 13:16], tsl))
                                kb.op("V", lambda v: v.tensor_copy(tail3[:, 0:3], raw[:, 525:528]))
                            else:
                                kb.op("V", lambda v: v.tensor_copy(raw[:, 13:16], tail3[:, 0:3]))
                                kb.op("V", lambda v, tsl=tsl: v.tensor_copy(tsl, raw[:, 525:528]))
                            tap0 = ctaps[:, cbase + head * 4: cbase + head * 4 + 1]
                            kb.op("V", lambda v, tap0=tap0: v.tensor_scalar_mul(
                                conv_acc[:], raw[:, 13:525], tap0))
                            for k in range(1, 4):
                                tapk = ctaps[:, cbase + head * 4 + k: cbase + head * 4 + k + 1]
                                rsl = raw[:, 13 + k:525 + k]
                                kb.op("V", lambda v, tapk=tapk, rsl=rsl: v.scalar_tensor_tensor(
                                    conv_acc[:], rsl, tapk, conv_acc[:], ALU.mult, ALU.add))
                            v_cv = kb.op("V", lambda v: v.engine_nop(), inc=True)
                            dproj = dsel[pi][:, hin * TH + tt * 512: hin * TH + (tt + 1) * 512]
                            s_si = kb.op("S", lambda s, dproj=dproj: s.activation(
                                dproj, conv_acc[:], AF.Silu), wait=[("DV", v_cv)], inc=True)
                            kb.wait("V", "DS", s_si)

                c1_done_s = kb.c("DS")
                if stop_after == "proj" and grp == 0:
                    dbg_taps.extend([qf, kf, vf, wff])
                    break

                y_f2 = kb.dma("Y", wf2g[:], scr["wf2"][:, grp * 512:(grp + 1) * 512],
                              wait=[scrw["wf2"], ("DT", kb.c("DT"))])
                y_g2 = kb.dma("Y", wg2g[:], scr["wg2"][:, grp * 512:(grp + 1) * 512],
                              wait=[scrw["wg2"], ("DT", kb.c("DT"))])
                kb.waitt("T", y_f2)
                kb.waitt("T", y_g2)
                kb.wait("T", "DS", c1_done_s)
                for cc in range(NCH):
                    tc0 = cc * 128
                    kb.wait("S", "DV", kb.c("DV"))
                    gp_sl = gpre1[:, tc0:tc0 + 128]
                    wf2_sl = wf2g[:]
                    b_p, ps_p, w8 = pool.acquire()
                    kb.op("T", lambda t, ps_p=ps_p, gp_sl=gp_sl, wf2_sl=wf2_sl: t.matmul(
                        ps_p[:], gp_sl, wf2_sl, start=True, stop=True), wait=w8)
                    t_pp = kb.op("T", lambda t: t.drain(), inc=True)
                    kb.wait("S", "DT", t_pp)
                    s_pe = kb.op("S", lambda s, ps_p=ps_p: s.activation(
                        p_sb[:], ps_p[:], AF.Exp, scale=-1.0), inc=True, bank=b_p)
                    s_p = kb.op("S", lambda s: s.activation(
                        p_sb[:], p_sb[:], AF.Ln, bias=1.0), inc=True)
                    b_ch, ps_ch, w1 = pool.acquire()
                    b_cf, ps_cf, w2_ = pool.acquire()
                    b_rc, ps_rc, w3 = pool.acquire()
                    b_sm, ps_sm, w4 = pool.acquire()
                    b_lc, ps_lc, w5 = pool.acquire()
                    kb.wait("T", "DS", s_p)
                    kb.op("T", lambda t, ps_ch=ps_ch: t.matmul(
                        ps_ch[:], Lhalf[:], p_sb[:], start=True, stop=True), wait=w1)
                    kb.op("T", lambda t, ps_cf=ps_cf: t.matmul(
                        ps_cf[:], Lfull[:], p_sb[:], start=True, stop=True), wait=w2_)
                    kb.op("T", lambda t, ps_rc=ps_rc: t.matmul(
                        ps_rc[:], Ustrict[:], p_sb[:], start=True, stop=True), wait=w3)
                    kb.op("T", lambda t, ps_sm=ps_sm: t.matmul(
                        ps_sm[0:1, :], ones_cf[:], p_sb[:], start=True, stop=True), wait=w4)
                    for hin in range(NG):
                        psl = p_sb[:, hin * 128:(hin + 1) * 128]
                        kb.op("T", lambda t, ps_lc=ps_lc, hin=hin, psl=psl: t.matmul(
                            ps_lc[:, hin:hin + 1], psl, ones_cf[:],
                            start=True, stop=True), wait=w5 if hin == 0 else None)
                    t_cum = kb.op("T", lambda t: t.drain(), inc=True)
                    kb.wait("S", "DT", t_cum)
                    kb.op("S", lambda s, ps_ch=ps_ch: s.activation(lam_c[:], ps_ch[:], AF.Exp, scale=-1.0), inc=True, bank=b_ch)
                    s_ch = kb.op("S", lambda s, ps_ch=ps_ch: s.activation(eu[:], ps_ch[:], AF.Exp, scale=1.0), inc=True, bank=b_ch)
                    s_cf = kb.op("S", lambda s, ps_cf=ps_cf: s.activation(lam_f[:], ps_cf[:], AF.Exp, scale=-1.0), inc=True, bank=b_cf)
                    s_rc = kb.op("S", lambda s, ps_rc=ps_rc: s.activation(rfac[:], ps_rc[:], AF.Exp, scale=-1.0), inc=True, bank=b_rc)
                    s_sm = kb.op("S", lambda s, ps_sm=ps_sm: s.activation(
                        elam_row[:], ps_sm[0:1, 0:512], AF.Exp, scale=-1.0), inc=True, bank=b_sm)
                    ec_out = elam_col[:].rearrange("p (h s) -> p h s", s=4)[:, :, 0:1]
                    s_cum = kb.op("S", lambda s, ps_lc=ps_lc, ec_out=ec_out: s.activation(
                        ec_out, ps_lc[:, 0:4].rearrange("p (h s) -> p h s", s=1),
                        AF.Exp, scale=-1.0), inc=True, bank=b_lc)

                    sk_g = Sk[:, grp * 512:(grp + 1) * 512]
                    sv_g = Sv[:, grp * 512:(grp + 1) * 512]
                    b_tk, ps_tk, w1 = pool.acquire()
                    b_tv, ps_tv, w2_ = pool.acquire()
                    b_tw, ps_tw, w3 = pool.acquire()
                    for hin in range(NG):
                        ksl = kf[:, hin * TH + tc0: hin * TH + tc0 + 128]
                        vsl = vf[:, hin * TH + tc0: hin * TH + tc0 + 128]
                        wsl = wff[:, hin * TH + tc0: hin * TH + tc0 + 128]
                        kb.op("T", lambda t, ps_tk=ps_tk, hin=hin, ksl=ksl: t.matmul(
                            ps_tk[:, hin * 128:(hin + 1) * 128], ksl, ident_b[:],
                            start=True, stop=True), wait=w1 if hin == 0 else None)
                        kb.op("T", lambda t, ps_tv=ps_tv, hin=hin, vsl=vsl: t.matmul(
                            ps_tv[:, hin * 128:(hin + 1) * 128], vsl, ident_b[:],
                            start=True, stop=True), wait=w2_ if hin == 0 else None)
                        kb.op("T", lambda t, ps_tw=ps_tw, hin=hin, wsl=wsl: t.matmul(
                            ps_tw[:, hin * 128:(hin + 1) * 128], wsl, ident_b[:],
                            start=True, stop=True), wait=w3 if hin == 0 else None)
                    t_tr = kb.op("T", lambda t: t.drain(), inc=True)
                    kb.wait("V", "DT", t_tr)
                    v_k = kb.op("V", lambda v, ps_tk=ps_tk: v.tensor_copy(ktok4[:], ps_tk[:]), inc=True, bank=b_tk)
                    v_v = kb.op("V", lambda v, ps_tv=ps_tv: v.tensor_copy(vtok4[:], ps_tv[:]), inc=True, bank=b_tv)
                    v_tr = kb.op("V", lambda v, ps_tw=ps_tw: v.tensor_copy(wtok4[:], ps_tw[:]), inc=True, bank=b_tw)
                    kb.wait("S", "DV", v_tr)
                    for hin in range(NG):
                        wdst = o24[:, hin * 128:(hin + 1) * 128]
                        wsrc = wtok4[:, hin * 128:(hin + 1) * 128]
                        acc_sl = wsum4[:, hin * 4: hin * 4 + 1]
                        kb.op("S", lambda sc, wdst=wdst, wsrc=wsrc, acc_sl=acc_sl: sc.activation(
                            wdst, wsrc, AF.Square, accum_out=acc_sl))
                    kb.op("S", lambda sc: sc.activation(wrs4[:], wsum4[:], AF.Ln, bias=b_l2eps))
                    s_wn = kb.op("S", lambda sc: sc.activation(wrs4[:], wrs4[:], AF.Exp, scale=-0.5), inc=True)
                    kb.wait("V", "DS", max(s_wn, s_cum))
                    wrs_bc = wrs4.rearrange("p (h s) -> p h s", s=4)[:, :, 0:1].broadcast_to([128, NG, 128])
                    kb.op("V", lambda v, wrs_bc=wrs_bc: v.tensor_tensor(
                        wn4[:].rearrange("p (h m) -> p h m", h=NG), 
                        wtok4[:].rearrange("p (h m) -> p h m", h=NG), wrs_bc, ALU.mult))
                    bsl = beta_all[:, cc * 16 + grp * 4: cc * 16 + grp * 4 + 4]
                    beta_bc = bsl.rearrange("p (h s) -> p h s", s=1).broadcast_to([128, NG, 128])
                    kb.op("V", lambda v, beta_bc=beta_bc: v.tensor_tensor(
                        wn4[:].rearrange("p (h m) -> p h m", h=NG),
                        wn4[:].rearrange("p (h m) -> p h m", h=NG), beta_bc, ALU.mult))
                    kb.op("V", lambda v: v.tensor_tensor(u4[:], wn4[:], eu[:], ALU.mult))
                    v_ur = kb.op("V", lambda v: v.tensor_tensor(r4[:], wn4[:], rfac[:], ALU.mult), inc=True)
                    b_a, ps_a, w8 = pool.acquire()
                    for hin in range(NG):
                        ksl = kf[:, hin * TH + tc0: hin * TH + tc0 + 128]
                        qsl = qf[:, hin * TH + tc0: hin * TH + tc0 + 128]
                        kb.op("T", lambda t, ps_a=ps_a, hin=hin, ksl=ksl, qsl=qsl: t.matmul(
                            ps_a[:, hin * 128:(hin + 1) * 128], ksl, qsl,
                            start=True, stop=True), wait=w8 if hin == 0 else None)
                    t_a = kb.op("T", lambda t: t.drain(), inc=True)
                    kb.wait("V", "DT", t_a)
                    v_am = kb.op("V", lambda v, ps_a=ps_a: v.tensor_tensor(
                        am4[:].rearrange("p (h t) -> p h t", h=NG),
                        ps_a[:].rearrange("p (h t) -> p h t", h=NG),
                        mask01[:].unsqueeze(1).broadcast_to([128, NG, 128]), ALU.mult), inc=True, bank=b_a)
                    b_i, ps_i, w8i = pool.acquire()
                    b_j, ps_j, w8j = pool.acquire()
                    kb.wait("T", "DV", max(v_am, v_ur))
                    for hin in range(NG):
                        hs_ = slice(hin * 128, (hin + 1) * 128)
                        qsl = qf[:, hin * TH + tc0: hin * TH + tc0 + 128]
                        sk_sl = Sk[:, (grp * NG + hin) * 128:(grp * NG + hin) * 128 + 128]
                        kb.op("T", lambda t, ps_i=ps_i, hs_=hs_: t.matmul(
                            ps_i[:, hs_], am4[:, hs_], u4[:, hs_],
                            start=True, stop=True), wait=w8i if hin == 0 else None)
                        kb.op("T", lambda t, ps_j=ps_j, hs_=hs_, qsl=qsl, sk_sl=sk_sl: t.matmul(
                            ps_j[:, hs_], qsl, sk_sl,
                            start=True, stop=True), wait=w8j if hin == 0 else None)
                    t_sc = kb.op("T", lambda t: t.drain(), inc=True)
                    kb.wait("V", "DT", t_sc)
                    v_j = kb.op("V", lambda v, ps_j=ps_j: v.tensor_tensor(st14[:], lam_f[:], ps_j[:], ALU.mult), inc=True, bank=b_j)
                    v_i = kb.op("V", lambda v, ps_i=ps_i: v.scalar_tensor_tensor(
                        st24[:], ps_i[:], SCALE, lam_c[:], ALU.mult, ALU.mult), inc=True, bank=b_i)
                    v_sc = kb.op("V", lambda v: v.scalar_tensor_tensor(
                        st24[:], st14[:], SCALE, st24[:], ALU.mult, ALU.add), inc=True)
                    st3d = st24[:].rearrange("p (h m) -> p h m", h=NG)
                    nmx_sl = nmx4.rearrange("p (h s) -> p h s", s=4)[:, :, 0:1]
                    kb.op("V", lambda v, st3d=st3d, nmx_sl=nmx_sl: v.reduce_max(
                        nmx_sl, st3d, axis=AX.X, negate=True))
                    kb.op("V", lambda v: v.drain())
                    nmx_bc = nmx4.rearrange("p (h s) -> p h s", s=4)[:, :, 0:1].broadcast_to([128, NG, 128])
                    v_nm = kb.op("V", lambda v, st3d=st3d, nmx_bc=nmx_bc: v.tensor_tensor(
                        st3d, st3d, nmx_bc, ALU.add), inc=True)
                    kb.wait("S", "DV", v_nm)
                    s_es = kb.op("S", lambda sc: sc.activation(st24[:], st24[:], AF.Exp), inc=True)
                    kb.wait("V", "DS", s_es)
                    smr_sl = smr4.rearrange("p (h s) -> p h s", s=4)[:, :, 0:1]
                    kb.op("V", lambda v, st3d=st3d, smr_sl=smr_sl: v.reduce_sum(smr_sl, st3d, axis=AX.X))
                    kb.op("V", lambda v: v.drain())
                    kb.op("V", lambda v: v.reciprocal(smr4[:], smr4[:]))
                    kb.op("V", lambda v: v.drain())
                    smr_bc = smr4.rearrange("p (h s) -> p h s", s=4)[:, :, 0:1].broadcast_to([128, NG, 128])
                    kb.op("V", lambda v, st3d=st3d, smr_bc=smr_bc: v.tensor_tensor(
                        st14[:].rearrange("p (h m) -> p h m", h=NG), st3d, smr_bc, ALU.mult))
                    kb.op("V", lambda v: v.tensor_tensor(ac4[:], st14[:], lam_c[:], ALU.mult))
                    v_a2 = kb.op("V", lambda v: v.tensor_tensor(af4[:], st14[:], lam_f[:], ALU.mult), inc=True)
                    b_t2, ps_t2, w8 = pool.acquire()
                    b_t3, ps_t3, w83 = pool.acquire()
                    b_t4, ps_t4, w84 = pool.acquire()
                    kb.wait("T", "DV", v_a2)
                    for hin in range(NG):
                        hs_ = slice(hin * 128, (hin + 1) * 128)
                        kb.op("T", lambda t, ps_t2=ps_t2, hs_=hs_: t.matmul(
                            ps_t2[:, hs_], ac4[:, hs_], ident_b[:], start=True, stop=True),
                            wait=w8 if hin == 0 else None)
                        kb.op("T", lambda t, ps_t3=ps_t3, hs_=hs_: t.matmul(
                            ps_t3[:, hs_], af4[:, hs_], ident_b[:], start=True, stop=True),
                            wait=w83 if hin == 0 else None)
                        kb.op("T", lambda t, ps_t4=ps_t4, hs_=hs_: t.matmul(
                            ps_t4[:, hs_], u4[:, hs_], ident_b[:], start=True, stop=True),
                            wait=w84 if hin == 0 else None)
                    t_t2 = kb.op("T", lambda t: t.drain(), inc=True)
                    kb.wait("V", "DT", t_t2)
                    v_c1 = kb.op("V", lambda v, ps_t2=ps_t2: v.tensor_copy(acT4[:], ps_t2[:]), inc=True, bank=b_t2)
                    v_c2 = kb.op("V", lambda v, ps_t3=ps_t3: v.tensor_copy(afT4[:], ps_t3[:]), inc=True, bank=b_t3)
                    v_t2 = kb.op("V", lambda v, ps_t4=ps_t4: v.tensor_copy(uT4[:], ps_t4[:]), inc=True, bank=b_t4)
                    b_pt, ps_pt, w8 = pool.acquire()
                    kb.wait("T", "DV", v_t2)
                    for hin in range(NG):
                        hs_ = slice(hin * 128, (hin + 1) * 128)
                        kb.op("T", lambda t, ps_pt=ps_pt, hs_=hs_: t.matmul(
                            ps_pt[:, hs_], uT4[:, hs_], acT4[:, hs_], start=True, stop=True),
                            wait=w8 if hin == 0 else None)
                    t_pt = kb.op("T", lambda t: t.drain(), inc=True)
                    kb.wait("V", "DT", t_pt)
                    v_pm = kb.op("V", lambda v, ps_pt=ps_pt: v.tensor_tensor(
                        pm4[:].rearrange("p (h t) -> p h t", h=NG),
                        ps_pt[:].rearrange("p (h t) -> p h t", h=NG),
                        mask01[:].unsqueeze(1).broadcast_to([128, NG, 128]), ALU.mult), inc=True, bank=b_pt)
                    b_o, ps_o, w8 = pool.acquire()
                    kb.wait("T", "DV", v_pm)
                    for hin in range(NG):
                        hs_ = slice(hin * 128, (hin + 1) * 128)
                        sv_sl = Sv[:, (grp * NG + hin) * 128:(grp * NG + hin) * 128 + 128]
                        kb.op("T", lambda t, ps_o=ps_o, hs_=hs_: t.matmul(
                            ps_o[:, hs_], vtok4[:, hs_], pm4[:, hs_], start=True, stop=False),
                            wait=w8 if hin == 0 else None)
                        kb.op("T", lambda t, ps_o=ps_o, hs_=hs_, sv_sl=sv_sl: t.matmul(
                            ps_o[:, hs_], sv_sl, afT4[:, hs_], start=False, stop=True))
                    t_o = kb.op("T", lambda t: t.drain(), inc=True)
                    b_su, ps_su, w8 = pool.acquire()
                    b_sb_, ps_sb_, w8b = pool.acquire()
                    b_sv, ps_sv, w8v = pool.acquire()
                    for hin in range(NG):
                        hs_ = slice(hin * 128, (hin + 1) * 128)
                        rowsl = elam_row[0:1, hin * 128:(hin + 1) * 128]
                        kb.op("T", lambda t, ps_su=ps_su, hs_=hs_: t.matmul(
                            ps_su[:, hs_], ktok4[:, hs_], r4[:, hs_], start=True, stop=True),
                            wait=w8 if hin == 0 else None)
                        kb.op("T", lambda t, ps_sb_=ps_sb_, hs_=hs_, rowsl=rowsl: t.matmul(
                            ps_sb_[:, hs_], ones_1b[:], rowsl, start=True, stop=True),
                            wait=w8b if hin == 0 else None)
                        kb.op("T", lambda t, ps_sv=ps_sv, hs_=hs_: t.matmul(
                            ps_sv[:, hs_], r4[:, hs_], vtok4[:, hs_], start=True, stop=True),
                            wait=w8v if hin == 0 else None)
                    t_su = kb.op("T", lambda t: t.drain(), inc=True)
                    kb.wait("V", "DT", t_su)
                    v_s1 = kb.op("V", lambda v, ps_sb_=ps_sb_, sk_g=sk_g: v.tensor_tensor(sktmp4[:], sk_g, ps_sb_[:], ALU.mult), inc=True, bank=b_sb_)
                    v_s2 = kb.op("V", lambda v, ps_su=ps_su, sk_g=sk_g: v.tensor_tensor(sk_g, sktmp4[:], ps_su[:], ALU.add), inc=True, bank=b_su)
                    elam_bc = elam_col[:].rearrange("p (h s) -> p h s", s=4)[:, :, 0:1].broadcast_to([128, NG, 128])
                    kb.op("V", lambda v, elam_bc=elam_bc, sv_g=sv_g: v.tensor_tensor(
                        sktmp4[:].rearrange("p (h m) -> p h m", h=NG),
                        sv_g.rearrange("p (h m) -> p h m", h=NG), elam_bc, ALU.mult))
                    v_su = kb.op("V", lambda v, ps_sv=ps_sv, sv_g=sv_g: v.tensor_tensor(
                        sv_g, sktmp4[:], ps_sv[:], ALU.add), inc=True, bank=b_sv)
                    kb.wait("V", "DT", t_o)
                    v_ocp = kb.op("V", lambda v, ps_o=ps_o: v.tensor_copy(o4[:], ps_o[:]), inc=True, bank=b_o)
                    kb.wait("S", "DV", v_ocp)
                    s_o2 = kb.op("S", lambda sc: sc.activation(o24[:], o4[:], AF.Square), inc=True)
                    b_r, ps_r, w8 = pool.acquire()
                    b_g, ps_g, w8g = pool.acquire()
                    kb.wait("T", "DS", s_o2)
                    kb.op("T", lambda t, ps_r=ps_r: t.matmul(
                        ps_r[0:1, :], ones_cb[:], o24[:], start=True, stop=True), wait=w8)
                    gsl = gpre2[:, tc0:tc0 + 128]
                    for hin in range(NG):
                        hs_ = slice(hin * 128, (hin + 1) * 128)
                        kb.op("T", lambda t, ps_g=ps_g, hs_=hs_, gsl=gsl: t.matmul(
                            ps_g[:, hs_], wg2g[:, hs_], gsl, start=True, stop=True),
                            wait=w8g if hin == 0 else None)
                    t_r = kb.op("T", lambda t: t.drain(), inc=True)
                    kb.wait("S", "DT", t_r)
                    s_ln = kb.op("S", lambda sc, ps_r=ps_r: sc.activation(
                        lrow, ps_r[0:1, :], AF.Ln, bias=b_rmse[0:1, :]), inc=True, bank=b_r)
                    kb.op("S", lambda sc: sc.activation(
                        rrow, lrow, AF.Exp, scale=-0.5, bias=b_hl128[0:1, :]))
                    s_eg2 = kb.op("S", lambda sc, ps_g=ps_g: sc.activation(
                        eg4[:], ps_g[:], AF.Exp, scale=-1.0), inc=True, bank=b_g)
                    b_rb, ps_rb, w8 = pool.acquire()
                    kb.wait("T", "DS", s_eg2)
                    kb.op("T", lambda t, ps_rb=ps_rb: t.matmul(
                        ps_rb[:], ones_1b[:], rrow, start=True, stop=True), wait=w8)
                    t_rb = kb.op("T", lambda t: t.drain(), inc=True)
                    kb.wait("V", "DS", s_eg2)
                    ebg_sl = ebg[:, grp * 4: grp * 4 + 4]
                    ebg_b = ebg_sl.rearrange("p (h s) -> p h s", s=1).broadcast_to([128, NG, 128])
                    kb.op("V", lambda v, ebg_b=ebg_b: v.tensor_tensor(
                        eg4[:].rearrange("p (h t) -> p h t", h=NG),
                        eg4[:].rearrange("p (h t) -> p h t", h=NG), ebg_b, ALU.mult))
                    kb.op("V", lambda v: v.tensor_scalar_add(st24[:], eg4[:], 1.0))
                    kb.op("V", lambda v: v.drain())
                    kb.op("V", lambda v: v.reciprocal(st24[:], st24[:]))
                    kb.wait("V", "DT", t_rb)
                    v_rb = kb.op("V", lambda v, ps_rb=ps_rb: v.tensor_tensor(o4[:], o4[:], ps_rb[:], ALU.mult), inc=True, bank=b_rb)
                    dsto = o_grp[:].rearrange("p (h t) -> p h t", h=NG)[:, :, tc0:tc0 + 128]
                    v_of = kb.op("V", lambda v, dsto=dsto: v.scalar_tensor_tensor(
                        dsto, o4[:].rearrange("p (h t) -> p h t", h=NG), normw[:],
                        st24[:].rearrange("p (h t) -> p h t", h=NG), ALU.mult, ALU.mult), inc=True)

                if stop_after == "scan" and grp == 0:
                    dbg_taps.extend([o_grp, Sk, Sv])
                    break

                scan_done_v = kb.c("DV")
                for dj in range(NDCH):
                    bi = dj % 2
                    wbuf = wp[bi]
                    src = scr["wo"].rearrange("(j p) c -> p j c", p=128)[
                        :, grp * NG:(grp + 1) * NG, dj * 128:(dj + 1) * 128]
                    yw = kb.dma("Y" if bi == 0 else "S", wbuf[:, 0:512], src,
                                wait=[scrw["wo"], ("DT", wp_free[bi])])
                    for tt in range(2):
                        b, ps, w8 = pool.acquire()
                        kb.waitt("T", yw)
                        kb.wait("T", "DV", scan_done_v)
                        for hin in range(NG):
                            osl = o_grp[:, hin * TH + tt * 512: hin * TH + (tt + 1) * 512]
                            kb.op("T", lambda t, ps=ps, wbuf=wbuf, hin=hin, osl=osl: t.matmul(
                                ps[:], wbuf[:, hin * 128:(hin + 1) * 128], osl,
                                start=(hin == 0), stop=(hin == NG - 1)),
                                wait=w8 if hin == 0 else None)
                        t_wo = kb.op("T", lambda t: t.drain(), inc=True)
                        if tt == 1:
                            wp_free[bi] = t_wo
                        asl = acc[:, dj * TH + tt * 512: dj * TH + (tt + 1) * 512]
                        kb.wait("V", "DT", t_wo)
                        if grp == 0:
                            v_ev = kb.op("V", lambda v, asl=asl, ps=ps: v.tensor_copy(asl, ps[:]), inc=True, bank=b)
                        else:
                            v_ev = kb.op("V", lambda v, asl=asl, ps=ps: v.tensor_tensor(
                                asl, asl, ps[:], ALU.add), inc=True, bank=b)

            if stop_after in ("proj", "scan"):
                break
            if stop_after == "wo":
                dbg_taps.extend([acc, Sk, Sv, o_grp])
                break

            last_vhs = 0
            for ex in range(NEXP):
                if ex < E_MOE:
                    w13src, w2src = scr["w13"][ex], scr["w2"][ex]
                    gup_off, up_off = 0, I_MOE
                    cf_row, dn_col0 = ex, 0
                    wtag13, wtag2 = f"w13_{ex}", f"w2_{ex}"
                else:
                    qq = ex - E_MOE
                    w13src, w2src = scr["sgu"], scr["sdw"]
                    gup_off, up_off = qq * I_MOE, I_SH + qq * I_MOE
                    cf_row, dn_col0 = 16, qq * I_MOE
                    wtag13, wtag2 = "sgu", "sdw"
                y_cr = kb.dma("Y", coef_row[:], coefT[cf_row:cf_row + 1, :],
                              wait=[("DV", phaseB_V), ("DT", kb.c("DT"))])
                v_cb = 0
                for tt in range(2):
                    b_cb, ps_cb, w8 = pool.acquire()
                    kb.waitt("T", y_cr)
                    rsl = coef_row[0:1, tt * 512:(tt + 1) * 512]
                    kb.op("T", lambda t, ps_cb=ps_cb, rsl=rsl: t.matmul(
                        ps_cb[:], ones_1b[:], rsl, start=True, stop=True), wait=w8)
                    t_cb = kb.op("T", lambda t: t.drain(), inc=True)
                    dstcb = coef_bc[:, tt * 512:(tt + 1) * 512]
                    v_cb = kb.op("V", lambda v, dstcb=dstcb, ps_cb=ps_cb: v.tensor_copy(
                        dstcb, ps_cb[:]), wait=[("DT", t_cb)], inc=True, bank=b_cb)
                for sup in range(6):
                    ic0 = sup * 2
                    nch_s = 2 if sup < 5 else 1
                    silu_ds = {}
                    for gu in range(2):
                        roff = gup_off if gu == 0 else up_off
                        bi = (sup * 2 + gu) % 2
                        slab = upslab[bi]
                        if half == 0:
                            ywl = None
                            for j in range(NDCH):
                                ssrc = w13src[roff + ic0 * 128: roff + (ic0 + nch_s) * 128,
                                              j * 128:(j + 1) * 128]
                                wlist = [scrw[wtag13], ("DT", up_free[bi])] if j == 0 else None
                                ywl = kb.dma("Y", slab[:, j * 256: j * 256 + nch_s * 128],
                                             ssrc, transpose=True, wait=wlist)
                        else:
                            w13Tsrc = scrT["w13T"][ex] if ex < E_MOE else scrT["sguT"]
                            ssrc = w13Tsrc[:, roff + ic0 * 128: roff + (ic0 + nch_s) * 128
                                           ].rearrange("(j p) c -> p j c", p=128)
                            sdst = slab[:].rearrange("p (j c) -> p j c", c=256)[
                                :, :, 0:nch_s * 128]
                            ywl = kb.dma("Y", sdst, ssrc,
                                         wait=[pump.done_tok[wtag13 + "@t"],
                                               ("DT", up_free[bi])])
                        t_last = 0
                        for icx in range(nch_s):
                            for tt in range(2):
                                b, ps, w8 = pool.acquire()
                                kb.waitt("T", ywl)
                                for j in range(NDCH):
                                    xsl = xnT[:, j * TH + tt * 512: j * TH + (tt + 1) * 512]
                                    wslab = slab[:, j * 256 + icx * 128: j * 256 + icx * 128 + 128]
                                    kb.op("T", lambda t, ps=ps, wslab=wslab, xsl=xsl, j=j: t.matmul(
                                        ps[:], wslab, xsl, start=(j == 0), stop=(j == NDCH - 1)),
                                        wait=w8 if j == 0 else None)
                                t_up = kb.op("T", lambda t: t.drain(), inc=True)
                                t_last = t_up
                                if gu == 0:
                                    dst = sgtmp[:, icx * TH + tt * 512: icx * TH + (tt + 1) * 512]
                                    s_ev = kb.op("S", lambda s, dst=dst, ps=ps: s.activation(
                                        dst, ps[:], AF.Silu),
                                        wait=[("DT", t_up), ("DV", last_vhs)], inc=True, bank=b)
                                    silu_ds[(icx, tt)] = s_ev
                                else:
                                    hsl = hs[:, (ic0 + icx) * TH + tt * 512: (ic0 + icx) * TH + (tt + 1) * 512]
                                    ssl = sgtmp[:, icx * TH + tt * 512: icx * TH + (tt + 1) * 512]
                                    cbsl = coef_bc[:, tt * 512:(tt + 1) * 512]
                                    kb.wait("V", "DT", t_up)
                                    kb.wait("V", "DV", v_cb)
                                    kb.wait("V", "DS", silu_ds[(icx, tt)])
                                    v_h1 = kb.op("V", lambda v, hsl=hsl, ssl=ssl, ps=ps: v.tensor_tensor(
                                        hsl, ssl, ps[:], ALU.mult), inc=True, bank=b)
                                    v_hs = kb.op("V", lambda v, hsl=hsl, cbsl=cbsl: v.tensor_tensor(
                                        hsl, hsl, cbsl, ALU.mult), inc=True)
                                    last_vhs = v_hs
                        up_free[bi] = t_last
                        if half == 0:
                            pump.emit(2)
                up_done_v = kb.c("DV")
                for hdsup in range(8):
                    hd0 = hdsup * 2
                    bi = hdsup % 2
                    slab = dnslab[bi]
                    if half == 0:
                        ywl = None
                        for ic in range(11):
                            ssrc = w2src[hd0 * 128:(hd0 + 2) * 128,
                                         dn_col0 + ic * 128: dn_col0 + (ic + 1) * 128]
                            wlist = [scrw[wtag2], ("DT", dn_free[bi])] if ic == 0 else None
                            ywl = kb.dma("Y", slab[:, ic * 256:(ic + 1) * 256], ssrc,
                                         transpose=True, wait=wlist)
                    else:
                        w2Tsrc = scrT["w2T"][ex] if ex < E_MOE else scrT["sdwT"]
                        ssrc = w2Tsrc[dn_col0: dn_col0 + I_MOE,
                                      hd0 * 128: hd0 * 128 + 256].rearrange(
                            "(j p) c -> p j c", p=128)
                        sdst = dnslab[bi][:].rearrange("p (j c) -> p j c", c=256)
                        ywl = kb.dma("Y", sdst, ssrc,
                                     wait=[pump.done_tok[wtag2 + "@t"],
                                           ("DT", dn_free[bi])])
                    t_last = 0
                    for hdc in range(2):
                        for tt in range(2):
                            b, ps, w8 = pool.acquire()
                            kb.waitt("T", ywl)
                            kb.wait("T", "DV", up_done_v)
                            for ic in range(11):
                                hsl = hs[:, ic * TH + tt * 512: ic * TH + (tt + 1) * 512]
                                wslab = slab[:, ic * 256 + hdc * 128: ic * 256 + hdc * 128 + 128]
                                kb.op("T", lambda t, ps=ps, wslab=wslab, hsl=hsl, ic=ic: t.matmul(
                                    ps[:], wslab, hsl, start=(ic == 0), stop=(ic == 10)),
                                    wait=w8 if ic == 0 else None)
                            t_dn = kb.op("T", lambda t: t.drain(), inc=True)
                            t_last = t_dn
                            asl = acc[:, (hd0 + hdc) * TH + tt * 512: (hd0 + hdc) * TH + (tt + 1) * 512]
                            kb.wait("V", "DT", t_dn)
                            v_dn = kb.op("V", lambda v, asl=asl, ps=ps: v.tensor_tensor(
                                asl, asl, ps[:], ALU.add), inc=True, bank=b)
                    dn_free[bi] = t_last
                    if half == 0:
                        pump.emit(2)

            if stop_after == "ffn":
                dbg_taps.append(acc)
                break

            ffn_done_v = kb.c("DV")
            for tc in range(NCH):
                xb = x_tok_e
                yx = kb.dma("Y", xb[:], x_d[th0 + tc * 128: th0 + (tc + 1) * 128, :],
                            wait=([st_tok, [("DT", kb.c("DT"))]] if tc == 0 else [st_tok]))
                for bq in range(4):
                    b, ps, w8 = pool.acquire()
                    kb.wait("T", "DV", ffn_done_v)
                    for jj in range(4):
                        dj = bq * 4 + jj
                        asl = acc[:, dj * TH + tc * 128: dj * TH + (tc + 1) * 128]
                        kb.op("T", lambda t, ps=ps, jj=jj, asl=asl: t.matmul(
                            ps[:, jj * 128:(jj + 1) * 128], asl, ident_f[:],
                            start=True, stop=True), wait=w8 if jj == 0 else None)
                    t_tr = kb.op("T", lambda t: t.drain(), inc=True)
                    kb.wait("V", "DT", t_tr)
                    kb.waitt("V", yx)
                    xsl = xb[:, bq * 512:(bq + 1) * 512]
                    v_ev = kb.op("V", lambda v, xsl=xsl, ps=ps: v.tensor_tensor(
                        xsl, xsl, ps[:], ALU.add), inc=True, bank=b)
                st_tok = kb.dma("S", out_d[th0 + tc * 128: th0 + (tc + 1) * 128, :], xb[:],
                       wait=[("DV", kb.c("DV"))])
            if half == 0:
                pump.flush()

        if stop_after is not None:
            kb.wait("G", "DV", kb.c("DV"))
            kb.wait("G", "DS", kb.c("DS"))
            kb.wait("G", "DT", kb.c("DT"))
            row = 0
            for tens in dbg_taps:
                ap = tens if isinstance(tens, bass.AP) else tens[:]
                P, FREE = ap.shape
                npieces = (FREE + D - 1) // D
                for pc in range(npieces):
                    w = min(D, FREE - pc * D)
                    if row + P <= T:
                        kb.dma("G", out_d[row:row + P, 0:w], ap[:, pc * D:pc * D + w])
                    row += 128

        for i in range(KB.NYS):
            kb.wait("G", f"QY{i}", kb.c(f"QY{i}"))
            kb.wait("G", f"QS{i}", kb.c(f"QS{i}"))
        kb.wait("G", "QG", kb.c("QG"))

        build_program.last_kb = kb
        block = E(nc.Block())
        block.sync(lambda e: [f(e) for f in kb.ops["Y"]])
        block.gpsimd(lambda e: [f(e) for f in kb.ops["G"]])
        block.vector(lambda e: [f(e) for f in kb.ops["V"]])
        block.scalar(lambda e: [f(e) for f in kb.ops["S"]])
        block.tensor(lambda e: [f(e) for f in kb.ops["T"]])

    return nc


def kernel(**inputs):
    x = np.ascontiguousarray(np.asarray(inputs["L0_attn_hidden_states"], np.float32))
    nc = build_program()
    in_maps = []
    for b in range(8):
        m = {"x": np.ascontiguousarray(x[b])}
        for n in W_NAMES:
            m[n] = np.ascontiguousarray(np.asarray(inputs[n], np.float32))
        in_maps.append(m)
    res = run_bass_kernel_spmd(nc, in_maps, core_ids=list(range(8)))
    out = np.stack([np.asarray(r["out"], np.float32) for r in res.results], axis=0)
    return out


# version 07
# speedup vs baseline: 1.3143x; 1.3143x over previous
import numpy as np
from contextlib import ExitStack

import concourse.bass as bass
import concourse.mybir as mybir
from concourse.bass_utils import run_bass_kernel_spmd

F32 = mybir.dt.float32
BF16 = mybir.dt.bfloat16
AF = mybir.ActivationFunctionType
ALU = mybir.AluOpType
AX = mybir.AxisListType

T = 2048
TH = 1024
D = 2048
NDCH = 16
H = 16
DK = DV = M = 128
C = 128
NCH = TH // C
NG = 4
NGRP = H // NG
E_MOE = 16
I_MOE = 1408
I_SH = 5632
NEXP = 20
SCALE = float(DK) ** -0.5
LN_EPS = 1e-5
RMS_EPS = 1e-5
L2_EPS = 1e-6

W_NAMES = ["L0_attn_Wq", "L0_attn_Wk", "L0_attn_Wv", "L0_attn_Ww", "L0_attn_cq",
           "L0_attn_ck", "L0_attn_cv", "L0_attn_Wf1", "L0_attn_Wf2", "L0_attn_Wb",
           "L0_attn_Wg1", "L0_attn_Wg2", "L0_attn_bg2", "L0_attn_norm_w",
           "L0_attn_Wo", "L0_ffn_gate_w", "L0_ffn_sgate_w", "L0_ffn_w13",
           "L0_ffn_w2", "L0_ffn_shared_gate_up_w", "L0_ffn_shared_down_w"]


class KB:
    ENGS = ("T", "V", "S", "G", "Y")
    NYS = 10

    def __init__(self, nc, ctx):
        self.nc = nc
        E = ctx.enter_context
        self.ops = {e: [] for e in self.ENGS}
        self.sem = {}
        self.cnt = {}
        names = (["DT", "DV", "DS", "DG", "QG", "QS"]
                 + [f"QY{i}" for i in range(self.NYS)]
                 + [f"PS{i}" for i in range(8)])
        for name in names:
            self.sem[name] = E(nc.semaphore(name))
            self.cnt[name] = 0
        self.done_of = {"T": "DT", "V": "DV", "S": "DS", "G": "DG"}
        self.q_of = {"G": "QG", "S": "QS"}
        self.ynext = 0
        self.trace = []

    def op(self, eng, f, inc=False, wait=None, drain=None, bank=None):
        semname = self.done_of.get(eng)
        if inc:
            self.cnt[semname] += 1
        sem_obj = self.sem.get(semname) if inc else None
        bank_sem = None
        if bank is not None:
            bn = f"PS{bank}"
            self.cnt[bn] += 1
            bank_sem = self.sem[bn]
        wait = self._flat(wait)
        waits = [(self.sem[s], v) for (s, v) in wait]
        incs = ([(semname, 1)] if inc else []) + (
            [(f"PS{bank}", 1)] if bank is not None else [])
        self.trace.append((eng, wait, incs))
        if drain is None:
            drain = eng in ("V", "S", "G")

        if bank_sem is not None:
            drain = True

        def run(e, f=f, waits=waits, sem_obj=sem_obj, drain=drain, bank_sem=bank_sem):
            for s, v in waits:
                e.wait_ge(s, v)
            inst = f(e)
            if sem_obj is not None:
                inst.then_inc(sem_obj, 1)
            if drain:
                d = e.drain()
                if bank_sem is not None:
                    d.then_inc(bank_sem, 1)
        self.ops[eng].append(run)
        return self.cnt[semname] if inc else None

    @staticmethod
    def _flat(wait):
        out = []
        for w in (wait or []):
            if isinstance(w, tuple) and isinstance(w[0], str):
                out.append(w)
            else:
                out.extend(w)
        return [(s, v) for (s, v) in out if v > 0]

    def dma(self, eng, out, in_, wait=None, **kw):
        wait = self._flat(wait)
        if eng == "Y":
            q = f"QY{self.ynext}"
            self.ynext = (self.ynext + 1) % self.NYS
        else:
            q = self.q_of[eng]
        if self.cnt[q] > 0:
            wait = wait + [(q, self.cnt[q])]
        self.cnt[q] += 16
        sem_obj = self.sem[q]
        waits = [(self.sem[s], v) for (s, v) in wait]
        self.trace.append((eng, wait, [(q, 16)]))

        def run(e, out=out, in_=in_, kw=kw, waits=waits, sem_obj=sem_obj):
            for s, v in waits:
                e.wait_ge(s, v)
            e.dma_start(out=out, in_=in_, **kw).then_inc(sem_obj, 16)
        self.ops[eng].append(run)
        if eng == "Y":
            return tuple((f"QY{i}", self.cnt[f"QY{i}"]) for i in range(self.NYS))
        return ((q, self.cnt[q]),)

    def dma1(self, eng, out, in_, wait=None, **kw):
        qprev = self.ynext
        tok = self.dma(eng, out, in_, wait=wait, **kw)
        if eng == "Y":
            return ((f"QY{qprev}", self.cnt[f"QY{qprev}"]),)
        return tok

    def wait(self, eng, sem_name, value):
        if value > 0:
            self.trace.append((eng, [(sem_name, value)], []))
            self.ops[eng].append(
                lambda e, s=self.sem[sem_name], v=value: e.wait_ge(s, v))

    def waitt(self, eng, token):
        for (sname, v) in token:
            self.wait(eng, sname, v)

    def c(self, sem_name):
        return self.cnt[sem_name]


class PsumPool:

    def __init__(self, kb, tiles):
        self.kb = kb
        self.tiles = tiles
        self.rr = 0

    def acquire(self):
        b = self.rr
        self.rr = (self.rr + 1) % 8
        wait = [(f"PS{b}", self.kb.cnt[f"PS{b}"])]
        return b, self.tiles[b], wait


def build_program(stop_after=None, n_half=2, do_scratch=True):
    nc = bass.Bass("TRN2", target_bir_lowering=False)

    x_d = nc.dram_tensor("x", [T, D], F32, kind="ExternalInput").ap()
    shapes = {
        "L0_attn_Wq": [D, D], "L0_attn_Wk": [D, D], "L0_attn_Wv": [D, D],
        "L0_attn_Ww": [D, D], "L0_attn_cq": [D, 4], "L0_attn_ck": [D, 4],
        "L0_attn_cv": [D, 4], "L0_attn_Wf1": [D, 128], "L0_attn_Wf2": [128, D],
        "L0_attn_Wb": [D, 16], "L0_attn_Wg1": [D, 128], "L0_attn_Wg2": [128, D],
        "L0_attn_bg2": [D], "L0_attn_norm_w": [128], "L0_attn_Wo": [D, D],
        "L0_ffn_gate_w": [16, D], "L0_ffn_sgate_w": [1, D],
        "L0_ffn_w13": [E_MOE, 2 * I_MOE, D], "L0_ffn_w2": [E_MOE, D, I_MOE],
        "L0_ffn_shared_gate_up_w": [2 * I_SH, D], "L0_ffn_shared_down_w": [D, I_SH],
    }
    w_d = {n: nc.dram_tensor(n, s, F32, kind="ExternalInput").ap()
           for n, s in shapes.items()}
    out_d = nc.dram_tensor("out", [T, D], F32, kind="ExternalOutput").ap()

    scr_shapes = {
        "wq": [D, D], "wk": [D, D], "wv": [D, D], "ww": [D, D],
        "wf1": [D, 128], "wf2": [128, D], "wb": [D, 16],
        "wg1": [D, 128], "wg2": [128, D], "wo": [D, D],
        "gatew": [16, D],
        "w13": [E_MOE, 2 * I_MOE, D], "w2": [E_MOE, D, I_MOE],
        "sgu": [2 * I_SH, D], "sdw": [D, I_SH],
    }
    scr = {n: nc.dram_tensor("scr_" + n, s, BF16, kind="Internal").ap()
           for n, s in scr_shapes.items()}

    dbg_taps = []

    with ExitStack() as ctx:
        E = ctx.enter_context
        kb = KB(nc, ctx)

        xnT = E(nc.sbuf_tensor("xnT", [128, NDCH * TH], BF16))
        acc = E(nc.sbuf_tensor("acc", [128, NDCH * TH], F32))
        big = E(nc.sbuf_tensor("big", [128, 20480], BF16))
        qf = big[:, 0:4096]
        kf = big[:, 4096:8192]
        vf = big[:, 8192:12288]
        wff = big[:, 12288:16384]
        o_grp = big[:, 16384:20480]
        hs = big[:, 0:11264]
        sgtmp = big[:, 11264:13312]

        slab = [E(nc.sbuf_tensor(f"slab{i}", [128, 4096], BF16)) for i in range(2)]
        wp = [slab[0][:, 0:2048], slab[1][:, 0:2048]]
        upslab = [slab[0], slab[1]]
        dnslab = [slab[0][:, 0:2816], slab[1][:, 0:2816]]
        wp_free = up_free = dn_free = slab_free = [0, 0]

        x_tok = acc[:, 0:2048]
        x_tok_e = big[:, 0:4096].bitcast(F32)
        xn_tok = E(nc.sbuf_tensor("xn_tok", [128, D], BF16))
        raw = E(nc.sbuf_tensor("raw", [128, 528], F32))
        conv_acc = E(nc.sbuf_tensor("conv_acc", [128, 512], F32))
        tail3 = E(nc.sbuf_tensor("tail3", [128, 4], F32))
        tails = E(nc.sbuf_tensor("tails", [128, 4 * 16 * 3], F32))
        p_sb = E(nc.sbuf_tensor("p_sb", [128, 512], F32))
        lam_c = E(nc.sbuf_tensor("lam_c", [128, 512], F32))
        lam_f = E(nc.sbuf_tensor("lam_f", [128, 512], F32))
        eu = E(nc.sbuf_tensor("eu", [128, 512], BF16))
        rfac = E(nc.sbuf_tensor("rfac", [128, 512], BF16))
        elam_row = E(nc.sbuf_tensor("elam_row", [1, 512], BF16))
        elam_col = E(nc.sbuf_tensor("elam_col", [128, 16], F32))
        gpre1 = E(nc.sbuf_tensor("gpre1", [128, TH], BF16))
        gpre2 = E(nc.sbuf_tensor("gpre2", [128, TH], BF16))
        Sk = E(nc.sbuf_tensor("Sk", [128, H * M], BF16))
        Sv = E(nc.sbuf_tensor("Sv", [128, H * DV], BF16))
        beta_all = E(nc.sbuf_tensor("beta_all", [128, NCH * 16], F32))
        coefT = E(nc.sbuf_tensor("coefT", [32, TH], BF16))
        coef_bc = E(nc.sbuf_tensor("coef_bc", [128, TH], BF16))
        st1 = E(nc.sbuf_tensor("st1", [128, 128], F32))
        wfg1 = E(nc.sbuf_tensor("wfg1", [128, 2048], BF16))
        ktok4 = wfg1[:, 0:512]
        vtok4 = wfg1[:, 512:1024]
        wn4 = wfg1[:, 1024:1536]
        u4 = wfg1[:, 1536:2048]
        r4 = E(nc.sbuf_tensor("r4", [128, 512], BF16))
        am4 = E(nc.sbuf_tensor("am4", [128, 512], BF16))
        st14 = E(nc.sbuf_tensor("st14", [128, 512], F32))
        st24 = E(nc.sbuf_tensor("st24", [128, 512], F32))
        coef_row = st24[:].bitcast(BF16)[0:1, 0:TH]
        ac4 = E(nc.sbuf_tensor("ac4", [128, 512], BF16))
        af4 = E(nc.sbuf_tensor("af4", [128, 512], BF16))
        o4 = af4
        o24 = E(nc.sbuf_tensor("o24", [128, 512], BF16))
        sktmp4 = ac4
        wtok4 = wn4
        pm4 = am4
        acT4 = ac4
        afT4 = af4
        uT4 = u4
        eg4 = p_sb[:].bitcast(BF16)[:, 0:512]
        cols2 = E(nc.sbuf_tensor("cols2", [128, 64], F32))
        wsum4 = cols2[:, 0:16]
        wrs4 = cols2[:, 16:32]
        nmx4 = cols2[:, 32:48]
        smr4 = cols2[:, 48:64]
        ebg = E(nc.sbuf_tensor("ebg", [128, 16], F32))
        cols = E(nc.sbuf_tensor("cols", [128, 20], F32))
        wsum = cols[:, 0:1]
        wrs = cols[:, 1:2]
        nmx = cols[:, 2:3]
        smr = cols[:, 3:4]
        sx = cols[:, 4:6]
        meanc = cols[:, 6:7]
        varc = cols[:, 7:8]
        rstdc = cols[:, 8:9]
        thr_t = cols[:, 9:10]
        m8_t = cols[:, 10:18]
        rrow = am4[0:1, 0:512]
        lrow = st14[0:1, 0:512]
        rw_t = E(nc.sbuf_tensor("rw_t", [128, 16], F32))
        cf_t = E(nc.sbuf_tensor("cf_t", [128, 17], F32))
        ctaps = E(nc.sbuf_tensor("ctaps", [128, 3 * 64], F32))
        nbg2 = E(nc.sbuf_tensor("nbg2", [128, 16], F32))
        normw = E(nc.sbuf_tensor("normw", [128, 1], F32))
        wb_sb = wfg1[:, 0:256]
        gw_sb = wfg1[:, 256:512]
        sgw_sb = wfg1[:, 512:528]
        wf2g = E(nc.sbuf_tensor("wf2g", [128, 512], BF16))
        wg2g = E(nc.sbuf_tensor("wg2g", [128, 512], BF16))
        ident_b = E(nc.sbuf_tensor("ident_b", [128, 128], BF16))
        ident_f = E(nc.sbuf_tensor("ident_f", [128, 128], F32))
        mask01 = E(nc.sbuf_tensor("mask01", [128, 128], BF16))
        Lhalf = E(nc.sbuf_tensor("Lhalf", [128, 128], F32))
        Lfull = E(nc.sbuf_tensor("Lfull", [128, 128], F32))
        Ustrict = E(nc.sbuf_tensor("Ustrict", [128, 128], F32))
        ones_cb = E(nc.sbuf_tensor("ones_cb", [128, 1], BF16))
        ones_cf = E(nc.sbuf_tensor("ones_cf", [128, 1], F32))
        ones_1b = E(nc.sbuf_tensor("ones_1b", [1, 128], BF16))
        cst = E(nc.sbuf_tensor("cst", [128, 4], F32))
        b_lneps = cst[:, 0:1]
        b_l2eps = cst[:, 1:2]
        b_rmse = cst[:, 2:3]
        b_hl128 = cst[:, 3:4]

        psum = [E(nc.psum_tensor(f"ps{i}", [128, 512], F32)) for i in range(8)]
        pool = PsumPool(kb, psum)

        def consts(g):
            for t_, fill, cmp in ((ident_b, 1.0, ALU.not_equal),
                                  (ident_f, 1.0, ALU.not_equal)):
                g.memset(t_[:], 0.0)
                g.drain()
                g.affine_select(out=t_[:], in_=t_[:], compare_op=cmp, fill=fill,
                                base=0, pattern=[[-1, 128]], channel_multiplier=1)
                g.drain()
            for t_, keepval, fillval in ((mask01, 1.0, 0.0), (Lfull, 1.0, 0.0),
                                         (Lhalf, 0.5, -0.5)):
                g.memset(t_[:], keepval)
                g.drain()
                g.affine_select(out=t_[:], in_=t_[:], compare_op=ALU.is_ge,
                                fill=fillval, base=0, pattern=[[1, 128]],
                                channel_multiplier=-1)
                g.drain()
            g.memset(Ustrict[:], 1.0)
            g.drain()
            g.affine_select(out=Ustrict[:], in_=Ustrict[:], compare_op=ALU.is_gt,
                            fill=0.0, base=0, pattern=[[-1, 128]], channel_multiplier=1)
            g.drain()
            g.memset(ones_cb[:], 1.0)
            g.memset(ones_cf[:], 1.0)
            g.memset(ones_1b[:], 1.0)
            g.memset(cst[:, 0:1], LN_EPS)
            g.memset(cst[:, 1:2], L2_EPS)
            g.memset(cst[:, 2:3], 128.0 * RMS_EPS)
            g.memset(cst[:, 3:4], 0.5 * float(np.log(128.0)))
            g.memset(tails[:], 0.0)
            g.memset(cols2[:], 1.0)
            g.memset(elam_col[:], 1.0)
            g.memset(Sk[:], 0.0)
            return g.memset(Sv[:], 0.0)
        g_consts = kb.op("G", consts, inc=True)

        for pi, wn in enumerate(["L0_attn_cq", "L0_attn_ck", "L0_attn_cv"]):
            src = w_d[wn].rearrange("(j p) k -> p j k", p=128)
            dst = ctaps[:, pi * 64:(pi + 1) * 64].rearrange("p (j k) -> p j k", k=4)
            kb.dma("G", dst, src)
        g_bg2 = kb.dma("G", nbg2[:], w_d["L0_attn_bg2"].rearrange("(h p) -> p h", p=128), allow_slow_non_contiguous=True)
        g_nw = kb.dma("G", normw[:], w_d["L0_attn_norm_w"].rearrange("(p one) -> p one", p=128))
        v_nbg = kb.op("V", lambda v: v.tensor_scalar_mul(nbg2[:], nbg2[:], -1.0),
              wait=[g_bg2], inc=True)
        kb.op("S", lambda sc: sc.activation(ebg[:], nbg2[:], AF.Exp),
              wait=[("DV", v_nbg)])

        scrw = {}
        def scratch_write(tag, dst, src):
            if not do_scratch:
                scrw[tag] = ()
                return
            scrw[tag] = kb.dma("G", dst, src)
        for tag, wn in [("wq", "L0_attn_Wq"), ("wk", "L0_attn_Wk"),
                        ("wv", "L0_attn_Wv"), ("ww", "L0_attn_Ww"),
                        ("wf1", "L0_attn_Wf1"), ("wf2", "L0_attn_Wf2"),
                        ("wb", "L0_attn_Wb"), ("wg1", "L0_attn_Wg1"),
                        ("wg2", "L0_attn_Wg2"), ("gatew", "L0_ffn_gate_w"),
                        ("wo", "L0_attn_Wo")]:
            scratch_write(tag, scr[tag], w_d[wn])
        for e in range(E_MOE):
            scratch_write(f"w13_{e}", scr["w13"][e], w_d["L0_ffn_w13"][e])
            scratch_write(f"w2_{e}", scr["w2"][e], w_d["L0_ffn_w2"][e])
        scratch_write("sgu", scr["sgu"], w_d["L0_ffn_shared_gate_up_w"])
        scratch_write("sdw", scr["sdw"], w_d["L0_ffn_shared_down_w"])

        def chunked_rows(ap):
            return ap.rearrange("(j p) c -> p j c", p=128)


        if stop_after == "consts":
            dbg_taps.extend([ident_b, mask01, Lhalf, Lfull, Ustrict])
        for half in range(0 if stop_after == "consts" else n_half):
            th0 = half * TH

            prev_vxn = 0
            for tc in range(NCH):
                xb = x_tok
                yx = kb.dma("Y", xb[:], x_d[th0 + tc * 128: th0 + (tc + 1) * 128, :],
                            wait=[("DV", kb.c("DV")), ("DS", kb.c("DS")), ("DT", kb.c("DT")),
                                  ("QS", kb.c("QS"))] if (tc >= 1 or half > 0) else None)
                kb.waitt("S", yx)
                kb.wait("S", "DV", prev_vxn)
                s_sums = kb.op("S", lambda s, xb=xb: s.activation(
                    xn_tok[:], xb[:], AF.Square, accum_out=sx[:, 1:2]), inc=True)
                kb.waitt("V", yx)
                kb.op("V", lambda v, xb=xb: v.reduce_sum(sx[:, 0:1], xb[:], axis=AX.X))
                kb.op("V", lambda v: v.drain())
                if stop_after == "ln_sums":
                    dbg_taps.append(cols)
                    break
                kb.wait("V", "DS", s_sums)
                kb.op("V", lambda v: v.tensor_scalar_mul(meanc[:], sx[:, 0:1], 1.0 / D))
                kb.op("V", lambda v: v.tensor_scalar_mul(varc[:], sx[:, 1:2], 1.0 / D))
                kb.op("V", lambda v: v.drain())
                kb.op("V", lambda v: v.tensor_tensor(st1[:, 0:1], meanc[:], meanc[:], ALU.mult))
                kb.op("V", lambda v: v.drain())
                v_var = kb.op("V", lambda v: v.tensor_tensor(
                    varc[:], varc[:], st1[:, 0:1], ALU.subtract), inc=True)
                kb.wait("S", "DV", v_var)
                kb.op("S", lambda s: s.activation(rstdc[:], varc[:], AF.Ln, bias=b_lneps))
                s_rstd = kb.op("S", lambda s: s.activation(
                    rstdc[:], rstdc[:], AF.Exp, scale=-0.5), inc=True)
                kb.wait("V", "DS", s_rstd)
                v_xn = kb.op("V", lambda v, xb=xb: v.tensor_scalar(
                    xn_tok[:], xb[:], meanc[:], rstdc[:], ALU.subtract, ALU.mult), inc=True)
                prev_vxn = v_xn
                if stop_after == "ln":
                    dbg_taps.extend([xn_tok, cols])
                    break
                for bq in range(4):
                    b, ps, w8 = pool.acquire()
                    kb.wait("T", "DG", g_consts)
                    kb.wait("T", "DV", v_xn)
                    for jj in range(4):
                        j = bq * 4 + jj
                        src_sl = xn_tok[:, j * 128:(j + 1) * 128]
                        kb.op("T", lambda t, ps=ps, jj=jj, src_sl=src_sl: t.matmul(
                            ps[:, jj * 128:(jj + 1) * 128], src_sl, ident_b[:],
                            start=True, stop=True), wait=w8 if jj == 0 else None)
                    t_tr = kb.op("T", lambda t: t.drain(), inc=True)
                    dst = xnT[:].rearrange("p (j t) -> p j t", j=NDCH)[
                        :, bq * 4:(bq + 1) * 4, tc * 128:(tc + 1) * 128]
                    src = ps[:].rearrange("p (j t) -> p j t", j=4)
                    s_ev = kb.op("S", lambda s, dst=dst, src=src: s.activation(
                        dst, src, AF.Copy), wait=[("DT", t_tr)], inc=True, bank=b)

            if stop_after in ("ln", "ln_sums"):
                break
            xn_done = kb.c("DS")
            if stop_after == "xn":
                dbg_taps.append(xnT)
                break

            y_wb = kb.dma("Y", wb_sb.rearrange("p (j c) -> p j c", j=16),
                          chunked_rows(scr["wb"]),
                          wait=[scrw["wb"], ("DT", kb.c("DT"))])
            y_gw = None
            for j in range(16):
                y_gw = kb.dma("Y", gw_sb[:, j * 16:(j + 1) * 16],
                              scr["gatew"][:, j * 128:(j + 1) * 128],
                              wait=[scrw["gatew"]] if j == 0 else None,
                              transpose=True)
            g_sgw = kb.dma("G", sgw_sb.rearrange("p (one j) -> p one j", one=1),
                           w_d["L0_ffn_sgate_w"].rearrange("one (j p) -> p one j", p=128),
                           allow_slow_non_contiguous=True,
                           wait=[("DT", kb.c("DT"))])
            prev_vct = 0
            for tc in range(NCH):
                b, ps, w8 = pool.acquire()
                kb.wait("T", "DS", xn_done)
                kb.waitt("T", y_wb)
                kb.waitt("T", y_gw)
                kb.waitt("T", g_sgw)
                for j in range(NDCH):
                    xsl = xnT[:, j * TH + tc * 128: j * TH + (tc + 1) * 128]
                    kb.op("T", lambda t, ps=ps, j=j, xsl=xsl: t.matmul(
                        ps[:, 0:16], xsl, wb_sb[:, j * 16:(j + 1) * 16],
                        start=(j == 0), stop=(j == NDCH - 1)), wait=w8 if j == 0 else None)
                for j in range(NDCH):
                    xsl = xnT[:, j * TH + tc * 128: j * TH + (tc + 1) * 128]
                    kb.op("T", lambda t, ps=ps, j=j, xsl=xsl: t.matmul(
                        ps[:, 16:32], xsl, gw_sb[:, j * 16:(j + 1) * 16],
                        start=(j == 0), stop=(j == NDCH - 1)))
                for j in range(NDCH):
                    xsl = xnT[:, j * TH + tc * 128: j * TH + (tc + 1) * 128]
                    kb.op("T", lambda t, ps=ps, j=j, xsl=xsl: t.matmul(
                        ps[:, 32:33], xsl, sgw_sb[:, j:j + 1],
                        start=(j == 0), stop=(j == NDCH - 1)))
                t_b = kb.op("T", lambda t: t.drain(), inc=True)
                kb.wait("S", "DT", t_b)
                kb.wait("S", "DV", prev_vct)
                s_eb = kb.op("S", lambda s, ps=ps: s.activation(
                    rw_t[:], ps[:, 0:16], AF.Exp, scale=-1.0), inc=True, bank=b)
                kb.wait("V", "DS", s_eb)
                kb.op("V", lambda v: v.tensor_scalar_add(rw_t[:], rw_t[:], 1.0))
                kb.op("V", lambda v: v.drain())
                dstb = beta_all[:, tc * 16:(tc + 1) * 16]
                kb.op("V", lambda v, dstb=dstb: v.reciprocal(dstb, rw_t[:]))
                s_er = kb.op("S", lambda s, ps=ps: s.activation(
                    cf_t[:, 0:16], ps[:, 16:32], AF.Exp, accum_out=thr_t[:]), inc=True, bank=b)
                kb.wait("V", "DS", s_er)
                kb.op("V", lambda v: v.reciprocal(smr[:], thr_t[:]))
                kb.op("V", lambda v: v.drain())
                kb.op("V", lambda v: v.tensor_scalar_mul(rw_t[:], cf_t[:, 0:16], smr[:]))
                kb.op("V", lambda v: v.drain())
                kb.op("V", lambda v: v.max(out=m8_t[:], in_=rw_t[:]))
                kb.op("V", lambda v: v.drain())
                kb.op("V", lambda v: v.tensor_single_scalar(thr_t[:], m8_t[:, 3:4], 1.0, ALU.mult))
                kb.op("V", lambda v: v.drain())
                kb.op("V", lambda v: v.tensor_single_scalar(cf_t[:, 0:16], rw_t[:], thr_t[:], ALU.is_ge))
                kb.op("V", lambda v: v.drain())
                kb.op("V", lambda v: v.tensor_tensor(cf_t[:, 0:16], cf_t[:, 0:16], rw_t[:], ALU.mult))
                s_eg = kb.op("S", lambda s, ps=ps: s.activation(
                    st1[:, 0:1], ps[:, 32:33], AF.Exp, scale=-1.0), inc=True, bank=b)
                kb.wait("V", "DS", s_eg)
                kb.op("V", lambda v: v.tensor_scalar_add(st1[:, 0:1], st1[:, 0:1], 1.0))
                kb.op("V", lambda v: v.drain())
                v_sgv = kb.op("V", lambda v: v.reciprocal(cf_t[:, 16:17], st1[:, 0:1]), inc=True)
                b2, ps2, w82 = pool.acquire()
                kb.wait("T", "DV", v_sgv)
                kb.op("T", lambda t, ps2=ps2: t.matmul(
                    ps2[0:17, 0:128], cf_t[:], ident_f[:], start=True, stop=True), wait=w82)
                t_ct = kb.op("T", lambda t: t.drain(), inc=True)
                dstc = coefT[0:17, tc * 128:(tc + 1) * 128]
                v_ct = kb.op("V", lambda v, dstc=dstc, ps2=ps2: v.tensor_copy(
                    dstc, ps2[0:17, 0:128]), wait=[("DT", t_ct)], inc=True, bank=b2)
                prev_vct = v_ct

            for (stag, gdst) in (("wf1", gpre1), ("wg1", gpre2)):
                y_g1 = kb.dma("Y", wfg1[:].rearrange("p (j c) -> p j c", j=16),
                              chunked_rows(scr[stag]),
                              wait=[scrw[stag], ("DT", kb.c("DT"))])
                for tt in range(2):
                    b, ps, w8 = pool.acquire()
                    kb.waitt("T", y_g1)
                    for j in range(NDCH):
                        xsl = xnT[:, j * TH + tt * 512: j * TH + (tt + 1) * 512]
                        kb.op("T", lambda t, ps=ps, j=j, xsl=xsl: t.matmul(
                            ps[:], wfg1[:, j * 128:(j + 1) * 128], xsl,
                            start=(j == 0), stop=(j == NDCH - 1)), wait=w8 if j == 0 else None)
                    t_g = kb.op("T", lambda t: t.drain(), inc=True)
                    dstsl = gdst[:, tt * 512:(tt + 1) * 512]
                    s_ev = kb.op("S", lambda s, dstsl=dstsl, ps=ps: s.activation(
                        dstsl, ps[:], AF.Copy), wait=[("DT", t_g)], inc=True, bank=b)

            phaseB_V = kb.c("DV")
            phaseB_S = kb.c("DS")
            if stop_after == "phaseB":
                dbg_taps.extend([beta_all, coefT[0:17, :], gpre1, gpre2])
                break

            for grp in range(NGRP):
                wsel = {0: "wq", 1: "wk", 2: "wv", 3: "ww"}
                dsel = {0: qf, 1: kf, 2: vf, 3: wff}
                for pi in range(4):
                    for hin in range(NG):
                        head = grp * NG + hin
                        bi = (pi * NG + hin) % 2
                        wbuf = wp[bi]
                        stag = wsel[pi]
                        src = scr[stag].rearrange("(j p) c -> p j c", p=128)[
                            :, :, head * 128:(head + 1) * 128]
                        yw = kb.dma("Y", wbuf[:], src,
                                    wait=[scrw[stag], ("DT", wp_free[bi])])
                        tsl = tails[:, (pi * 16 + head) * 3:(pi * 16 + head) * 3 + 3]
                        cbase = {0: 0, 1: 64, 2: 128, 3: 128}[pi]
                        for tt in range(2):
                            b, ps, w8 = pool.acquire()
                            kb.waitt("T", yw)
                            for j in range(NDCH):
                                xsl = xnT[:, j * TH + tt * 512: j * TH + (tt + 1) * 512]
                                kb.op("T", lambda t, ps=ps, wbuf=wbuf, j=j, xsl=xsl: t.matmul(
                                    ps[:], wbuf[:, j * 128:(j + 1) * 128], xsl,
                                    start=(j == 0), stop=(j == NDCH - 1)),
                                    wait=w8 if j == 0 else None)
                            t_p = kb.op("T", lambda t: t.drain(), inc=True)
                            if tt == 1:
                                wp_free[bi] = t_p
                            v_ev = kb.op("V", lambda v, ps=ps: v.tensor_copy(
                                raw[:, 16:528], ps[:]), wait=[("DT", t_p)], inc=True, bank=b)
                            if tt == 0:
                                kb.op("V", lambda v, tsl=tsl: v.tensor_copy(raw[:, 13:16], tsl))
                                kb.op("V", lambda v: v.tensor_copy(tail3[:, 0:3], raw[:, 525:528]))
                            else:
                                kb.op("V", lambda v: v.tensor_copy(raw[:, 13:16], tail3[:, 0:3]))
                                kb.op("V", lambda v, tsl=tsl: v.tensor_copy(tsl, raw[:, 525:528]))
                            tap0 = ctaps[:, cbase + head * 4: cbase + head * 4 + 1]
                            kb.op("V", lambda v, tap0=tap0: v.tensor_scalar_mul(
                                conv_acc[:], raw[:, 13:525], tap0))
                            for k in range(1, 4):
                                tapk = ctaps[:, cbase + head * 4 + k: cbase + head * 4 + k + 1]
                                rsl = raw[:, 13 + k:525 + k]
                                kb.op("V", lambda v, tapk=tapk, rsl=rsl: v.scalar_tensor_tensor(
                                    conv_acc[:], rsl, tapk, conv_acc[:], ALU.mult, ALU.add))
                            v_cv = kb.op("V", lambda v: v.engine_nop(), inc=True)
                            dproj = dsel[pi][:, hin * TH + tt * 512: hin * TH + (tt + 1) * 512]
                            s_si = kb.op("S", lambda s, dproj=dproj: s.activation(
                                dproj, conv_acc[:], AF.Silu), wait=[("DV", v_cv)], inc=True)
                            kb.wait("V", "DS", s_si)

                c1_done_s = kb.c("DS")
                if stop_after == "proj" and grp == 0:
                    dbg_taps.extend([qf, kf, vf, wff])
                    break

                y_f2 = kb.dma("Y", wf2g[:], scr["wf2"][:, grp * 512:(grp + 1) * 512],
                              wait=[scrw["wf2"], ("DT", kb.c("DT"))])
                y_g2 = kb.dma("Y", wg2g[:], scr["wg2"][:, grp * 512:(grp + 1) * 512],
                              wait=[scrw["wg2"], ("DT", kb.c("DT"))])
                kb.waitt("T", y_f2)
                kb.waitt("T", y_g2)
                kb.wait("T", "DS", c1_done_s)
                for cc in range(NCH):
                    tc0 = cc * 128
                    kb.wait("S", "DV", kb.c("DV"))
                    gp_sl = gpre1[:, tc0:tc0 + 128]
                    wf2_sl = wf2g[:]
                    b_p, ps_p, w8 = pool.acquire()
                    kb.op("T", lambda t, ps_p=ps_p, gp_sl=gp_sl, wf2_sl=wf2_sl: t.matmul(
                        ps_p[:], gp_sl, wf2_sl, start=True, stop=True), wait=w8)
                    t_pp = kb.op("T", lambda t: t.drain(), inc=True)
                    kb.wait("S", "DT", t_pp)
                    s_pe = kb.op("S", lambda s, ps_p=ps_p: s.activation(
                        p_sb[:], ps_p[:], AF.Exp, scale=-1.0), inc=True, bank=b_p)
                    s_p = kb.op("S", lambda s: s.activation(
                        p_sb[:], p_sb[:], AF.Ln, bias=1.0), inc=True)
                    b_ch, ps_ch, w1 = pool.acquire()
                    b_cf, ps_cf, w2_ = pool.acquire()
                    b_rc, ps_rc, w3 = pool.acquire()
                    b_sm, ps_sm, w4 = pool.acquire()
                    b_lc, ps_lc, w5 = pool.acquire()
                    kb.wait("T", "DS", s_p)
                    kb.op("T", lambda t, ps_ch=ps_ch: t.matmul(
                        ps_ch[:], Lhalf[:], p_sb[:], start=True, stop=True), wait=w1)
                    kb.op("T", lambda t, ps_cf=ps_cf: t.matmul(
                        ps_cf[:], Lfull[:], p_sb[:], start=True, stop=True), wait=w2_)
                    kb.op("T", lambda t, ps_rc=ps_rc: t.matmul(
                        ps_rc[:], Ustrict[:], p_sb[:], start=True, stop=True), wait=w3)
                    kb.op("T", lambda t, ps_sm=ps_sm: t.matmul(
                        ps_sm[0:1, :], ones_cf[:], p_sb[:], start=True, stop=True), wait=w4)
                    for hin in range(NG):
                        psl = p_sb[:, hin * 128:(hin + 1) * 128]
                        kb.op("T", lambda t, ps_lc=ps_lc, hin=hin, psl=psl: t.matmul(
                            ps_lc[:, hin:hin + 1], psl, ones_cf[:],
                            start=True, stop=True), wait=w5 if hin == 0 else None)
                    t_cum = kb.op("T", lambda t: t.drain(), inc=True)
                    kb.wait("S", "DT", t_cum)
                    kb.op("S", lambda s, ps_ch=ps_ch: s.activation(lam_c[:], ps_ch[:], AF.Exp, scale=-1.0), inc=True, bank=b_ch)
                    s_ch = kb.op("S", lambda s, ps_ch=ps_ch: s.activation(eu[:], ps_ch[:], AF.Exp, scale=1.0), inc=True, bank=b_ch)
                    s_cf = kb.op("S", lambda s, ps_cf=ps_cf: s.activation(lam_f[:], ps_cf[:], AF.Exp, scale=-1.0), inc=True, bank=b_cf)
                    s_rc = kb.op("S", lambda s, ps_rc=ps_rc: s.activation(rfac[:], ps_rc[:], AF.Exp, scale=-1.0), inc=True, bank=b_rc)
                    s_sm = kb.op("S", lambda s, ps_sm=ps_sm: s.activation(
                        elam_row[:], ps_sm[0:1, 0:512], AF.Exp, scale=-1.0), inc=True, bank=b_sm)
                    ec_out = elam_col[:].rearrange("p (h s) -> p h s", s=4)[:, :, 0:1]
                    s_cum = kb.op("S", lambda s, ps_lc=ps_lc, ec_out=ec_out: s.activation(
                        ec_out, ps_lc[:, 0:4].rearrange("p (h s) -> p h s", s=1),
                        AF.Exp, scale=-1.0), inc=True, bank=b_lc)

                    sk_g = Sk[:, grp * 512:(grp + 1) * 512]
                    sv_g = Sv[:, grp * 512:(grp + 1) * 512]
                    b_tk, ps_tk, w1 = pool.acquire()
                    b_tv, ps_tv, w2_ = pool.acquire()
                    b_tw, ps_tw, w3 = pool.acquire()
                    for hin in range(NG):
                        ksl = kf[:, hin * TH + tc0: hin * TH + tc0 + 128]
                        vsl = vf[:, hin * TH + tc0: hin * TH + tc0 + 128]
                        wsl = wff[:, hin * TH + tc0: hin * TH + tc0 + 128]
                        kb.op("T", lambda t, ps_tk=ps_tk, hin=hin, ksl=ksl: t.matmul(
                            ps_tk[:, hin * 128:(hin + 1) * 128], ksl, ident_b[:],
                            start=True, stop=True), wait=w1 if hin == 0 else None)
                        kb.op("T", lambda t, ps_tv=ps_tv, hin=hin, vsl=vsl: t.matmul(
                            ps_tv[:, hin * 128:(hin + 1) * 128], vsl, ident_b[:],
                            start=True, stop=True), wait=w2_ if hin == 0 else None)
                        kb.op("T", lambda t, ps_tw=ps_tw, hin=hin, wsl=wsl: t.matmul(
                            ps_tw[:, hin * 128:(hin + 1) * 128], wsl, ident_b[:],
                            start=True, stop=True), wait=w3 if hin == 0 else None)
                    t_tr = kb.op("T", lambda t: t.drain(), inc=True)
                    kb.wait("V", "DT", t_tr)
                    v_k = kb.op("V", lambda v, ps_tk=ps_tk: v.tensor_copy(ktok4[:], ps_tk[:]), inc=True, bank=b_tk)
                    v_v = kb.op("V", lambda v, ps_tv=ps_tv: v.tensor_copy(vtok4[:], ps_tv[:]), inc=True, bank=b_tv)
                    v_tr = kb.op("V", lambda v, ps_tw=ps_tw: v.tensor_copy(wtok4[:], ps_tw[:]), inc=True, bank=b_tw)
                    kb.wait("S", "DV", v_tr)
                    for hin in range(NG):
                        wdst = o24[:, hin * 128:(hin + 1) * 128]
                        wsrc = wtok4[:, hin * 128:(hin + 1) * 128]
                        acc_sl = wsum4[:, hin * 4: hin * 4 + 1]
                        kb.op("S", lambda sc, wdst=wdst, wsrc=wsrc, acc_sl=acc_sl: sc.activation(
                            wdst, wsrc, AF.Square, accum_out=acc_sl))
                    kb.op("S", lambda sc: sc.activation(wrs4[:], wsum4[:], AF.Ln, bias=b_l2eps))
                    s_wn = kb.op("S", lambda sc: sc.activation(wrs4[:], wrs4[:], AF.Exp, scale=-0.5), inc=True)
                    kb.wait("V", "DS", max(s_wn, s_cum))
                    wrs_bc = wrs4.rearrange("p (h s) -> p h s", s=4)[:, :, 0:1].broadcast_to([128, NG, 128])
                    kb.op("V", lambda v, wrs_bc=wrs_bc: v.tensor_tensor(
                        wn4[:].rearrange("p (h m) -> p h m", h=NG), 
                        wtok4[:].rearrange("p (h m) -> p h m", h=NG), wrs_bc, ALU.mult))
                    bsl = beta_all[:, cc * 16 + grp * 4: cc * 16 + grp * 4 + 4]
                    beta_bc = bsl.rearrange("p (h s) -> p h s", s=1).broadcast_to([128, NG, 128])
                    kb.op("V", lambda v, beta_bc=beta_bc: v.tensor_tensor(
                        wn4[:].rearrange("p (h m) -> p h m", h=NG),
                        wn4[:].rearrange("p (h m) -> p h m", h=NG), beta_bc, ALU.mult))
                    kb.op("V", lambda v: v.tensor_tensor(u4[:], wn4[:], eu[:], ALU.mult))
                    v_ur = kb.op("V", lambda v: v.tensor_tensor(r4[:], wn4[:], rfac[:], ALU.mult), inc=True)
                    b_a, ps_a, w8 = pool.acquire()
                    for hin in range(NG):
                        ksl = kf[:, hin * TH + tc0: hin * TH + tc0 + 128]
                        qsl = qf[:, hin * TH + tc0: hin * TH + tc0 + 128]
                        kb.op("T", lambda t, ps_a=ps_a, hin=hin, ksl=ksl, qsl=qsl: t.matmul(
                            ps_a[:, hin * 128:(hin + 1) * 128], ksl, qsl,
                            start=True, stop=True), wait=w8 if hin == 0 else None)
                    t_a = kb.op("T", lambda t: t.drain(), inc=True)
                    kb.wait("V", "DT", t_a)
                    v_am = kb.op("V", lambda v, ps_a=ps_a: v.tensor_tensor(
                        am4[:].rearrange("p (h t) -> p h t", h=NG),
                        ps_a[:].rearrange("p (h t) -> p h t", h=NG),
                        mask01[:].unsqueeze(1).broadcast_to([128, NG, 128]), ALU.mult), inc=True, bank=b_a)
                    b_i, ps_i, w8i = pool.acquire()
                    b_j, ps_j, w8j = pool.acquire()
                    kb.wait("T", "DV", max(v_am, v_ur))
                    for hin in range(NG):
                        hs_ = slice(hin * 128, (hin + 1) * 128)
                        qsl = qf[:, hin * TH + tc0: hin * TH + tc0 + 128]
                        sk_sl = Sk[:, (grp * NG + hin) * 128:(grp * NG + hin) * 128 + 128]
                        kb.op("T", lambda t, ps_i=ps_i, hs_=hs_: t.matmul(
                            ps_i[:, hs_], am4[:, hs_], u4[:, hs_],
                            start=True, stop=True), wait=w8i if hin == 0 else None)
                        kb.op("T", lambda t, ps_j=ps_j, hs_=hs_, qsl=qsl, sk_sl=sk_sl: t.matmul(
                            ps_j[:, hs_], qsl, sk_sl,
                            start=True, stop=True), wait=w8j if hin == 0 else None)
                    t_sc = kb.op("T", lambda t: t.drain(), inc=True)
                    kb.wait("V", "DT", t_sc)
                    v_j = kb.op("V", lambda v, ps_j=ps_j: v.tensor_tensor(st14[:], lam_f[:], ps_j[:], ALU.mult), inc=True, bank=b_j)
                    v_i = kb.op("V", lambda v, ps_i=ps_i: v.scalar_tensor_tensor(
                        st24[:], ps_i[:], SCALE, lam_c[:], ALU.mult, ALU.mult), inc=True, bank=b_i)
                    v_sc = kb.op("V", lambda v: v.scalar_tensor_tensor(
                        st24[:], st14[:], SCALE, st24[:], ALU.mult, ALU.add), inc=True)
                    st3d = st24[:].rearrange("p (h m) -> p h m", h=NG)
                    nmx_sl = nmx4.rearrange("p (h s) -> p h s", s=4)[:, :, 0:1]
                    kb.op("V", lambda v, st3d=st3d, nmx_sl=nmx_sl: v.reduce_max(
                        nmx_sl, st3d, axis=AX.X, negate=True))
                    kb.op("V", lambda v: v.drain())
                    nmx_bc = nmx4.rearrange("p (h s) -> p h s", s=4)[:, :, 0:1].broadcast_to([128, NG, 128])
                    v_nm = kb.op("V", lambda v, st3d=st3d, nmx_bc=nmx_bc: v.tensor_tensor(
                        st3d, st3d, nmx_bc, ALU.add), inc=True)
                    kb.wait("S", "DV", v_nm)
                    s_es = kb.op("S", lambda sc: sc.activation(st24[:], st24[:], AF.Exp), inc=True)
                    kb.wait("V", "DS", s_es)
                    smr_sl = smr4.rearrange("p (h s) -> p h s", s=4)[:, :, 0:1]
                    kb.op("V", lambda v, st3d=st3d, smr_sl=smr_sl: v.reduce_sum(smr_sl, st3d, axis=AX.X))
                    kb.op("V", lambda v: v.drain())
                    kb.op("V", lambda v: v.reciprocal(smr4[:], smr4[:]))
                    kb.op("V", lambda v: v.drain())
                    smr_bc = smr4.rearrange("p (h s) -> p h s", s=4)[:, :, 0:1].broadcast_to([128, NG, 128])
                    kb.op("V", lambda v, st3d=st3d, smr_bc=smr_bc: v.tensor_tensor(
                        st14[:].rearrange("p (h m) -> p h m", h=NG), st3d, smr_bc, ALU.mult))
                    kb.op("V", lambda v: v.tensor_tensor(ac4[:], st14[:], lam_c[:], ALU.mult))
                    v_a2 = kb.op("V", lambda v: v.tensor_tensor(af4[:], st14[:], lam_f[:], ALU.mult), inc=True)
                    b_t2, ps_t2, w8 = pool.acquire()
                    b_t3, ps_t3, w83 = pool.acquire()
                    b_t4, ps_t4, w84 = pool.acquire()
                    kb.wait("T", "DV", v_a2)
                    for hin in range(NG):
                        hs_ = slice(hin * 128, (hin + 1) * 128)
                        kb.op("T", lambda t, ps_t2=ps_t2, hs_=hs_: t.matmul(
                            ps_t2[:, hs_], ac4[:, hs_], ident_b[:], start=True, stop=True),
                            wait=w8 if hin == 0 else None)
                        kb.op("T", lambda t, ps_t3=ps_t3, hs_=hs_: t.matmul(
                            ps_t3[:, hs_], af4[:, hs_], ident_b[:], start=True, stop=True),
                            wait=w83 if hin == 0 else None)
                        kb.op("T", lambda t, ps_t4=ps_t4, hs_=hs_: t.matmul(
                            ps_t4[:, hs_], u4[:, hs_], ident_b[:], start=True, stop=True),
                            wait=w84 if hin == 0 else None)
                    t_t2 = kb.op("T", lambda t: t.drain(), inc=True)
                    kb.wait("V", "DT", t_t2)
                    v_c1 = kb.op("V", lambda v, ps_t2=ps_t2: v.tensor_copy(acT4[:], ps_t2[:]), inc=True, bank=b_t2)
                    v_c2 = kb.op("V", lambda v, ps_t3=ps_t3: v.tensor_copy(afT4[:], ps_t3[:]), inc=True, bank=b_t3)
                    v_t2 = kb.op("V", lambda v, ps_t4=ps_t4: v.tensor_copy(uT4[:], ps_t4[:]), inc=True, bank=b_t4)
                    b_pt, ps_pt, w8 = pool.acquire()
                    kb.wait("T", "DV", v_t2)
                    for hin in range(NG):
                        hs_ = slice(hin * 128, (hin + 1) * 128)
                        kb.op("T", lambda t, ps_pt=ps_pt, hs_=hs_: t.matmul(
                            ps_pt[:, hs_], uT4[:, hs_], acT4[:, hs_], start=True, stop=True),
                            wait=w8 if hin == 0 else None)
                    t_pt = kb.op("T", lambda t: t.drain(), inc=True)
                    kb.wait("V", "DT", t_pt)
                    v_pm = kb.op("V", lambda v, ps_pt=ps_pt: v.tensor_tensor(
                        pm4[:].rearrange("p (h t) -> p h t", h=NG),
                        ps_pt[:].rearrange("p (h t) -> p h t", h=NG),
                        mask01[:].unsqueeze(1).broadcast_to([128, NG, 128]), ALU.mult), inc=True, bank=b_pt)
                    b_o, ps_o, w8 = pool.acquire()
                    kb.wait("T", "DV", v_pm)
                    for hin in range(NG):
                        hs_ = slice(hin * 128, (hin + 1) * 128)
                        sv_sl = Sv[:, (grp * NG + hin) * 128:(grp * NG + hin) * 128 + 128]
                        kb.op("T", lambda t, ps_o=ps_o, hs_=hs_: t.matmul(
                            ps_o[:, hs_], vtok4[:, hs_], pm4[:, hs_], start=True, stop=False),
                            wait=w8 if hin == 0 else None)
                        kb.op("T", lambda t, ps_o=ps_o, hs_=hs_, sv_sl=sv_sl: t.matmul(
                            ps_o[:, hs_], sv_sl, afT4[:, hs_], start=False, stop=True))
                    t_o = kb.op("T", lambda t: t.drain(), inc=True)
                    b_su, ps_su, w8 = pool.acquire()
                    b_sb_, ps_sb_, w8b = pool.acquire()
                    b_sv, ps_sv, w8v = pool.acquire()
                    for hin in range(NG):
                        hs_ = slice(hin * 128, (hin + 1) * 128)
                        rowsl = elam_row[0:1, hin * 128:(hin + 1) * 128]
                        kb.op("T", lambda t, ps_su=ps_su, hs_=hs_: t.matmul(
                            ps_su[:, hs_], ktok4[:, hs_], r4[:, hs_], start=True, stop=True),
                            wait=w8 if hin == 0 else None)
                        kb.op("T", lambda t, ps_sb_=ps_sb_, hs_=hs_, rowsl=rowsl: t.matmul(
                            ps_sb_[:, hs_], ones_1b[:], rowsl, start=True, stop=True),
                            wait=w8b if hin == 0 else None)
                        kb.op("T", lambda t, ps_sv=ps_sv, hs_=hs_: t.matmul(
                            ps_sv[:, hs_], r4[:, hs_], vtok4[:, hs_], start=True, stop=True),
                            wait=w8v if hin == 0 else None)
                    t_su = kb.op("T", lambda t: t.drain(), inc=True)
                    kb.wait("V", "DT", t_su)
                    v_s1 = kb.op("V", lambda v, ps_sb_=ps_sb_, sk_g=sk_g: v.tensor_tensor(sktmp4[:], sk_g, ps_sb_[:], ALU.mult), inc=True, bank=b_sb_)
                    v_s2 = kb.op("V", lambda v, ps_su=ps_su, sk_g=sk_g: v.tensor_tensor(sk_g, sktmp4[:], ps_su[:], ALU.add), inc=True, bank=b_su)
                    elam_bc = elam_col[:].rearrange("p (h s) -> p h s", s=4)[:, :, 0:1].broadcast_to([128, NG, 128])
                    kb.op("V", lambda v, elam_bc=elam_bc, sv_g=sv_g: v.tensor_tensor(
                        sktmp4[:].rearrange("p (h m) -> p h m", h=NG),
                        sv_g.rearrange("p (h m) -> p h m", h=NG), elam_bc, ALU.mult))
                    v_su = kb.op("V", lambda v, ps_sv=ps_sv, sv_g=sv_g: v.tensor_tensor(
                        sv_g, sktmp4[:], ps_sv[:], ALU.add), inc=True, bank=b_sv)
                    kb.wait("V", "DT", t_o)
                    v_ocp = kb.op("V", lambda v, ps_o=ps_o: v.tensor_copy(o4[:], ps_o[:]), inc=True, bank=b_o)
                    kb.wait("S", "DV", v_ocp)
                    s_o2 = kb.op("S", lambda sc: sc.activation(o24[:], o4[:], AF.Square), inc=True)
                    b_r, ps_r, w8 = pool.acquire()
                    b_g, ps_g, w8g = pool.acquire()
                    kb.wait("T", "DS", s_o2)
                    kb.op("T", lambda t, ps_r=ps_r: t.matmul(
                        ps_r[0:1, :], ones_cb[:], o24[:], start=True, stop=True), wait=w8)
                    gsl = gpre2[:, tc0:tc0 + 128]
                    for hin in range(NG):
                        hs_ = slice(hin * 128, (hin + 1) * 128)
                        kb.op("T", lambda t, ps_g=ps_g, hs_=hs_, gsl=gsl: t.matmul(
                            ps_g[:, hs_], wg2g[:, hs_], gsl, start=True, stop=True),
                            wait=w8g if hin == 0 else None)
                    t_r = kb.op("T", lambda t: t.drain(), inc=True)
                    kb.wait("S", "DT", t_r)
                    s_ln = kb.op("S", lambda sc, ps_r=ps_r: sc.activation(
                        lrow, ps_r[0:1, :], AF.Ln, bias=b_rmse[0:1, :]), inc=True, bank=b_r)
                    kb.op("S", lambda sc: sc.activation(
                        rrow, lrow, AF.Exp, scale=-0.5, bias=b_hl128[0:1, :]))
                    s_eg2 = kb.op("S", lambda sc, ps_g=ps_g: sc.activation(
                        eg4[:], ps_g[:], AF.Exp, scale=-1.0), inc=True, bank=b_g)
                    b_rb, ps_rb, w8 = pool.acquire()
                    kb.wait("T", "DS", s_eg2)
                    kb.op("T", lambda t, ps_rb=ps_rb: t.matmul(
                        ps_rb[:], ones_1b[:], rrow, start=True, stop=True), wait=w8)
                    t_rb = kb.op("T", lambda t: t.drain(), inc=True)
                    kb.wait("V", "DS", s_eg2)
                    ebg_sl = ebg[:, grp * 4: grp * 4 + 4]
                    ebg_b = ebg_sl.rearrange("p (h s) -> p h s", s=1).broadcast_to([128, NG, 128])
                    kb.op("V", lambda v, ebg_b=ebg_b: v.tensor_tensor(
                        eg4[:].rearrange("p (h t) -> p h t", h=NG),
                        eg4[:].rearrange("p (h t) -> p h t", h=NG), ebg_b, ALU.mult))
                    kb.op("V", lambda v: v.tensor_scalar_add(st24[:], eg4[:], 1.0))
                    kb.op("V", lambda v: v.drain())
                    kb.op("V", lambda v: v.reciprocal(st24[:], st24[:]))
                    kb.wait("V", "DT", t_rb)
                    v_rb = kb.op("V", lambda v, ps_rb=ps_rb: v.tensor_tensor(o4[:], o4[:], ps_rb[:], ALU.mult), inc=True, bank=b_rb)
                    dsto = o_grp[:].rearrange("p (h t) -> p h t", h=NG)[:, :, tc0:tc0 + 128]
                    v_of = kb.op("V", lambda v, dsto=dsto: v.scalar_tensor_tensor(
                        dsto, o4[:].rearrange("p (h t) -> p h t", h=NG), normw[:],
                        st24[:].rearrange("p (h t) -> p h t", h=NG), ALU.mult, ALU.mult), inc=True)

                if stop_after == "scan" and grp == 0:
                    dbg_taps.extend([o_grp, Sk, Sv])
                    break

                scan_done_v = kb.c("DV")
                for dj in range(NDCH):
                    bi = dj % 2
                    wbuf = wp[bi]
                    src = scr["wo"].rearrange("(j p) c -> p j c", p=128)[
                        :, grp * NG:(grp + 1) * NG, dj * 128:(dj + 1) * 128]
                    yw = kb.dma("Y", wbuf[:, 0:512], src,
                                wait=[scrw["wo"], ("DT", wp_free[bi])])
                    for tt in range(2):
                        b, ps, w8 = pool.acquire()
                        kb.waitt("T", yw)
                        kb.wait("T", "DV", scan_done_v)
                        for hin in range(NG):
                            osl = o_grp[:, hin * TH + tt * 512: hin * TH + (tt + 1) * 512]
                            kb.op("T", lambda t, ps=ps, wbuf=wbuf, hin=hin, osl=osl: t.matmul(
                                ps[:], wbuf[:, hin * 128:(hin + 1) * 128], osl,
                                start=(hin == 0), stop=(hin == NG - 1)),
                                wait=w8 if hin == 0 else None)
                        t_wo = kb.op("T", lambda t: t.drain(), inc=True)
                        if tt == 1:
                            wp_free[bi] = t_wo
                        asl = acc[:, dj * TH + tt * 512: dj * TH + (tt + 1) * 512]
                        kb.wait("V", "DT", t_wo)
                        if grp == 0:
                            v_ev = kb.op("V", lambda v, asl=asl, ps=ps: v.tensor_copy(asl, ps[:]), inc=True, bank=b)
                        else:
                            v_ev = kb.op("V", lambda v, asl=asl, ps=ps: v.tensor_tensor(
                                asl, asl, ps[:], ALU.add), inc=True, bank=b)

            if stop_after in ("proj", "scan"):
                break
            if stop_after == "wo":
                dbg_taps.extend([acc, Sk, Sv, o_grp])
                break

            last_vhs = 0
            ffn_t0 = kb.c("DT")
            nat_last_tr = [ffn_t0, ffn_t0]
            natb = [big[:, 13312:16128], big[:, 16128:18944]]

            units = []
            groups = []
            for ex in range(NEXP):
                if ex < E_MOE:
                    w13src, w2src = scr["w13"][ex], scr["w2"][ex]
                    gup_off, up_off = 0, I_MOE
                    dn_col0 = 0
                    wtag13, wtag2 = f"w13_{ex}", f"w2_{ex}"
                else:
                    qq = ex - E_MOE
                    w13src, w2src = scr["sgu"], scr["sdw"]
                    gup_off, up_off = qq * I_MOE, I_SH + qq * I_MOE
                    dn_col0 = qq * I_MOE
                    wtag13, wtag2 = "sgu", "sdw"
                for sup in range(6):
                    ic0 = sup * 2
                    nch_s = 2 if sup < 5 else 1
                    for gu in range(2):
                        roff = (gup_off if gu == 0 else up_off) + ic0 * 128
                        u0 = len(units)
                        sbi = len(groups) % 2
                        if nch_s == 2:
                            for h in range(2):
                                banks = []
                                for a in range(4):
                                    pieces = [2 * a * 128, 1024 + 2 * a * 128,
                                              (2 * a + 1) * 128,
                                              1024 + (2 * a + 1) * 128]
                                    banks.append((512, (h * 8 + 2 * a) * 256, pieces))
                                units.append(dict(
                                    src=w13src[roff: roff + 256,
                                               h * 1024:(h + 1) * 1024]
                                    .rearrange("(q p) d -> p q d", p=128),
                                    ncols=2048, qw=1024, tag=wtag13,
                                    banks=banks, slab_t=upslab, sbi=sbi))
                        else:
                            banks = [(512, a * 512,
                                      [(4 * a + m) * 128 for m in range(4)])
                                     for a in range(4)]
                            units.append(dict(
                                src=w13src[roff: roff + 128, :],
                                ncols=2048, qw=0, tag=wtag13,
                                banks=banks, slab_t=upslab, sbi=sbi))
                        groups.append(dict(kind="up", ex=ex, sup=sup, gu=gu,
                                           ic0=ic0, nch_s=nch_s,
                                           u0=u0, u1=len(units)))
                for hd in range(8):
                    u0 = len(units)
                    sbi = len(groups) % 2
                    banks = []
                    for icc in range(5):
                        pieces = [2 * icc * 128, 1408 + 2 * icc * 128,
                                  (2 * icc + 1) * 128, 1408 + (2 * icc + 1) * 128]
                        banks.append((512, icc * 512, pieces))
                    banks.append((256, 2560, [10 * 128, 1408 + 10 * 128]))
                    units.append(dict(
                        src=w2src[hd * 256: hd * 256 + 256,
                                  dn_col0: dn_col0 + 1408]
                        .rearrange("(q p) c -> p q c", p=128),
                        ncols=2816, qw=1408, tag=wtag2,
                        banks=banks, slab_t=dnslab, sbi=sbi))
                    groups.append(dict(kind="dn", ex=ex, hd0=hd * 2,
                                       u0=u0, u1=len(units)))
            for i, u in enumerate(units):
                u["idx"] = i

            def emit_unit_dma(u):
                np_ = u["idx"] % 2
                dst = natb[np_][:, 0:u["ncols"]]
                if u["qw"]:
                    dst = dst.rearrange("p (q d) -> p q d", d=u["qw"])
                u["dma_tok"] = kb.dma1("Y", dst, u["src"],
                                       wait=[scrw[u["tag"]],
                                             ("DT", nat_last_tr[np_])])

            def emit_unit_tr(u):
                np_ = u["idx"] % 2
                slabv = u["slab_t"][u["sbi"]]
                binfo = []
                for bidx, (w, soff, pieces) in enumerate(u["banks"]):
                    b, ps, w8 = pool.acquire()
                    for k, noff in enumerate(pieces):
                        nsl = natb[np_][:, noff: noff + 128]
                        wl = list(w8) if k == 0 else []
                        if bidx == 0 and k == 0:
                            wl = wl + [u["dma_tok"]]
                        kb.op("T", lambda t, ps=ps, k=k, nsl=nsl: t.matmul(
                            ps[:, k * 128:(k + 1) * 128], nsl, ident_b[:],
                            start=True, stop=True), wait=wl or None)
                    binfo.append((b, ps, w, soff))
                trd = kb.op("T", lambda t: t.drain(), inc=True)
                nat_last_tr[np_] = trd
                ev = 0
                for (b, ps, w, soff) in binfo:
                    dstv = slabv[:, soff: soff + w]
                    ev = kb.op("S", lambda s, dstv=dstv, ps=ps, w=w: s.activation(
                        dstv, ps[:, 0:w], AF.Copy),
                        wait=[("DT", trd)], inc=True, bank=b)
                u["ev"] = ev

            ud = ut = 0
            cb_tr_prev = 0
            v_cb = 0
            t_cb = 0
            silu_ds = {}
            up_done_v = 0
            cur_ex = -1
            ng = len(groups)
            while ud < groups[min(1, ng - 1)]["u1"]:
                emit_unit_dma(units[ud])
                ud += 1
            for gi, g in enumerate(groups):
                tgt_t = groups[min(gi + 1, ng - 1)]["u1"]
                while ut < tgt_t:
                    emit_unit_tr(units[ut])
                    ut += 1
                tgt_d = groups[min(gi + 2, ng - 1)]["u1"]
                while ud < tgt_d:
                    emit_unit_dma(units[ud])
                    ud += 1
                if g["ex"] != cur_ex:
                    cur_ex = g["ex"]
                    cf_row = cur_ex if cur_ex < E_MOE else 16
                    if cur_ex == 0:
                        wl = [("DV", phaseB_V), ("DT", kb.c("DT"))]
                    else:
                        wl = [("DT", cb_tr_prev)]
                    y_cr = kb.dma("Y", coef_row[:], coefT[cf_row:cf_row + 1, :],
                                  wait=wl)
                    for tt in range(2):
                        b_cb, ps_cb, w8 = pool.acquire()
                        kb.waitt("T", y_cr)
                        rsl = coef_row[0:1, tt * 512:(tt + 1) * 512]
                        kb.op("T", lambda t, ps_cb=ps_cb, rsl=rsl: t.matmul(
                            ps_cb[:], ones_1b[:], rsl, start=True, stop=True), wait=w8)
                        t_cb = kb.op("T", lambda t: t.drain(), inc=True)
                        dstcb = coef_bc[:, tt * 512:(tt + 1) * 512]
                        v_cb = kb.op("V", lambda v, dstcb=dstcb, ps_cb=ps_cb: v.tensor_copy(
                            dstcb, ps_cb[:]), wait=[("DT", t_cb)], inc=True, bank=b_cb)
                    cb_tr_prev = t_cb
                    silu_ds = {}
                sbi = gi % 2
                kb.wait("T", "DS", units[g["u1"] - 1]["ev"])
                if g["kind"] == "up":
                    slabv = upslab[sbi]
                    nch_s, ic0, gu, sup = g["nch_s"], g["ic0"], g["gu"], g["sup"]
                    pst = 256 if nch_s == 2 else 128
                    t_last = 0
                    for icx in range(nch_s):
                        for tt in range(2):
                            b, ps, w8 = pool.acquire()
                            for j in range(NDCH):
                                xsl = xnT[:, j * TH + tt * 512: j * TH + (tt + 1) * 512]
                                wslab = slabv[:, j * pst + icx * 128: j * pst + icx * 128 + 128]
                                kb.op("T", lambda t, ps=ps, wslab=wslab, xsl=xsl, j=j: t.matmul(
                                    ps[:], wslab, xsl, start=(j == 0), stop=(j == NDCH - 1)),
                                    wait=w8 if j == 0 else None)
                            t_up = kb.op("T", lambda t: t.drain(), inc=True)
                            t_last = t_up
                            if gu == 0:
                                dst = sgtmp[:, icx * TH + tt * 512: icx * TH + (tt + 1) * 512]
                                s_ev = kb.op("S", lambda s, dst=dst, ps=ps: s.activation(
                                    dst, ps[:], AF.Silu),
                                    wait=[("DT", t_up), ("DV", last_vhs)], inc=True, bank=b)
                                silu_ds[(sup, icx, tt)] = s_ev
                            else:
                                hsl = hs[:, (ic0 + icx) * TH + tt * 512: (ic0 + icx) * TH + (tt + 1) * 512]
                                ssl = sgtmp[:, icx * TH + tt * 512: icx * TH + (tt + 1) * 512]
                                cbsl = coef_bc[:, tt * 512:(tt + 1) * 512]
                                kb.wait("V", "DT", t_up)
                                kb.wait("V", "DV", v_cb)
                                kb.wait("V", "DS", silu_ds[(sup, icx, tt)])
                                v_h1 = kb.op("V", lambda v, hsl=hsl, ssl=ssl, ps=ps: v.tensor_tensor(
                                    hsl, ssl, ps[:], ALU.mult), inc=True, bank=b)
                                v_hs = kb.op("V", lambda v, hsl=hsl, cbsl=cbsl: v.tensor_tensor(
                                    hsl, hsl, cbsl, ALU.mult), inc=True)
                                last_vhs = v_hs
                    slab_free[sbi] = t_last
                else:
                    slabv = dnslab[sbi]
                    if g["hd0"] == 0:
                        up_done_v = kb.c("DV")
                    hd0 = g["hd0"]
                    t_last = 0
                    for hdc in range(2):
                        for tt in range(2):
                            b, ps, w8 = pool.acquire()
                            kb.wait("T", "DV", up_done_v)
                            for ic in range(11):
                                hsl = hs[:, ic * TH + tt * 512: ic * TH + (tt + 1) * 512]
                                wslab = slabv[:, ic * 256 + hdc * 128: ic * 256 + hdc * 128 + 128]
                                kb.op("T", lambda t, ps=ps, wslab=wslab, hsl=hsl, ic=ic: t.matmul(
                                    ps[:], wslab, hsl, start=(ic == 0), stop=(ic == 10)),
                                    wait=w8 if ic == 0 else None)
                            t_dn = kb.op("T", lambda t: t.drain(), inc=True)
                            t_last = t_dn
                            asl = acc[:, (hd0 + hdc) * TH + tt * 512: (hd0 + hdc) * TH + (tt + 1) * 512]
                            kb.wait("V", "DT", t_dn)
                            v_dn = kb.op("V", lambda v, asl=asl, ps=ps: v.tensor_tensor(
                                asl, asl, ps[:], ALU.add), inc=True, bank=b)
                    slab_free[sbi] = t_last

            if stop_after == "ffn":
                dbg_taps.append(acc)
                break

            ffn_done_v = kb.c("DV")
            for tc in range(NCH):
                xb = x_tok_e
                yx = kb.dma("Y", xb[:], x_d[th0 + tc * 128: th0 + (tc + 1) * 128, :],
                            wait=[("QS", kb.c("QS")), ("DT", kb.c("DT"))] if tc == 0 else [("QS", kb.c("QS"))])
                for bq in range(4):
                    b, ps, w8 = pool.acquire()
                    kb.wait("T", "DV", ffn_done_v)
                    for jj in range(4):
                        dj = bq * 4 + jj
                        asl = acc[:, dj * TH + tc * 128: dj * TH + (tc + 1) * 128]
                        kb.op("T", lambda t, ps=ps, jj=jj, asl=asl: t.matmul(
                            ps[:, jj * 128:(jj + 1) * 128], asl, ident_f[:],
                            start=True, stop=True), wait=w8 if jj == 0 else None)
                    t_tr = kb.op("T", lambda t: t.drain(), inc=True)
                    kb.wait("V", "DT", t_tr)
                    kb.waitt("V", yx)
                    xsl = xb[:, bq * 512:(bq + 1) * 512]
                    v_ev = kb.op("V", lambda v, xsl=xsl, ps=ps: v.tensor_tensor(
                        xsl, xsl, ps[:], ALU.add), inc=True, bank=b)
                kb.dma("S", out_d[th0 + tc * 128: th0 + (tc + 1) * 128, :], xb[:],
                       wait=[("DV", kb.c("DV"))])

        if stop_after is not None:
            kb.wait("G", "DV", kb.c("DV"))
            kb.wait("G", "DS", kb.c("DS"))
            kb.wait("G", "DT", kb.c("DT"))
            row = 0
            for tens in dbg_taps:
                ap = tens if isinstance(tens, bass.AP) else tens[:]
                P, FREE = ap.shape
                npieces = (FREE + D - 1) // D
                for pc in range(npieces):
                    w = min(D, FREE - pc * D)
                    if row + P <= T:
                        kb.dma("G", out_d[row:row + P, 0:w], ap[:, pc * D:pc * D + w])
                    row += 128

        kb.wait("G", "QS", kb.c("QS"))
        for i in range(KB.NYS):
            kb.wait("G", f"QY{i}", kb.c(f"QY{i}"))
        kb.wait("G", "QG", kb.c("QG"))

        build_program.last_kb = kb
        block = E(nc.Block())
        block.sync(lambda e: [f(e) for f in kb.ops["Y"]])
        block.gpsimd(lambda e: [f(e) for f in kb.ops["G"]])
        block.vector(lambda e: [f(e) for f in kb.ops["V"]])
        block.scalar(lambda e: [f(e) for f in kb.ops["S"]])
        block.tensor(lambda e: [f(e) for f in kb.ops["T"]])

    return nc


def kernel(**inputs):
    x = np.ascontiguousarray(np.asarray(inputs["L0_attn_hidden_states"], np.float32))
    nc = build_program()
    in_maps = []
    for b in range(8):
        m = {"x": np.ascontiguousarray(x[b])}
        for n in W_NAMES:
            m[n] = np.ascontiguousarray(np.asarray(inputs[n], np.float32))
        in_maps.append(m)
    res = run_bass_kernel_spmd(nc, in_maps, core_ids=list(range(8)))
    out = np.stack([np.asarray(r["out"], np.float32) for r in res.results], axis=0)
    return out


# version 08 (final)
# speedup vs baseline: 1.5342x; 1.1673x over previous
import numpy as np
from contextlib import ExitStack

import concourse.bass as bass
import concourse.mybir as mybir
from concourse.bass_utils import run_bass_kernel_spmd

F32 = mybir.dt.float32
BF16 = mybir.dt.bfloat16
AF = mybir.ActivationFunctionType
ALU = mybir.AluOpType
AX = mybir.AxisListType

T = 2048
TH = 1024
D = 2048
NDCH = 16
H = 16
DK = DV = M = 128
C = 128
NCH = TH // C
NG = 4
NGRP = H // NG
E_MOE = 16
I_MOE = 1408
I_SH = 5632
NEXP = 20
SCALE = float(DK) ** -0.5
LN_EPS = 1e-5
RMS_EPS = 1e-5
L2_EPS = 1e-6

W_NAMES = ["L0_attn_Wq", "L0_attn_Wk", "L0_attn_Wv", "L0_attn_Ww", "L0_attn_cq",
           "L0_attn_ck", "L0_attn_cv", "L0_attn_Wf1", "L0_attn_Wf2", "L0_attn_Wb",
           "L0_attn_Wg1", "L0_attn_Wg2", "L0_attn_bg2", "L0_attn_norm_w",
           "L0_attn_Wo", "L0_ffn_gate_w", "L0_ffn_sgate_w", "L0_ffn_w13",
           "L0_ffn_w2", "L0_ffn_shared_gate_up_w", "L0_ffn_shared_down_w"]


class KB:
    ENGS = ("T", "V", "S", "G", "Y")
    NYS = 10

    def __init__(self, nc, ctx):
        self.nc = nc
        E = ctx.enter_context
        self.ops = {e: [] for e in self.ENGS}
        self.sem = {}
        self.cnt = {}
        names = (["DT", "DV", "DS", "DG", "QG", "QS"]
                 + [f"QY{i}" for i in range(self.NYS)]
                 + [f"PS{i}" for i in range(8)])
        for name in names:
            self.sem[name] = E(nc.semaphore(name))
            self.cnt[name] = 0
        self.done_of = {"T": "DT", "V": "DV", "S": "DS", "G": "DG"}
        self.q_of = {"G": "QG", "S": "QS"}
        self.ynext = 0
        self.trace = []

    def op(self, eng, f, inc=False, wait=None, drain=None, bank=None):
        semname = self.done_of.get(eng)
        if inc:
            self.cnt[semname] += 1
        sem_obj = self.sem.get(semname) if inc else None
        bank_sem = None
        if bank is not None:
            bn = f"PS{bank}"
            self.cnt[bn] += 1
            bank_sem = self.sem[bn]
        wait = self._flat(wait)
        waits = [(self.sem[s], v) for (s, v) in wait]
        incs = ([(semname, 1)] if inc else []) + (
            [(f"PS{bank}", 1)] if bank is not None else [])
        self.trace.append((eng, wait, incs))
        if drain is None:
            drain = eng in ("V", "S", "G")

        if bank_sem is not None:
            drain = True

        def run(e, f=f, waits=waits, sem_obj=sem_obj, drain=drain, bank_sem=bank_sem):
            for s, v in waits:
                e.wait_ge(s, v)
            inst = f(e)
            if sem_obj is not None:
                inst.then_inc(sem_obj, 1)
            if drain:
                d = e.drain()
                if bank_sem is not None:
                    d.then_inc(bank_sem, 1)
        self.ops[eng].append(run)
        return self.cnt[semname] if inc else None

    @staticmethod
    def _flat(wait):
        out = []
        for w in (wait or []):
            if isinstance(w, tuple) and isinstance(w[0], str):
                out.append(w)
            else:
                out.extend(w)
        return [(s, v) for (s, v) in out if v > 0]

    def dma(self, eng, out, in_, wait=None, **kw):
        wait = self._flat(wait)
        if eng == "Y":
            q = f"QY{self.ynext}"
            self.ynext = (self.ynext + 1) % self.NYS
        else:
            q = self.q_of[eng]
        if self.cnt[q] > 0:
            wait = wait + [(q, self.cnt[q])]
        self.cnt[q] += 16
        sem_obj = self.sem[q]
        waits = [(self.sem[s], v) for (s, v) in wait]
        self.trace.append((eng, wait, [(q, 16)]))

        def run(e, out=out, in_=in_, kw=kw, waits=waits, sem_obj=sem_obj):
            for s, v in waits:
                e.wait_ge(s, v)
            e.dma_start(out=out, in_=in_, **kw).then_inc(sem_obj, 16)
        self.ops[eng].append(run)
        if eng == "Y":
            return tuple((f"QY{i}", self.cnt[f"QY{i}"]) for i in range(self.NYS))
        return ((q, self.cnt[q]),)

    def dma1(self, eng, out, in_, wait=None, **kw):
        qprev = self.ynext
        tok = self.dma(eng, out, in_, wait=wait, **kw)
        if eng == "Y":
            return ((f"QY{qprev}", self.cnt[f"QY{qprev}"]),)
        return tok

    def wait(self, eng, sem_name, value):
        if value > 0:
            self.trace.append((eng, [(sem_name, value)], []))
            self.ops[eng].append(
                lambda e, s=self.sem[sem_name], v=value: e.wait_ge(s, v))

    def waitt(self, eng, token):
        for (sname, v) in token:
            self.wait(eng, sname, v)

    def c(self, sem_name):
        return self.cnt[sem_name]


class PsumPool:

    def __init__(self, kb, tiles):
        self.kb = kb
        self.tiles = tiles
        self.rr = 0

    def acquire(self):
        b = self.rr
        self.rr = (self.rr + 1) % 8
        wait = [(f"PS{b}", self.kb.cnt[f"PS{b}"])]
        return b, self.tiles[b], wait


def build_program(stop_after=None, n_half=2, do_scratch=True):
    nc = bass.Bass("TRN2", target_bir_lowering=False)

    x_d = nc.dram_tensor("x", [T, D], F32, kind="ExternalInput").ap()
    shapes = {
        "L0_attn_Wq": [D, D], "L0_attn_Wk": [D, D], "L0_attn_Wv": [D, D],
        "L0_attn_Ww": [D, D], "L0_attn_cq": [D, 4], "L0_attn_ck": [D, 4],
        "L0_attn_cv": [D, 4], "L0_attn_Wf1": [D, 128], "L0_attn_Wf2": [128, D],
        "L0_attn_Wb": [D, 16], "L0_attn_Wg1": [D, 128], "L0_attn_Wg2": [128, D],
        "L0_attn_bg2": [D], "L0_attn_norm_w": [128], "L0_attn_Wo": [D, D],
        "L0_ffn_gate_w": [16, D], "L0_ffn_sgate_w": [1, D],
        "L0_ffn_w13": [E_MOE, 2 * I_MOE, D], "L0_ffn_w2": [E_MOE, D, I_MOE],
        "L0_ffn_shared_gate_up_w": [2 * I_SH, D], "L0_ffn_shared_down_w": [D, I_SH],
    }
    w_d = {n: nc.dram_tensor(n, s, F32, kind="ExternalInput").ap()
           for n, s in shapes.items()}
    out_d = nc.dram_tensor("out", [T, D], F32, kind="ExternalOutput").ap()

    scr_shapes = {
        "wq": [D, D], "wk": [D, D], "wv": [D, D], "ww": [D, D],
        "wf1": [D, 128], "wf2": [128, D], "wb": [D, 16],
        "wg1": [D, 128], "wg2": [128, D], "wo": [D, D],
        "gatew": [16, D],
        "w13": [E_MOE, 2 * I_MOE, D], "w2": [E_MOE, D, I_MOE],
        "sgu": [2 * I_SH, D], "sdw": [D, I_SH],
    }
    scr = {n: nc.dram_tensor("scr_" + n, s, BF16, kind="Internal").ap()
           for n, s in scr_shapes.items()}

    dbg_taps = []

    with ExitStack() as ctx:
        E = ctx.enter_context
        kb = KB(nc, ctx)

        xnT = E(nc.sbuf_tensor("xnT", [128, NDCH * TH], BF16))
        acc = E(nc.sbuf_tensor("acc", [128, NDCH * TH], F32))
        big = E(nc.sbuf_tensor("big", [128, 20480], BF16))
        qf = big[:, 0:4096]
        kf = big[:, 4096:8192]
        vf = big[:, 8192:12288]
        wff = big[:, 12288:16384]
        o_grp = big[:, 16384:20480]
        hs = big[:, 0:11264]
        sgtmp = big[:, 11264:13312]

        slab = [E(nc.sbuf_tensor(f"slab{i}", [128, 4096], BF16)) for i in range(2)]
        wp = [slab[0][:, 0:2048], slab[1][:, 0:2048]]
        upslab = [slab[0], slab[1]]
        dnslab = [slab[0][:, 0:2816], slab[1][:, 0:2816]]
        wp_free = up_free = dn_free = slab_free = [0, 0]

        x_tok = acc[:, 0:2048]
        x_tok_e = big[:, 0:4096].bitcast(F32)
        xn_tok = E(nc.sbuf_tensor("xn_tok", [128, D], BF16))
        raw = E(nc.sbuf_tensor("raw", [128, 528], F32))
        conv_acc = E(nc.sbuf_tensor("conv_acc", [128, 512], F32))
        tail3 = E(nc.sbuf_tensor("tail3", [128, 4], F32))
        tails = E(nc.sbuf_tensor("tails", [128, 4 * 16 * 3], F32))
        p_sb = E(nc.sbuf_tensor("p_sb", [128, 512], F32))
        lam_c = E(nc.sbuf_tensor("lam_c", [128, 512], F32))
        lam_f = E(nc.sbuf_tensor("lam_f", [128, 512], F32))
        eu = E(nc.sbuf_tensor("eu", [128, 512], BF16))
        rfac = E(nc.sbuf_tensor("rfac", [128, 512], BF16))
        elam_row = E(nc.sbuf_tensor("elam_row", [1, 512], BF16))
        elam_col = E(nc.sbuf_tensor("elam_col", [128, 16], F32))
        gpre1 = E(nc.sbuf_tensor("gpre1", [128, TH], BF16))
        gpre2 = E(nc.sbuf_tensor("gpre2", [128, TH], BF16))
        Sk = E(nc.sbuf_tensor("Sk", [128, H * M], BF16))
        Sv = E(nc.sbuf_tensor("Sv", [128, H * DV], BF16))
        beta_all = E(nc.sbuf_tensor("beta_all", [128, NCH * 16], F32))
        coefT = E(nc.sbuf_tensor("coefT", [32, TH], BF16))
        coef_bc = E(nc.sbuf_tensor("coef_bc", [128, TH], BF16))
        st1 = E(nc.sbuf_tensor("st1", [128, 128], F32))
        wfg1 = E(nc.sbuf_tensor("wfg1", [128, 2048], BF16))
        ktok4 = wfg1[:, 0:512]
        vtok4 = wfg1[:, 512:1024]
        wn4 = wfg1[:, 1024:1536]
        u4 = wfg1[:, 1536:2048]
        r4 = E(nc.sbuf_tensor("r4", [128, 512], BF16))
        am4 = E(nc.sbuf_tensor("am4", [128, 512], BF16))
        st14 = E(nc.sbuf_tensor("st14", [128, 512], F32))
        st24 = E(nc.sbuf_tensor("st24", [128, 512], F32))
        coef_row = st24[:].bitcast(BF16)[0:1, 0:TH]
        ac4 = E(nc.sbuf_tensor("ac4", [128, 512], BF16))
        af4 = E(nc.sbuf_tensor("af4", [128, 512], BF16))
        o4 = af4
        o24 = E(nc.sbuf_tensor("o24", [128, 512], BF16))
        sktmp4 = ac4
        wtok4 = wn4
        pm4 = am4
        acT4 = ac4
        afT4 = af4
        uT4 = u4
        eg4 = p_sb[:].bitcast(BF16)[:, 0:512]
        cols2 = E(nc.sbuf_tensor("cols2", [128, 64], F32))
        wsum4 = cols2[:, 0:16]
        wrs4 = cols2[:, 16:32]
        nmx4 = cols2[:, 32:48]
        smr4 = cols2[:, 48:64]
        ebg = E(nc.sbuf_tensor("ebg", [128, 16], F32))
        cols = E(nc.sbuf_tensor("cols", [128, 20], F32))
        wsum = cols[:, 0:1]
        wrs = cols[:, 1:2]
        nmx = cols[:, 2:3]
        smr = cols[:, 3:4]
        sx = cols[:, 4:6]
        meanc = cols[:, 6:7]
        varc = cols[:, 7:8]
        rstdc = cols[:, 8:9]
        thr_t = cols[:, 9:10]
        m8_t = cols[:, 10:18]
        rrow = am4[0:1, 0:512]
        lrow = st14[0:1, 0:512]
        rw_t = E(nc.sbuf_tensor("rw_t", [128, 16], F32))
        cf_t = E(nc.sbuf_tensor("cf_t", [128, 17], F32))
        ctaps = E(nc.sbuf_tensor("ctaps", [128, 3 * 64], F32))
        nbg2 = E(nc.sbuf_tensor("nbg2", [128, 16], F32))
        normw = E(nc.sbuf_tensor("normw", [128, 1], F32))
        wb_sb = wfg1[:, 0:256]
        gw_sb = wfg1[:, 256:512]
        sgw_sb = wfg1[:, 512:528]
        wf2g = E(nc.sbuf_tensor("wf2g", [128, 512], BF16))
        wg2g = E(nc.sbuf_tensor("wg2g", [128, 512], BF16))
        ident_b = E(nc.sbuf_tensor("ident_b", [128, 128], BF16))
        ident_f = E(nc.sbuf_tensor("ident_f", [128, 128], F32))
        mask01 = E(nc.sbuf_tensor("mask01", [128, 128], BF16))
        Lhalf = E(nc.sbuf_tensor("Lhalf", [128, 128], F32))
        Lfull = E(nc.sbuf_tensor("Lfull", [128, 128], F32))
        Ustrict = E(nc.sbuf_tensor("Ustrict", [128, 128], F32))
        ones_cb = E(nc.sbuf_tensor("ones_cb", [128, 1], BF16))
        ones_cf = E(nc.sbuf_tensor("ones_cf", [128, 1], F32))
        ones_1b = E(nc.sbuf_tensor("ones_1b", [1, 128], BF16))
        cst = E(nc.sbuf_tensor("cst", [128, 4], F32))
        b_lneps = cst[:, 0:1]
        b_l2eps = cst[:, 1:2]
        b_rmse = cst[:, 2:3]
        b_hl128 = cst[:, 3:4]

        psum = [E(nc.psum_tensor(f"ps{i}", [128, 512], F32)) for i in range(8)]
        pool = PsumPool(kb, psum)

        def consts(g):
            for t_, fill, cmp in ((ident_b, 1.0, ALU.not_equal),
                                  (ident_f, 1.0, ALU.not_equal)):
                g.memset(t_[:], 0.0)
                g.drain()
                g.affine_select(out=t_[:], in_=t_[:], compare_op=cmp, fill=fill,
                                base=0, pattern=[[-1, 128]], channel_multiplier=1)
                g.drain()
            for t_, keepval, fillval in ((mask01, 1.0, 0.0), (Lfull, 1.0, 0.0),
                                         (Lhalf, 0.5, -0.5)):
                g.memset(t_[:], keepval)
                g.drain()
                g.affine_select(out=t_[:], in_=t_[:], compare_op=ALU.is_ge,
                                fill=fillval, base=0, pattern=[[1, 128]],
                                channel_multiplier=-1)
                g.drain()
            g.memset(Ustrict[:], 1.0)
            g.drain()
            g.affine_select(out=Ustrict[:], in_=Ustrict[:], compare_op=ALU.is_gt,
                            fill=0.0, base=0, pattern=[[-1, 128]], channel_multiplier=1)
            g.drain()
            g.memset(ones_cb[:], 1.0)
            g.memset(ones_cf[:], 1.0)
            g.memset(ones_1b[:], 1.0)
            g.memset(cst[:, 0:1], LN_EPS)
            g.memset(cst[:, 1:2], L2_EPS)
            g.memset(cst[:, 2:3], 128.0 * RMS_EPS)
            g.memset(cst[:, 3:4], 0.5 * float(np.log(128.0)))
            g.memset(tails[:], 0.0)
            g.memset(cols2[:], 1.0)
            g.memset(elam_col[:], 1.0)
            g.memset(Sk[:], 0.0)
            return g.memset(Sv[:], 0.0)
        g_consts = kb.op("G", consts, inc=True)

        for pi, wn in enumerate(["L0_attn_cq", "L0_attn_ck", "L0_attn_cv"]):
            src = w_d[wn].rearrange("(j p) k -> p j k", p=128)
            dst = ctaps[:, pi * 64:(pi + 1) * 64].rearrange("p (j k) -> p j k", k=4)
            kb.dma("G", dst, src)
        g_bg2 = kb.dma("G", nbg2[:], w_d["L0_attn_bg2"].rearrange("(h p) -> p h", p=128), allow_slow_non_contiguous=True)
        g_nw = kb.dma("G", normw[:], w_d["L0_attn_norm_w"].rearrange("(p one) -> p one", p=128))
        v_nbg = kb.op("V", lambda v: v.tensor_scalar_mul(nbg2[:], nbg2[:], -1.0),
              wait=[g_bg2], inc=True)
        kb.op("S", lambda sc: sc.activation(ebg[:], nbg2[:], AF.Exp),
              wait=[("DV", v_nbg)])

        scrw = {}
        def scratch_write(tag, dst, src):
            if not do_scratch:
                scrw[tag] = ()
                return
            scrw[tag] = kb.dma("G", dst, src)
        for tag, wn in [("wq", "L0_attn_Wq"), ("wk", "L0_attn_Wk"),
                        ("wv", "L0_attn_Wv"), ("ww", "L0_attn_Ww"),
                        ("wf1", "L0_attn_Wf1"), ("wf2", "L0_attn_Wf2"),
                        ("wb", "L0_attn_Wb"), ("wg1", "L0_attn_Wg1"),
                        ("wg2", "L0_attn_Wg2"), ("gatew", "L0_ffn_gate_w"),
                        ("wo", "L0_attn_Wo")]:
            scratch_write(tag, scr[tag], w_d[wn])
        for e in range(E_MOE):
            scratch_write(f"w13_{e}", scr["w13"][e], w_d["L0_ffn_w13"][e])
            scratch_write(f"w2_{e}", scr["w2"][e], w_d["L0_ffn_w2"][e])
        scratch_write("sgu", scr["sgu"], w_d["L0_ffn_shared_gate_up_w"])
        scratch_write("sdw", scr["sdw"], w_d["L0_ffn_shared_down_w"])

        def chunked_rows(ap):
            return ap.rearrange("(j p) c -> p j c", p=128)


        if stop_after == "consts":
            dbg_taps.extend([ident_b, mask01, Lhalf, Lfull, Ustrict])
        for half in range(0 if stop_after == "consts" else n_half):
            th0 = half * TH

            prev_vxn = 0
            for tc in range(NCH):
                xb = x_tok
                yx = kb.dma("Y", xb[:], x_d[th0 + tc * 128: th0 + (tc + 1) * 128, :],
                            wait=[("DV", kb.c("DV")), ("DS", kb.c("DS")), ("DT", kb.c("DT")),
                                  ("QS", kb.c("QS"))] if (tc >= 1 or half > 0) else None)
                kb.waitt("S", yx)
                kb.wait("S", "DV", prev_vxn)
                s_sums = kb.op("S", lambda s, xb=xb: s.activation(
                    xn_tok[:], xb[:], AF.Square, accum_out=sx[:, 1:2]), inc=True)
                kb.waitt("V", yx)
                kb.op("V", lambda v, xb=xb: v.reduce_sum(sx[:, 0:1], xb[:], axis=AX.X))
                kb.op("V", lambda v: v.drain())
                if stop_after == "ln_sums":
                    dbg_taps.append(cols)
                    break
                kb.wait("V", "DS", s_sums)
                kb.op("V", lambda v: v.tensor_scalar_mul(meanc[:], sx[:, 0:1], 1.0 / D))
                kb.op("V", lambda v: v.tensor_scalar_mul(varc[:], sx[:, 1:2], 1.0 / D))
                kb.op("V", lambda v: v.drain())
                kb.op("V", lambda v: v.tensor_tensor(st1[:, 0:1], meanc[:], meanc[:], ALU.mult))
                kb.op("V", lambda v: v.drain())
                v_var = kb.op("V", lambda v: v.tensor_tensor(
                    varc[:], varc[:], st1[:, 0:1], ALU.subtract), inc=True)
                kb.wait("S", "DV", v_var)
                kb.op("S", lambda s: s.activation(rstdc[:], varc[:], AF.Ln, bias=b_lneps))
                s_rstd = kb.op("S", lambda s: s.activation(
                    rstdc[:], rstdc[:], AF.Exp, scale=-0.5), inc=True)
                kb.wait("V", "DS", s_rstd)
                v_xn = kb.op("V", lambda v, xb=xb: v.tensor_scalar(
                    xn_tok[:], xb[:], meanc[:], rstdc[:], ALU.subtract, ALU.mult), inc=True)
                prev_vxn = v_xn
                if stop_after == "ln":
                    dbg_taps.extend([xn_tok, cols])
                    break
                for bq in range(4):
                    b, ps, w8 = pool.acquire()
                    kb.wait("T", "DG", g_consts)
                    kb.wait("T", "DV", v_xn)
                    for jj in range(4):
                        j = bq * 4 + jj
                        src_sl = xn_tok[:, j * 128:(j + 1) * 128]
                        kb.op("T", lambda t, ps=ps, jj=jj, src_sl=src_sl: t.matmul(
                            ps[:, jj * 128:(jj + 1) * 128], src_sl, ident_b[:],
                            start=True, stop=True), wait=w8 if jj == 0 else None)
                    t_tr = kb.op("T", lambda t: t.drain(), inc=True)
                    dst = xnT[:].rearrange("p (j t) -> p j t", j=NDCH)[
                        :, bq * 4:(bq + 1) * 4, tc * 128:(tc + 1) * 128]
                    src = ps[:].rearrange("p (j t) -> p j t", j=4)
                    s_ev = kb.op("S", lambda s, dst=dst, src=src: s.activation(
                        dst, src, AF.Copy), wait=[("DT", t_tr)], inc=True, bank=b)

            if stop_after in ("ln", "ln_sums"):
                break
            xn_done = kb.c("DS")
            if stop_after == "xn":
                dbg_taps.append(xnT)
                break

            y_wb = kb.dma("Y", wb_sb.rearrange("p (j c) -> p j c", j=16),
                          chunked_rows(scr["wb"]),
                          wait=[scrw["wb"], ("DT", kb.c("DT"))])
            y_gw = None
            for j in range(16):
                y_gw = kb.dma("Y", gw_sb[:, j * 16:(j + 1) * 16],
                              scr["gatew"][:, j * 128:(j + 1) * 128],
                              wait=[scrw["gatew"]] if j == 0 else None,
                              transpose=True)
            g_sgw = kb.dma("G", sgw_sb.rearrange("p (one j) -> p one j", one=1),
                           w_d["L0_ffn_sgate_w"].rearrange("one (j p) -> p one j", p=128),
                           allow_slow_non_contiguous=True,
                           wait=[("DT", kb.c("DT"))])
            prev_vct = 0
            for tc in range(NCH):
                b, ps, w8 = pool.acquire()
                kb.wait("T", "DS", xn_done)
                kb.waitt("T", y_wb)
                kb.waitt("T", y_gw)
                kb.waitt("T", g_sgw)
                for j in range(NDCH):
                    xsl = xnT[:, j * TH + tc * 128: j * TH + (tc + 1) * 128]
                    kb.op("T", lambda t, ps=ps, j=j, xsl=xsl: t.matmul(
                        ps[:, 0:16], xsl, wb_sb[:, j * 16:(j + 1) * 16],
                        start=(j == 0), stop=(j == NDCH - 1)), wait=w8 if j == 0 else None)
                for j in range(NDCH):
                    xsl = xnT[:, j * TH + tc * 128: j * TH + (tc + 1) * 128]
                    kb.op("T", lambda t, ps=ps, j=j, xsl=xsl: t.matmul(
                        ps[:, 16:32], xsl, gw_sb[:, j * 16:(j + 1) * 16],
                        start=(j == 0), stop=(j == NDCH - 1)))
                for j in range(NDCH):
                    xsl = xnT[:, j * TH + tc * 128: j * TH + (tc + 1) * 128]
                    kb.op("T", lambda t, ps=ps, j=j, xsl=xsl: t.matmul(
                        ps[:, 32:33], xsl, sgw_sb[:, j:j + 1],
                        start=(j == 0), stop=(j == NDCH - 1)))
                t_b = kb.op("T", lambda t: t.drain(), inc=True)
                kb.wait("S", "DT", t_b)
                kb.wait("S", "DV", prev_vct)
                s_eb = kb.op("S", lambda s, ps=ps: s.activation(
                    rw_t[:], ps[:, 0:16], AF.Exp, scale=-1.0), inc=True, bank=b)
                kb.wait("V", "DS", s_eb)
                kb.op("V", lambda v: v.tensor_scalar_add(rw_t[:], rw_t[:], 1.0))
                kb.op("V", lambda v: v.drain())
                dstb = beta_all[:, tc * 16:(tc + 1) * 16]
                kb.op("V", lambda v, dstb=dstb: v.reciprocal(dstb, rw_t[:]))
                s_er = kb.op("S", lambda s, ps=ps: s.activation(
                    cf_t[:, 0:16], ps[:, 16:32], AF.Exp, accum_out=thr_t[:]), inc=True, bank=b)
                kb.wait("V", "DS", s_er)
                kb.op("V", lambda v: v.reciprocal(smr[:], thr_t[:]))
                kb.op("V", lambda v: v.drain())
                kb.op("V", lambda v: v.tensor_scalar_mul(rw_t[:], cf_t[:, 0:16], smr[:]))
                kb.op("V", lambda v: v.drain())
                kb.op("V", lambda v: v.max(out=m8_t[:], in_=rw_t[:]))
                kb.op("V", lambda v: v.drain())
                kb.op("V", lambda v: v.tensor_single_scalar(thr_t[:], m8_t[:, 3:4], 1.0, ALU.mult))
                kb.op("V", lambda v: v.drain())
                kb.op("V", lambda v: v.tensor_single_scalar(cf_t[:, 0:16], rw_t[:], thr_t[:], ALU.is_ge))
                kb.op("V", lambda v: v.drain())
                kb.op("V", lambda v: v.tensor_tensor(cf_t[:, 0:16], cf_t[:, 0:16], rw_t[:], ALU.mult))
                s_eg = kb.op("S", lambda s, ps=ps: s.activation(
                    st1[:, 0:1], ps[:, 32:33], AF.Exp, scale=-1.0), inc=True, bank=b)
                kb.wait("V", "DS", s_eg)
                kb.op("V", lambda v: v.tensor_scalar_add(st1[:, 0:1], st1[:, 0:1], 1.0))
                kb.op("V", lambda v: v.drain())
                v_sgv = kb.op("V", lambda v: v.reciprocal(cf_t[:, 16:17], st1[:, 0:1]), inc=True)
                b2, ps2, w82 = pool.acquire()
                kb.wait("T", "DV", v_sgv)
                kb.op("T", lambda t, ps2=ps2: t.matmul(
                    ps2[0:17, 0:128], cf_t[:], ident_f[:], start=True, stop=True), wait=w82)
                t_ct = kb.op("T", lambda t: t.drain(), inc=True)
                dstc = coefT[0:17, tc * 128:(tc + 1) * 128]
                v_ct = kb.op("V", lambda v, dstc=dstc, ps2=ps2: v.tensor_copy(
                    dstc, ps2[0:17, 0:128]), wait=[("DT", t_ct)], inc=True, bank=b2)
                prev_vct = v_ct

            for (stag, gdst) in (("wf1", gpre1), ("wg1", gpre2)):
                y_g1 = kb.dma("Y", wfg1[:].rearrange("p (j c) -> p j c", j=16),
                              chunked_rows(scr[stag]),
                              wait=[scrw[stag], ("DT", kb.c("DT"))])
                for tt in range(2):
                    b, ps, w8 = pool.acquire()
                    kb.waitt("T", y_g1)
                    for j in range(NDCH):
                        xsl = xnT[:, j * TH + tt * 512: j * TH + (tt + 1) * 512]
                        kb.op("T", lambda t, ps=ps, j=j, xsl=xsl: t.matmul(
                            ps[:], wfg1[:, j * 128:(j + 1) * 128], xsl,
                            start=(j == 0), stop=(j == NDCH - 1)), wait=w8 if j == 0 else None)
                    t_g = kb.op("T", lambda t: t.drain(), inc=True)
                    dstsl = gdst[:, tt * 512:(tt + 1) * 512]
                    s_ev = kb.op("S", lambda s, dstsl=dstsl, ps=ps: s.activation(
                        dstsl, ps[:], AF.Copy), wait=[("DT", t_g)], inc=True, bank=b)

            phaseB_V = kb.c("DV")
            phaseB_S = kb.c("DS")
            if stop_after == "phaseB":
                dbg_taps.extend([beta_all, coefT[0:17, :], gpre1, gpre2])
                break

            for grp in range(NGRP):
                wsel = {0: "wq", 1: "wk", 2: "wv", 3: "ww"}
                dsel = {0: qf, 1: kf, 2: vf, 3: wff}
                for pi in range(4):
                    for hin in range(NG):
                        head = grp * NG + hin
                        bi = (pi * NG + hin) % 2
                        wbuf = wp[bi]
                        stag = wsel[pi]
                        src = scr[stag].rearrange("(j p) c -> p j c", p=128)[
                            :, :, head * 128:(head + 1) * 128]
                        yw = kb.dma("Y", wbuf[:], src,
                                    wait=[scrw[stag], ("DT", wp_free[bi])])
                        tsl = tails[:, (pi * 16 + head) * 3:(pi * 16 + head) * 3 + 3]
                        cbase = {0: 0, 1: 64, 2: 128, 3: 128}[pi]
                        for tt in range(2):
                            b, ps, w8 = pool.acquire()
                            kb.waitt("T", yw)
                            for j in range(NDCH):
                                xsl = xnT[:, j * TH + tt * 512: j * TH + (tt + 1) * 512]
                                kb.op("T", lambda t, ps=ps, wbuf=wbuf, j=j, xsl=xsl: t.matmul(
                                    ps[:], wbuf[:, j * 128:(j + 1) * 128], xsl,
                                    start=(j == 0), stop=(j == NDCH - 1)),
                                    wait=w8 if j == 0 else None)
                            t_p = kb.op("T", lambda t: t.drain(), inc=True)
                            if tt == 1:
                                wp_free[bi] = t_p
                            v_ev = kb.op("V", lambda v, ps=ps: v.tensor_copy(
                                raw[:, 16:528], ps[:]), wait=[("DT", t_p)], inc=True, bank=b)
                            if tt == 0:
                                kb.op("V", lambda v, tsl=tsl: v.tensor_copy(raw[:, 13:16], tsl))
                                kb.op("V", lambda v: v.tensor_copy(tail3[:, 0:3], raw[:, 525:528]))
                            else:
                                kb.op("V", lambda v: v.tensor_copy(raw[:, 13:16], tail3[:, 0:3]))
                                kb.op("V", lambda v, tsl=tsl: v.tensor_copy(tsl, raw[:, 525:528]))
                            tap0 = ctaps[:, cbase + head * 4: cbase + head * 4 + 1]
                            kb.op("V", lambda v, tap0=tap0: v.tensor_scalar_mul(
                                conv_acc[:], raw[:, 13:525], tap0))
                            for k in range(1, 4):
                                tapk = ctaps[:, cbase + head * 4 + k: cbase + head * 4 + k + 1]
                                rsl = raw[:, 13 + k:525 + k]
                                kb.op("V", lambda v, tapk=tapk, rsl=rsl: v.scalar_tensor_tensor(
                                    conv_acc[:], rsl, tapk, conv_acc[:], ALU.mult, ALU.add))
                            v_cv = kb.op("V", lambda v: v.engine_nop(), inc=True)
                            dproj = dsel[pi][:, hin * TH + tt * 512: hin * TH + (tt + 1) * 512]
                            s_si = kb.op("S", lambda s, dproj=dproj: s.activation(
                                dproj, conv_acc[:], AF.Silu), wait=[("DV", v_cv)], inc=True)
                            kb.wait("V", "DS", s_si)

                c1_done_s = kb.c("DS")
                if stop_after == "proj" and grp == 0:
                    dbg_taps.extend([qf, kf, vf, wff])
                    break

                y_f2 = kb.dma("Y", wf2g[:], scr["wf2"][:, grp * 512:(grp + 1) * 512],
                              wait=[scrw["wf2"], ("DT", kb.c("DT"))])
                y_g2 = kb.dma("Y", wg2g[:], scr["wg2"][:, grp * 512:(grp + 1) * 512],
                              wait=[scrw["wg2"], ("DT", kb.c("DT"))])
                kb.waitt("T", y_f2)
                kb.waitt("T", y_g2)
                kb.wait("T", "DS", c1_done_s)
                for cc in range(NCH):
                    tc0 = cc * 128
                    kb.wait("S", "DV", kb.c("DV"))
                    gp_sl = gpre1[:, tc0:tc0 + 128]
                    wf2_sl = wf2g[:]
                    b_p, ps_p, w8 = pool.acquire()
                    kb.op("T", lambda t, ps_p=ps_p, gp_sl=gp_sl, wf2_sl=wf2_sl: t.matmul(
                        ps_p[:], gp_sl, wf2_sl, start=True, stop=True), wait=w8)
                    t_pp = kb.op("T", lambda t: t.drain(), inc=True)
                    kb.wait("S", "DT", t_pp)
                    s_pe = kb.op("S", lambda s, ps_p=ps_p: s.activation(
                        p_sb[:], ps_p[:], AF.Exp, scale=-1.0), inc=True, bank=b_p)
                    s_p = kb.op("S", lambda s: s.activation(
                        p_sb[:], p_sb[:], AF.Ln, bias=1.0), inc=True)
                    b_ch, ps_ch, w1 = pool.acquire()
                    b_cf, ps_cf, w2_ = pool.acquire()
                    b_rc, ps_rc, w3 = pool.acquire()
                    b_sm, ps_sm, w4 = pool.acquire()
                    b_lc, ps_lc, w5 = pool.acquire()
                    kb.wait("T", "DS", s_p)
                    kb.op("T", lambda t, ps_ch=ps_ch: t.matmul(
                        ps_ch[:], Lhalf[:], p_sb[:], start=True, stop=True), wait=w1)
                    kb.op("T", lambda t, ps_cf=ps_cf: t.matmul(
                        ps_cf[:], Lfull[:], p_sb[:], start=True, stop=True), wait=w2_)
                    kb.op("T", lambda t, ps_rc=ps_rc: t.matmul(
                        ps_rc[:], Ustrict[:], p_sb[:], start=True, stop=True), wait=w3)
                    kb.op("T", lambda t, ps_sm=ps_sm: t.matmul(
                        ps_sm[0:1, :], ones_cf[:], p_sb[:], start=True, stop=True), wait=w4)
                    for hin in range(NG):
                        psl = p_sb[:, hin * 128:(hin + 1) * 128]
                        kb.op("T", lambda t, ps_lc=ps_lc, hin=hin, psl=psl: t.matmul(
                            ps_lc[:, hin:hin + 1], psl, ones_cf[:],
                            start=True, stop=True), wait=w5 if hin == 0 else None)
                    t_cum = kb.op("T", lambda t: t.drain(), inc=True)
                    kb.wait("S", "DT", t_cum)
                    kb.op("S", lambda s, ps_ch=ps_ch: s.activation(lam_c[:], ps_ch[:], AF.Exp, scale=-1.0), inc=True, bank=b_ch)
                    s_ch = kb.op("S", lambda s, ps_ch=ps_ch: s.activation(eu[:], ps_ch[:], AF.Exp, scale=1.0), inc=True, bank=b_ch)
                    s_cf = kb.op("S", lambda s, ps_cf=ps_cf: s.activation(lam_f[:], ps_cf[:], AF.Exp, scale=-1.0), inc=True, bank=b_cf)
                    s_rc = kb.op("S", lambda s, ps_rc=ps_rc: s.activation(rfac[:], ps_rc[:], AF.Exp, scale=-1.0), inc=True, bank=b_rc)
                    s_sm = kb.op("S", lambda s, ps_sm=ps_sm: s.activation(
                        elam_row[:], ps_sm[0:1, 0:512], AF.Exp, scale=-1.0), inc=True, bank=b_sm)
                    ec_out = elam_col[:].rearrange("p (h s) -> p h s", s=4)[:, :, 0:1]
                    s_cum = kb.op("S", lambda s, ps_lc=ps_lc, ec_out=ec_out: s.activation(
                        ec_out, ps_lc[:, 0:4].rearrange("p (h s) -> p h s", s=1),
                        AF.Exp, scale=-1.0), inc=True, bank=b_lc)

                    sk_g = Sk[:, grp * 512:(grp + 1) * 512]
                    sv_g = Sv[:, grp * 512:(grp + 1) * 512]
                    b_tk, ps_tk, w1 = pool.acquire()
                    b_tv, ps_tv, w2_ = pool.acquire()
                    b_tw, ps_tw, w3 = pool.acquire()
                    for hin in range(NG):
                        ksl = kf[:, hin * TH + tc0: hin * TH + tc0 + 128]
                        vsl = vf[:, hin * TH + tc0: hin * TH + tc0 + 128]
                        wsl = wff[:, hin * TH + tc0: hin * TH + tc0 + 128]
                        kb.op("T", lambda t, ps_tk=ps_tk, hin=hin, ksl=ksl: t.matmul(
                            ps_tk[:, hin * 128:(hin + 1) * 128], ksl, ident_b[:],
                            start=True, stop=True), wait=w1 if hin == 0 else None)
                        kb.op("T", lambda t, ps_tv=ps_tv, hin=hin, vsl=vsl: t.matmul(
                            ps_tv[:, hin * 128:(hin + 1) * 128], vsl, ident_b[:],
                            start=True, stop=True), wait=w2_ if hin == 0 else None)
                        kb.op("T", lambda t, ps_tw=ps_tw, hin=hin, wsl=wsl: t.matmul(
                            ps_tw[:, hin * 128:(hin + 1) * 128], wsl, ident_b[:],
                            start=True, stop=True), wait=w3 if hin == 0 else None)
                    t_tr = kb.op("T", lambda t: t.drain(), inc=True)
                    kb.wait("V", "DT", t_tr)
                    v_k = kb.op("V", lambda v, ps_tk=ps_tk: v.tensor_copy(ktok4[:], ps_tk[:]), inc=True, bank=b_tk)
                    v_v = kb.op("V", lambda v, ps_tv=ps_tv: v.tensor_copy(vtok4[:], ps_tv[:]), inc=True, bank=b_tv)
                    v_tr = kb.op("V", lambda v, ps_tw=ps_tw: v.tensor_copy(wtok4[:], ps_tw[:]), inc=True, bank=b_tw)
                    kb.wait("S", "DV", v_tr)
                    for hin in range(NG):
                        wdst = o24[:, hin * 128:(hin + 1) * 128]
                        wsrc = wtok4[:, hin * 128:(hin + 1) * 128]
                        acc_sl = wsum4[:, hin * 4: hin * 4 + 1]
                        kb.op("S", lambda sc, wdst=wdst, wsrc=wsrc, acc_sl=acc_sl: sc.activation(
                            wdst, wsrc, AF.Square, accum_out=acc_sl))
                    kb.op("S", lambda sc: sc.activation(wrs4[:], wsum4[:], AF.Ln, bias=b_l2eps))
                    s_wn = kb.op("S", lambda sc: sc.activation(wrs4[:], wrs4[:], AF.Exp, scale=-0.5), inc=True)
                    kb.wait("V", "DS", max(s_wn, s_cum))
                    wrs_bc = wrs4.rearrange("p (h s) -> p h s", s=4)[:, :, 0:1].broadcast_to([128, NG, 128])
                    kb.op("V", lambda v, wrs_bc=wrs_bc: v.tensor_tensor(
                        wn4[:].rearrange("p (h m) -> p h m", h=NG), 
                        wtok4[:].rearrange("p (h m) -> p h m", h=NG), wrs_bc, ALU.mult))
                    bsl = beta_all[:, cc * 16 + grp * 4: cc * 16 + grp * 4 + 4]
                    beta_bc = bsl.rearrange("p (h s) -> p h s", s=1).broadcast_to([128, NG, 128])
                    kb.op("V", lambda v, beta_bc=beta_bc: v.tensor_tensor(
                        wn4[:].rearrange("p (h m) -> p h m", h=NG),
                        wn4[:].rearrange("p (h m) -> p h m", h=NG), beta_bc, ALU.mult))
                    kb.op("V", lambda v: v.tensor_tensor(u4[:], wn4[:], eu[:], ALU.mult))
                    v_ur = kb.op("V", lambda v: v.tensor_tensor(r4[:], wn4[:], rfac[:], ALU.mult), inc=True)
                    b_a, ps_a, w8 = pool.acquire()
                    for hin in range(NG):
                        ksl = kf[:, hin * TH + tc0: hin * TH + tc0 + 128]
                        qsl = qf[:, hin * TH + tc0: hin * TH + tc0 + 128]
                        kb.op("T", lambda t, ps_a=ps_a, hin=hin, ksl=ksl, qsl=qsl: t.matmul(
                            ps_a[:, hin * 128:(hin + 1) * 128], ksl, qsl,
                            start=True, stop=True), wait=w8 if hin == 0 else None)
                    t_a = kb.op("T", lambda t: t.drain(), inc=True)
                    kb.wait("V", "DT", t_a)
                    v_am = kb.op("V", lambda v, ps_a=ps_a: v.tensor_tensor(
                        am4[:].rearrange("p (h t) -> p h t", h=NG),
                        ps_a[:].rearrange("p (h t) -> p h t", h=NG),
                        mask01[:].unsqueeze(1).broadcast_to([128, NG, 128]), ALU.mult), inc=True, bank=b_a)
                    b_i, ps_i, w8i = pool.acquire()
                    b_j, ps_j, w8j = pool.acquire()
                    kb.wait("T", "DV", max(v_am, v_ur))
                    for hin in range(NG):
                        hs_ = slice(hin * 128, (hin + 1) * 128)
                        qsl = qf[:, hin * TH + tc0: hin * TH + tc0 + 128]
                        sk_sl = Sk[:, (grp * NG + hin) * 128:(grp * NG + hin) * 128 + 128]
                        kb.op("T", lambda t, ps_i=ps_i, hs_=hs_: t.matmul(
                            ps_i[:, hs_], am4[:, hs_], u4[:, hs_],
                            start=True, stop=True), wait=w8i if hin == 0 else None)
                        kb.op("T", lambda t, ps_j=ps_j, hs_=hs_, qsl=qsl, sk_sl=sk_sl: t.matmul(
                            ps_j[:, hs_], qsl, sk_sl,
                            start=True, stop=True), wait=w8j if hin == 0 else None)
                    t_sc = kb.op("T", lambda t: t.drain(), inc=True)
                    kb.wait("V", "DT", t_sc)
                    v_j = kb.op("V", lambda v, ps_j=ps_j: v.tensor_tensor(st14[:], lam_f[:], ps_j[:], ALU.mult), inc=True, bank=b_j)
                    v_i = kb.op("V", lambda v, ps_i=ps_i: v.scalar_tensor_tensor(
                        st24[:], ps_i[:], SCALE, lam_c[:], ALU.mult, ALU.mult), inc=True, bank=b_i)
                    v_sc = kb.op("V", lambda v: v.scalar_tensor_tensor(
                        st24[:], st14[:], SCALE, st24[:], ALU.mult, ALU.add), inc=True)
                    st3d = st24[:].rearrange("p (h m) -> p h m", h=NG)
                    nmx_sl = nmx4.rearrange("p (h s) -> p h s", s=4)[:, :, 0:1]
                    kb.op("V", lambda v, st3d=st3d, nmx_sl=nmx_sl: v.reduce_max(
                        nmx_sl, st3d, axis=AX.X, negate=True))
                    kb.op("V", lambda v: v.drain())
                    nmx_bc = nmx4.rearrange("p (h s) -> p h s", s=4)[:, :, 0:1].broadcast_to([128, NG, 128])
                    v_nm = kb.op("V", lambda v, st3d=st3d, nmx_bc=nmx_bc: v.tensor_tensor(
                        st3d, st3d, nmx_bc, ALU.add), inc=True)
                    kb.wait("S", "DV", v_nm)
                    s_es = kb.op("S", lambda sc: sc.activation(st24[:], st24[:], AF.Exp), inc=True)
                    kb.wait("V", "DS", s_es)
                    smr_sl = smr4.rearrange("p (h s) -> p h s", s=4)[:, :, 0:1]
                    kb.op("V", lambda v, st3d=st3d, smr_sl=smr_sl: v.reduce_sum(smr_sl, st3d, axis=AX.X))
                    kb.op("V", lambda v: v.drain())
                    kb.op("V", lambda v: v.reciprocal(smr4[:], smr4[:]))
                    kb.op("V", lambda v: v.drain())
                    smr_bc = smr4.rearrange("p (h s) -> p h s", s=4)[:, :, 0:1].broadcast_to([128, NG, 128])
                    kb.op("V", lambda v, st3d=st3d, smr_bc=smr_bc: v.tensor_tensor(
                        st14[:].rearrange("p (h m) -> p h m", h=NG), st3d, smr_bc, ALU.mult))
                    kb.op("V", lambda v: v.tensor_tensor(ac4[:], st14[:], lam_c[:], ALU.mult))
                    v_a2 = kb.op("V", lambda v: v.tensor_tensor(af4[:], st14[:], lam_f[:], ALU.mult), inc=True)
                    b_t2, ps_t2, w8 = pool.acquire()
                    b_t3, ps_t3, w83 = pool.acquire()
                    b_t4, ps_t4, w84 = pool.acquire()
                    kb.wait("T", "DV", v_a2)
                    for hin in range(NG):
                        hs_ = slice(hin * 128, (hin + 1) * 128)
                        kb.op("T", lambda t, ps_t2=ps_t2, hs_=hs_: t.matmul(
                            ps_t2[:, hs_], ac4[:, hs_], ident_b[:], start=True, stop=True),
                            wait=w8 if hin == 0 else None)
                        kb.op("T", lambda t, ps_t3=ps_t3, hs_=hs_: t.matmul(
                            ps_t3[:, hs_], af4[:, hs_], ident_b[:], start=True, stop=True),
                            wait=w83 if hin == 0 else None)
                        kb.op("T", lambda t, ps_t4=ps_t4, hs_=hs_: t.matmul(
                            ps_t4[:, hs_], u4[:, hs_], ident_b[:], start=True, stop=True),
                            wait=w84 if hin == 0 else None)
                    t_t2 = kb.op("T", lambda t: t.drain(), inc=True)
                    kb.wait("V", "DT", t_t2)
                    v_c1 = kb.op("V", lambda v, ps_t2=ps_t2: v.tensor_copy(acT4[:], ps_t2[:]), inc=True, bank=b_t2)
                    v_c2 = kb.op("V", lambda v, ps_t3=ps_t3: v.tensor_copy(afT4[:], ps_t3[:]), inc=True, bank=b_t3)
                    v_t2 = kb.op("V", lambda v, ps_t4=ps_t4: v.tensor_copy(uT4[:], ps_t4[:]), inc=True, bank=b_t4)
                    b_pt, ps_pt, w8 = pool.acquire()
                    kb.wait("T", "DV", v_t2)
                    for hin in range(NG):
                        hs_ = slice(hin * 128, (hin + 1) * 128)
                        kb.op("T", lambda t, ps_pt=ps_pt, hs_=hs_: t.matmul(
                            ps_pt[:, hs_], uT4[:, hs_], acT4[:, hs_], start=True, stop=True),
                            wait=w8 if hin == 0 else None)
                    t_pt = kb.op("T", lambda t: t.drain(), inc=True)
                    kb.wait("V", "DT", t_pt)
                    v_pm = kb.op("V", lambda v, ps_pt=ps_pt: v.tensor_tensor(
                        pm4[:].rearrange("p (h t) -> p h t", h=NG),
                        ps_pt[:].rearrange("p (h t) -> p h t", h=NG),
                        mask01[:].unsqueeze(1).broadcast_to([128, NG, 128]), ALU.mult), inc=True, bank=b_pt)
                    b_o, ps_o, w8 = pool.acquire()
                    kb.wait("T", "DV", v_pm)
                    for hin in range(NG):
                        hs_ = slice(hin * 128, (hin + 1) * 128)
                        sv_sl = Sv[:, (grp * NG + hin) * 128:(grp * NG + hin) * 128 + 128]
                        kb.op("T", lambda t, ps_o=ps_o, hs_=hs_: t.matmul(
                            ps_o[:, hs_], vtok4[:, hs_], pm4[:, hs_], start=True, stop=False),
                            wait=w8 if hin == 0 else None)
                        kb.op("T", lambda t, ps_o=ps_o, hs_=hs_, sv_sl=sv_sl: t.matmul(
                            ps_o[:, hs_], sv_sl, afT4[:, hs_], start=False, stop=True))
                    t_o = kb.op("T", lambda t: t.drain(), inc=True)
                    b_su, ps_su, w8 = pool.acquire()
                    b_sb_, ps_sb_, w8b = pool.acquire()
                    b_sv, ps_sv, w8v = pool.acquire()
                    for hin in range(NG):
                        hs_ = slice(hin * 128, (hin + 1) * 128)
                        rowsl = elam_row[0:1, hin * 128:(hin + 1) * 128]
                        kb.op("T", lambda t, ps_su=ps_su, hs_=hs_: t.matmul(
                            ps_su[:, hs_], ktok4[:, hs_], r4[:, hs_], start=True, stop=True),
                            wait=w8 if hin == 0 else None)
                        kb.op("T", lambda t, ps_sb_=ps_sb_, hs_=hs_, rowsl=rowsl: t.matmul(
                            ps_sb_[:, hs_], ones_1b[:], rowsl, start=True, stop=True),
                            wait=w8b if hin == 0 else None)
                        kb.op("T", lambda t, ps_sv=ps_sv, hs_=hs_: t.matmul(
                            ps_sv[:, hs_], r4[:, hs_], vtok4[:, hs_], start=True, stop=True),
                            wait=w8v if hin == 0 else None)
                    t_su = kb.op("T", lambda t: t.drain(), inc=True)
                    kb.wait("V", "DT", t_su)
                    v_s1 = kb.op("V", lambda v, ps_sb_=ps_sb_, sk_g=sk_g: v.tensor_tensor(sktmp4[:], sk_g, ps_sb_[:], ALU.mult), inc=True, bank=b_sb_)
                    v_s2 = kb.op("V", lambda v, ps_su=ps_su, sk_g=sk_g: v.tensor_tensor(sk_g, sktmp4[:], ps_su[:], ALU.add), inc=True, bank=b_su)
                    elam_bc = elam_col[:].rearrange("p (h s) -> p h s", s=4)[:, :, 0:1].broadcast_to([128, NG, 128])
                    kb.op("V", lambda v, elam_bc=elam_bc, sv_g=sv_g: v.tensor_tensor(
                        sktmp4[:].rearrange("p (h m) -> p h m", h=NG),
                        sv_g.rearrange("p (h m) -> p h m", h=NG), elam_bc, ALU.mult))
                    v_su = kb.op("V", lambda v, ps_sv=ps_sv, sv_g=sv_g: v.tensor_tensor(
                        sv_g, sktmp4[:], ps_sv[:], ALU.add), inc=True, bank=b_sv)
                    kb.wait("V", "DT", t_o)
                    v_ocp = kb.op("V", lambda v, ps_o=ps_o: v.tensor_copy(o4[:], ps_o[:]), inc=True, bank=b_o)
                    kb.wait("S", "DV", v_ocp)
                    s_o2 = kb.op("S", lambda sc: sc.activation(o24[:], o4[:], AF.Square), inc=True)
                    b_r, ps_r, w8 = pool.acquire()
                    b_g, ps_g, w8g = pool.acquire()
                    kb.wait("T", "DS", s_o2)
                    kb.op("T", lambda t, ps_r=ps_r: t.matmul(
                        ps_r[0:1, :], ones_cb[:], o24[:], start=True, stop=True), wait=w8)
                    gsl = gpre2[:, tc0:tc0 + 128]
                    for hin in range(NG):
                        hs_ = slice(hin * 128, (hin + 1) * 128)
                        kb.op("T", lambda t, ps_g=ps_g, hs_=hs_, gsl=gsl: t.matmul(
                            ps_g[:, hs_], wg2g[:, hs_], gsl, start=True, stop=True),
                            wait=w8g if hin == 0 else None)
                    t_r = kb.op("T", lambda t: t.drain(), inc=True)
                    kb.wait("S", "DT", t_r)
                    s_ln = kb.op("S", lambda sc, ps_r=ps_r: sc.activation(
                        lrow, ps_r[0:1, :], AF.Ln, bias=b_rmse[0:1, :]), inc=True, bank=b_r)
                    kb.op("S", lambda sc: sc.activation(
                        rrow, lrow, AF.Exp, scale=-0.5, bias=b_hl128[0:1, :]))
                    s_eg2 = kb.op("S", lambda sc, ps_g=ps_g: sc.activation(
                        eg4[:], ps_g[:], AF.Exp, scale=-1.0), inc=True, bank=b_g)
                    b_rb, ps_rb, w8 = pool.acquire()
                    kb.wait("T", "DS", s_eg2)
                    kb.op("T", lambda t, ps_rb=ps_rb: t.matmul(
                        ps_rb[:], ones_1b[:], rrow, start=True, stop=True), wait=w8)
                    t_rb = kb.op("T", lambda t: t.drain(), inc=True)
                    kb.wait("V", "DS", s_eg2)
                    ebg_sl = ebg[:, grp * 4: grp * 4 + 4]
                    ebg_b = ebg_sl.rearrange("p (h s) -> p h s", s=1).broadcast_to([128, NG, 128])
                    kb.op("V", lambda v, ebg_b=ebg_b: v.tensor_tensor(
                        eg4[:].rearrange("p (h t) -> p h t", h=NG),
                        eg4[:].rearrange("p (h t) -> p h t", h=NG), ebg_b, ALU.mult))
                    kb.op("V", lambda v: v.tensor_scalar_add(st24[:], eg4[:], 1.0))
                    kb.op("V", lambda v: v.drain())
                    kb.op("V", lambda v: v.reciprocal(st24[:], st24[:]))
                    kb.wait("V", "DT", t_rb)
                    v_rb = kb.op("V", lambda v, ps_rb=ps_rb: v.tensor_tensor(o4[:], o4[:], ps_rb[:], ALU.mult), inc=True, bank=b_rb)
                    dsto = o_grp[:].rearrange("p (h t) -> p h t", h=NG)[:, :, tc0:tc0 + 128]
                    v_of = kb.op("V", lambda v, dsto=dsto: v.scalar_tensor_tensor(
                        dsto, o4[:].rearrange("p (h t) -> p h t", h=NG), normw[:],
                        st24[:].rearrange("p (h t) -> p h t", h=NG), ALU.mult, ALU.mult), inc=True)

                if stop_after == "scan" and grp == 0:
                    dbg_taps.extend([o_grp, Sk, Sv])
                    break

                scan_done_v = kb.c("DV")
                for dj in range(NDCH):
                    bi = dj % 2
                    wbuf = wp[bi]
                    src = scr["wo"].rearrange("(j p) c -> p j c", p=128)[
                        :, grp * NG:(grp + 1) * NG, dj * 128:(dj + 1) * 128]
                    yw = kb.dma("Y", wbuf[:, 0:512], src,
                                wait=[scrw["wo"], ("DT", wp_free[bi])])
                    for tt in range(2):
                        b, ps, w8 = pool.acquire()
                        kb.waitt("T", yw)
                        kb.wait("T", "DV", scan_done_v)
                        for hin in range(NG):
                            osl = o_grp[:, hin * TH + tt * 512: hin * TH + (tt + 1) * 512]
                            kb.op("T", lambda t, ps=ps, wbuf=wbuf, hin=hin, osl=osl: t.matmul(
                                ps[:], wbuf[:, hin * 128:(hin + 1) * 128], osl,
                                start=(hin == 0), stop=(hin == NG - 1)),
                                wait=w8 if hin == 0 else None)
                        t_wo = kb.op("T", lambda t: t.drain(), inc=True)
                        if tt == 1:
                            wp_free[bi] = t_wo
                        asl = acc[:, dj * TH + tt * 512: dj * TH + (tt + 1) * 512]
                        kb.wait("V", "DT", t_wo)
                        if grp == 0:
                            v_ev = kb.op("V", lambda v, asl=asl, ps=ps: v.tensor_copy(asl, ps[:]), inc=True, bank=b)
                        else:
                            v_ev = kb.op("V", lambda v, asl=asl, ps=ps: v.tensor_tensor(
                                asl, asl, ps[:], ALU.add), inc=True, bank=b)

            if stop_after in ("proj", "scan"):
                break
            if stop_after == "wo":
                dbg_taps.extend([acc, Sk, Sv, o_grp])
                break

            last_vhs = 0
            ffn_t0 = kb.c("DT")
            nat_last_tr = [ffn_t0, ffn_t0]
            natb = [big[:, 13312:16128], big[:, 16128:18944]]

            units = []
            groups = []
            for ex in range(NEXP):
                if ex < E_MOE:
                    w13src, w2src = scr["w13"][ex], scr["w2"][ex]
                    gup_off, up_off = 0, I_MOE
                    dn_col0 = 0
                    wtag13, wtag2 = f"w13_{ex}", f"w2_{ex}"
                else:
                    qq = ex - E_MOE
                    w13src, w2src = scr["sgu"], scr["sdw"]
                    gup_off, up_off = qq * I_MOE, I_SH + qq * I_MOE
                    dn_col0 = qq * I_MOE
                    wtag13, wtag2 = "sgu", "sdw"
                for sup in range(6):
                    ic0 = sup * 2
                    nch_s = 2 if sup < 5 else 1
                    for gu in range(2):
                        roff = (gup_off if gu == 0 else up_off) + ic0 * 128
                        u0 = len(units)
                        sbi = len(groups) % 2
                        if nch_s == 2:
                            for h in range(2):
                                banks = []
                                for a in range(4):
                                    pieces = [2 * a * 128, 1024 + 2 * a * 128,
                                              (2 * a + 1) * 128,
                                              1024 + (2 * a + 1) * 128]
                                    banks.append((512, (h * 8 + 2 * a) * 256, pieces))
                                units.append(dict(
                                    src=w13src[roff: roff + 256,
                                               h * 1024:(h + 1) * 1024]
                                    .rearrange("(q p) d -> p q d", p=128),
                                    ncols=2048, qw=1024, tag=wtag13,
                                    banks=banks, slab_t=upslab, sbi=sbi))
                        else:
                            banks = [(512, a * 512,
                                      [(4 * a + m) * 128 for m in range(4)])
                                     for a in range(4)]
                            units.append(dict(
                                src=w13src[roff: roff + 128, :],
                                ncols=2048, qw=0, tag=wtag13,
                                banks=banks, slab_t=upslab, sbi=sbi))
                        groups.append(dict(kind="up", ex=ex, sup=sup, gu=gu,
                                           ic0=ic0, nch_s=nch_s,
                                           u0=u0, u1=len(units)))
                for hd in range(8):
                    u0 = len(units)
                    sbi = len(groups) % 2
                    banks = []
                    for icc in range(5):
                        pieces = [2 * icc * 128, 1408 + 2 * icc * 128,
                                  (2 * icc + 1) * 128, 1408 + (2 * icc + 1) * 128]
                        banks.append((512, icc * 512, pieces))
                    banks.append((256, 2560, [10 * 128, 1408 + 10 * 128]))
                    units.append(dict(
                        src=w2src[hd * 256: hd * 256 + 256,
                                  dn_col0: dn_col0 + 1408]
                        .rearrange("(q p) c -> p q c", p=128),
                        ncols=2816, qw=1408, tag=wtag2,
                        banks=banks, slab_t=dnslab, sbi=sbi))
                    groups.append(dict(kind="dn", ex=ex, hd0=hd * 2,
                                       u0=u0, u1=len(units)))
            for i, u in enumerate(units):
                u["idx"] = i

            def emit_unit_dma(u):
                np_ = u["idx"] % 2
                dst = natb[np_][:, 0:u["ncols"]]
                if u["qw"]:
                    dst = dst.rearrange("p (q d) -> p q d", d=u["qw"])
                u["dma_tok"] = kb.dma1("Y", dst, u["src"],
                                       wait=[scrw[u["tag"]],
                                             ("DT", nat_last_tr[np_])])

            def emit_unit_tr(u):
                np_ = u["idx"] % 2
                slabv = u["slab_t"][u["sbi"]]
                binfo = []
                for bidx, (w, soff, pieces) in enumerate(u["banks"]):
                    b, ps, w8 = pool.acquire()
                    for k, noff in enumerate(pieces):
                        nsl = natb[np_][:, noff: noff + 128]
                        wl = list(w8) if k == 0 else []
                        if bidx == 0 and k == 0:
                            wl = wl + [u["dma_tok"]]
                        kb.op("T", lambda t, ps=ps, k=k, nsl=nsl: t.matmul(
                            ps[:, k * 128:(k + 1) * 128], nsl, ident_b[:],
                            start=True, stop=True), wait=wl or None)
                    binfo.append((b, ps, w, soff))
                trd = kb.op("T", lambda t: t.drain(), inc=True)
                nat_last_tr[np_] = trd
                ev = 0
                for (b, ps, w, soff) in binfo:
                    dstv = slabv[:, soff: soff + w]
                    ev = kb.op("S", lambda s, dstv=dstv, ps=ps, w=w: s.activation(
                        dstv, ps[:, 0:w], AF.Copy),
                        wait=[("DT", trd)], inc=True, bank=b)
                u["ev"] = ev

            ud = ut = 0
            cb_tr_prev = 0
            v_cb = 0
            t_cb = 0
            silu_ds = {}
            up_done_v = 0
            cur_ex = -1
            ng = len(groups)
            nu = len(units)
            for gi, g in enumerate(groups):
                tgt_t = groups[min(gi + 1, ng - 1)]["u1"]
                while ut < tgt_t:
                    while ud < min(ut + 2, nu):
                        emit_unit_dma(units[ud])
                        ud += 1
                    emit_unit_tr(units[ut])
                    ut += 1
                tgt_d = min(groups[min(gi + 2, ng - 1)]["u1"], ut + 2, nu)
                while ud < tgt_d:
                    emit_unit_dma(units[ud])
                    ud += 1
                if g["ex"] != cur_ex:
                    cur_ex = g["ex"]
                    cf_row = cur_ex if cur_ex < E_MOE else 16
                    if cur_ex == 0:
                        wl = [("DV", phaseB_V), ("DT", kb.c("DT"))]
                    else:
                        wl = [("DT", cb_tr_prev)]
                    y_cr = kb.dma("Y", coef_row[:], coefT[cf_row:cf_row + 1, :],
                                  wait=wl)
                    for tt in range(2):
                        b_cb, ps_cb, w8 = pool.acquire()
                        kb.waitt("T", y_cr)
                        rsl = coef_row[0:1, tt * 512:(tt + 1) * 512]
                        kb.op("T", lambda t, ps_cb=ps_cb, rsl=rsl: t.matmul(
                            ps_cb[:], ones_1b[:], rsl, start=True, stop=True), wait=w8)
                        t_cb = kb.op("T", lambda t: t.drain(), inc=True)
                        dstcb = coef_bc[:, tt * 512:(tt + 1) * 512]
                        v_cb = kb.op("V", lambda v, dstcb=dstcb, ps_cb=ps_cb: v.tensor_copy(
                            dstcb, ps_cb[:]), wait=[("DT", t_cb)], inc=True, bank=b_cb)
                    cb_tr_prev = t_cb
                    silu_ds = {}
                sbi = gi % 2
                kb.wait("T", "DS", units[g["u1"] - 1]["ev"])
                if g["kind"] == "up":
                    slabv = upslab[sbi]
                    nch_s, ic0, gu, sup = g["nch_s"], g["ic0"], g["gu"], g["sup"]
                    pst = 256 if nch_s == 2 else 128
                    t_last = 0
                    for icx in range(nch_s):
                        for tt in range(2):
                            b, ps, w8 = pool.acquire()
                            for j in range(NDCH):
                                xsl = xnT[:, j * TH + tt * 512: j * TH + (tt + 1) * 512]
                                wslab = slabv[:, j * pst + icx * 128: j * pst + icx * 128 + 128]
                                kb.op("T", lambda t, ps=ps, wslab=wslab, xsl=xsl, j=j: t.matmul(
                                    ps[:], wslab, xsl, start=(j == 0), stop=(j == NDCH - 1)),
                                    wait=w8 if j == 0 else None)
                            t_up = kb.op("T", lambda t: t.drain(), inc=True)
                            t_last = t_up
                            if gu == 0:
                                dst = sgtmp[:, icx * TH + tt * 512: icx * TH + (tt + 1) * 512]
                                s_ev = kb.op("S", lambda s, dst=dst, ps=ps: s.activation(
                                    dst, ps[:], AF.Silu),
                                    wait=[("DT", t_up), ("DV", last_vhs)], inc=True, bank=b)
                                silu_ds[(sup, icx, tt)] = s_ev
                            else:
                                hsl = hs[:, (ic0 + icx) * TH + tt * 512: (ic0 + icx) * TH + (tt + 1) * 512]
                                ssl = sgtmp[:, icx * TH + tt * 512: icx * TH + (tt + 1) * 512]
                                cbsl = coef_bc[:, tt * 512:(tt + 1) * 512]
                                kb.wait("V", "DT", t_up)
                                kb.wait("V", "DV", v_cb)
                                kb.wait("V", "DS", silu_ds[(sup, icx, tt)])
                                v_h1 = kb.op("V", lambda v, hsl=hsl, ssl=ssl, ps=ps: v.tensor_tensor(
                                    hsl, ssl, ps[:], ALU.mult), inc=True, bank=b)
                                v_hs = kb.op("V", lambda v, hsl=hsl, cbsl=cbsl: v.tensor_tensor(
                                    hsl, hsl, cbsl, ALU.mult), inc=True)
                                last_vhs = v_hs
                    slab_free[sbi] = t_last
                else:
                    slabv = dnslab[sbi]
                    if g["hd0"] == 0:
                        up_done_v = kb.c("DV")
                    hd0 = g["hd0"]
                    t_last = 0
                    for hdc in range(2):
                        for tt in range(2):
                            b, ps, w8 = pool.acquire()
                            kb.wait("T", "DV", up_done_v)
                            for ic in range(11):
                                hsl = hs[:, ic * TH + tt * 512: ic * TH + (tt + 1) * 512]
                                wslab = slabv[:, ic * 256 + hdc * 128: ic * 256 + hdc * 128 + 128]
                                kb.op("T", lambda t, ps=ps, wslab=wslab, hsl=hsl, ic=ic: t.matmul(
                                    ps[:], wslab, hsl, start=(ic == 0), stop=(ic == 10)),
                                    wait=w8 if ic == 0 else None)
                            t_dn = kb.op("T", lambda t: t.drain(), inc=True)
                            t_last = t_dn
                            asl = acc[:, (hd0 + hdc) * TH + tt * 512: (hd0 + hdc) * TH + (tt + 1) * 512]
                            kb.wait("V", "DT", t_dn)
                            v_dn = kb.op("V", lambda v, asl=asl, ps=ps: v.tensor_tensor(
                                asl, asl, ps[:], ALU.add), inc=True, bank=b)
                    slab_free[sbi] = t_last

            if stop_after == "ffn":
                dbg_taps.append(acc)
                break

            ffn_done_v = kb.c("DV")
            for tc in range(NCH):
                xb = x_tok_e
                yx = kb.dma("Y", xb[:], x_d[th0 + tc * 128: th0 + (tc + 1) * 128, :],
                            wait=[("QS", kb.c("QS")), ("DT", kb.c("DT"))] if tc == 0 else [("QS", kb.c("QS"))])
                for bq in range(4):
                    b, ps, w8 = pool.acquire()
                    kb.wait("T", "DV", ffn_done_v)
                    for jj in range(4):
                        dj = bq * 4 + jj
                        asl = acc[:, dj * TH + tc * 128: dj * TH + (tc + 1) * 128]
                        kb.op("T", lambda t, ps=ps, jj=jj, asl=asl: t.matmul(
                            ps[:, jj * 128:(jj + 1) * 128], asl, ident_f[:],
                            start=True, stop=True), wait=w8 if jj == 0 else None)
                    t_tr = kb.op("T", lambda t: t.drain(), inc=True)
                    kb.wait("V", "DT", t_tr)
                    kb.waitt("V", yx)
                    xsl = xb[:, bq * 512:(bq + 1) * 512]
                    v_ev = kb.op("V", lambda v, xsl=xsl, ps=ps: v.tensor_tensor(
                        xsl, xsl, ps[:], ALU.add), inc=True, bank=b)
                kb.dma("S", out_d[th0 + tc * 128: th0 + (tc + 1) * 128, :], xb[:],
                       wait=[("DV", kb.c("DV"))])

        if stop_after is not None:
            kb.wait("G", "DV", kb.c("DV"))
            kb.wait("G", "DS", kb.c("DS"))
            kb.wait("G", "DT", kb.c("DT"))
            row = 0
            for tens in dbg_taps:
                ap = tens if isinstance(tens, bass.AP) else tens[:]
                P, FREE = ap.shape
                npieces = (FREE + D - 1) // D
                for pc in range(npieces):
                    w = min(D, FREE - pc * D)
                    if row + P <= T:
                        kb.dma("G", out_d[row:row + P, 0:w], ap[:, pc * D:pc * D + w])
                    row += 128

        kb.wait("G", "QS", kb.c("QS"))
        for i in range(KB.NYS):
            kb.wait("G", f"QY{i}", kb.c(f"QY{i}"))
        kb.wait("G", "QG", kb.c("QG"))

        build_program.last_kb = kb
        block = E(nc.Block())
        block.sync(lambda e: [f(e) for f in kb.ops["Y"]])
        block.gpsimd(lambda e: [f(e) for f in kb.ops["G"]])
        block.vector(lambda e: [f(e) for f in kb.ops["V"]])
        block.scalar(lambda e: [f(e) for f in kb.ops["S"]])
        block.tensor(lambda e: [f(e) for f in kb.ops["T"]])

    return nc


def kernel(**inputs):
    x = np.ascontiguousarray(np.asarray(inputs["L0_attn_hidden_states"], np.float32))
    nc = build_program()
    in_maps = []
    for b in range(8):
        m = {"x": np.ascontiguousarray(x[b])}
        for n in W_NAMES:
            m[n] = np.ascontiguousarray(np.asarray(inputs[n], np.float32))
        in_maps.append(m)
    res = run_bass_kernel_spmd(nc, in_maps, core_ids=list(range(8)))
    out = np.stack([np.asarray(r["out"], np.float32) for r in res.results], axis=0)
    return out


# version 09
# speedup vs baseline: 1.5631x; 1.0188x over previous
import numpy as np
from contextlib import ExitStack

import concourse.bass as bass
import concourse.mybir as mybir
from concourse.bass_utils import run_bass_kernel_spmd

F32 = mybir.dt.float32
BF16 = mybir.dt.bfloat16
AF = mybir.ActivationFunctionType
ALU = mybir.AluOpType
AX = mybir.AxisListType

T = 2048
TH = 1024
D = 2048
NDCH = 16
H = 16
DK = DV = M = 128
C = 128
NCH = TH // C
NG = 4
NGRP = H // NG
E_MOE = 16
I_MOE = 1408
I_SH = 5632
NEXP = 20
SCALE = float(DK) ** -0.5
LN_EPS = 1e-5
RMS_EPS = 1e-5
L2_EPS = 1e-6

W_NAMES = ["L0_attn_Wq", "L0_attn_Wk", "L0_attn_Wv", "L0_attn_Ww", "L0_attn_cq",
           "L0_attn_ck", "L0_attn_cv", "L0_attn_Wf1", "L0_attn_Wf2", "L0_attn_Wb",
           "L0_attn_Wg1", "L0_attn_Wg2", "L0_attn_bg2", "L0_attn_norm_w",
           "L0_attn_Wo", "L0_ffn_gate_w", "L0_ffn_sgate_w", "L0_ffn_w13",
           "L0_ffn_w2", "L0_ffn_shared_gate_up_w", "L0_ffn_shared_down_w"]


class KB:
    ENGS = ("T", "V", "S", "G", "Y")
    NYS = 10

    def __init__(self, nc, ctx):
        self.nc = nc
        E = ctx.enter_context
        self.ops = {e: [] for e in self.ENGS}
        self.sem = {}
        self.cnt = {}
        names = (["DT", "DV", "DS", "DG", "QG", "QS"]
                 + [f"QY{i}" for i in range(self.NYS)]
                 + [f"PS{i}" for i in range(8)])
        for name in names:
            self.sem[name] = E(nc.semaphore(name))
            self.cnt[name] = 0
        self.done_of = {"T": "DT", "V": "DV", "S": "DS", "G": "DG"}
        self.q_of = {"G": "QG", "S": "QS"}
        self.ynext = 0
        self.trace = []

    def op(self, eng, f, inc=False, wait=None, drain=None, bank=None):
        semname = self.done_of.get(eng)
        if inc:
            self.cnt[semname] += 1
        sem_obj = self.sem.get(semname) if inc else None
        bank_sem = None
        if bank is not None:
            bn = f"PS{bank}"
            self.cnt[bn] += 1
            bank_sem = self.sem[bn]
        wait = self._flat(wait)
        waits = [(self.sem[s], v) for (s, v) in wait]
        incs = ([(semname, 1)] if inc else []) + (
            [(f"PS{bank}", 1)] if bank is not None else [])
        self.trace.append((eng, wait, incs))
        if drain is None:
            drain = eng in ("V", "S", "G")

        if bank_sem is not None:
            drain = True

        def run(e, f=f, waits=waits, sem_obj=sem_obj, drain=drain, bank_sem=bank_sem):
            for s, v in waits:
                e.wait_ge(s, v)
            inst = f(e)
            if sem_obj is not None:
                inst.then_inc(sem_obj, 1)
            if drain:
                d = e.drain()
                if bank_sem is not None:
                    d.then_inc(bank_sem, 1)
        self.ops[eng].append(run)
        return self.cnt[semname] if inc else None

    @staticmethod
    def _flat(wait):
        out = []
        for w in (wait or []):
            if isinstance(w, tuple) and isinstance(w[0], str):
                out.append(w)
            else:
                out.extend(w)
        return [(s, v) for (s, v) in out if v > 0]

    def dma(self, eng, out, in_, wait=None, **kw):
        wait = self._flat(wait)
        if eng == "Y":
            q = f"QY{self.ynext}"
            self.ynext = (self.ynext + 1) % self.NYS
        else:
            q = self.q_of[eng]
        if self.cnt[q] > 0:
            wait = wait + [(q, self.cnt[q])]
        self.cnt[q] += 16
        sem_obj = self.sem[q]
        waits = [(self.sem[s], v) for (s, v) in wait]
        self.trace.append((eng, wait, [(q, 16)]))

        def run(e, out=out, in_=in_, kw=kw, waits=waits, sem_obj=sem_obj):
            for s, v in waits:
                e.wait_ge(s, v)
            e.dma_start(out=out, in_=in_, **kw).then_inc(sem_obj, 16)
        self.ops[eng].append(run)
        if eng == "Y":
            return tuple((f"QY{i}", self.cnt[f"QY{i}"]) for i in range(self.NYS))
        return ((q, self.cnt[q]),)

    def dma1(self, eng, out, in_, wait=None, **kw):
        qprev = self.ynext
        tok = self.dma(eng, out, in_, wait=wait, **kw)
        if eng == "Y":
            return ((f"QY{qprev}", self.cnt[f"QY{qprev}"]),)
        return tok

    def wait(self, eng, sem_name, value):
        if value > 0:
            self.trace.append((eng, [(sem_name, value)], []))
            self.ops[eng].append(
                lambda e, s=self.sem[sem_name], v=value: e.wait_ge(s, v))

    def waitt(self, eng, token):
        for (sname, v) in token:
            self.wait(eng, sname, v)

    def c(self, sem_name):
        return self.cnt[sem_name]


class PsumPool:

    def __init__(self, kb, tiles):
        self.kb = kb
        self.tiles = tiles
        self.rr = 0

    def acquire(self):
        b = self.rr
        self.rr = (self.rr + 1) % 8
        wait = [(f"PS{b}", self.kb.cnt[f"PS{b}"])]
        return b, self.tiles[b], wait


def build_program(stop_after=None, n_half=2, do_scratch=True):
    nc = bass.Bass("TRN2", target_bir_lowering=False)

    x_d = nc.dram_tensor("x", [T, D], F32, kind="ExternalInput").ap()
    shapes = {
        "L0_attn_Wq": [D, D], "L0_attn_Wk": [D, D], "L0_attn_Wv": [D, D],
        "L0_attn_Ww": [D, D], "L0_attn_cq": [D, 4], "L0_attn_ck": [D, 4],
        "L0_attn_cv": [D, 4], "L0_attn_Wf1": [D, 128], "L0_attn_Wf2": [128, D],
        "L0_attn_Wb": [D, 16], "L0_attn_Wg1": [D, 128], "L0_attn_Wg2": [128, D],
        "L0_attn_bg2": [D], "L0_attn_norm_w": [128], "L0_attn_Wo": [D, D],
        "L0_ffn_gate_w": [16, D], "L0_ffn_sgate_w": [1, D],
        "L0_ffn_w13": [E_MOE, 2 * I_MOE, D], "L0_ffn_w2": [E_MOE, D, I_MOE],
        "L0_ffn_shared_gate_up_w": [2 * I_SH, D], "L0_ffn_shared_down_w": [D, I_SH],
    }
    w_d = {n: nc.dram_tensor(n, s, F32, kind="ExternalInput").ap()
           for n, s in shapes.items()}
    out_d = nc.dram_tensor("out", [T, D], F32, kind="ExternalOutput").ap()

    scr_shapes = {
        "wq": [D, D], "wk": [D, D], "wv": [D, D], "ww": [D, D],
        "wf1": [D, 128], "wf2": [128, D], "wb": [D, 16],
        "wg1": [D, 128], "wg2": [128, D], "wo": [D, D],
        "gatew": [16, D],
        "w13": [E_MOE, 2 * I_MOE, D], "w2": [E_MOE, D, I_MOE],
        "sgu": [2 * I_SH, D], "sdw": [D, I_SH],
    }
    scr = {n: nc.dram_tensor("scr_" + n, s, BF16, kind="Internal").ap()
           for n, s in scr_shapes.items()}

    dbg_taps = []

    with ExitStack() as ctx:
        E = ctx.enter_context
        kb = KB(nc, ctx)

        xnT = E(nc.sbuf_tensor("xnT", [128, NDCH * TH], BF16))
        acc = E(nc.sbuf_tensor("acc", [128, NDCH * TH], F32))
        big = E(nc.sbuf_tensor("big", [128, 20480], BF16))
        qf = big[:, 0:4096]
        kf = big[:, 4096:8192]
        vf = big[:, 8192:12288]
        wff = big[:, 12288:16384]
        o_grp = big[:, 16384:20480]
        hs = big[:, 0:11264]
        sgtmp = big[:, 11264:13312]

        slab = [E(nc.sbuf_tensor(f"slab{i}", [128, 4096], BF16)) for i in range(2)]
        wp = [slab[0][:, 0:2048], slab[1][:, 0:2048]]
        upslab = [slab[0], slab[1]]
        dnslab = [slab[0][:, 0:2816], slab[1][:, 0:2816]]
        wp_free = up_free = dn_free = slab_free = [0, 0]

        x_tok = acc[:, 0:2048]
        x_tok_e = big[:, 0:4096].bitcast(F32)
        xn_tok = E(nc.sbuf_tensor("xn_tok", [128, D], BF16))
        raw = E(nc.sbuf_tensor("raw", [128, 528], F32))
        conv_acc = E(nc.sbuf_tensor("conv_acc", [128, 512], F32))
        tail3 = E(nc.sbuf_tensor("tail3", [128, 4], F32))
        tails = E(nc.sbuf_tensor("tails", [128, 4 * 16 * 3], F32))
        p_sb = E(nc.sbuf_tensor("p_sb", [128, 512], F32))
        lam_c = E(nc.sbuf_tensor("lam_c", [128, 512], F32))
        lam_f = E(nc.sbuf_tensor("lam_f", [128, 512], F32))
        eu = E(nc.sbuf_tensor("eu", [128, 512], BF16))
        rfac = E(nc.sbuf_tensor("rfac", [128, 512], BF16))
        elam_row = E(nc.sbuf_tensor("elam_row", [1, 512], BF16))
        elam_col = E(nc.sbuf_tensor("elam_col", [128, 16], F32))
        gpre1 = E(nc.sbuf_tensor("gpre1", [128, TH], BF16))
        gpre2 = E(nc.sbuf_tensor("gpre2", [128, TH], BF16))
        Sk = E(nc.sbuf_tensor("Sk", [128, H * M], BF16))
        Sv = E(nc.sbuf_tensor("Sv", [128, H * DV], BF16))
        beta_all = E(nc.sbuf_tensor("beta_all", [128, NCH * 16], F32))
        coefT = E(nc.sbuf_tensor("coefT", [32, TH], BF16))
        coef_bc = E(nc.sbuf_tensor("coef_bc", [128, TH], BF16))
        st1 = E(nc.sbuf_tensor("st1", [128, 128], F32))
        wfg1 = E(nc.sbuf_tensor("wfg1", [128, 2048], BF16))
        ktok4 = wfg1[:, 0:512]
        vtok4 = wfg1[:, 512:1024]
        wn4 = wfg1[:, 1024:1536]
        u4 = wfg1[:, 1536:2048]
        r4 = E(nc.sbuf_tensor("r4", [128, 512], BF16))
        am4 = E(nc.sbuf_tensor("am4", [128, 512], BF16))
        st14 = E(nc.sbuf_tensor("st14", [128, 512], F32))
        st24 = E(nc.sbuf_tensor("st24", [128, 512], F32))
        coef_row = st24[:].bitcast(BF16)[0:1, 0:TH]
        ac4 = E(nc.sbuf_tensor("ac4", [128, 512], BF16))
        af4 = E(nc.sbuf_tensor("af4", [128, 512], BF16))
        o4 = af4
        o24 = E(nc.sbuf_tensor("o24", [128, 512], BF16))
        sktmp4 = ac4
        wtok4 = wn4
        pm4 = am4
        acT4 = ac4
        afT4 = af4
        uT4 = u4
        eg4 = p_sb[:].bitcast(BF16)[:, 0:512]
        cols2 = E(nc.sbuf_tensor("cols2", [128, 64], F32))
        wsum4 = cols2[:, 0:16]
        wrs4 = cols2[:, 16:32]
        nmx4 = cols2[:, 32:48]
        smr4 = cols2[:, 48:64]
        ebg = E(nc.sbuf_tensor("ebg", [128, 16], F32))
        cols = E(nc.sbuf_tensor("cols", [128, 20], F32))
        wsum = cols[:, 0:1]
        wrs = cols[:, 1:2]
        nmx = cols[:, 2:3]
        smr = cols[:, 3:4]
        sx = cols[:, 4:6]
        meanc = cols[:, 6:7]
        varc = cols[:, 7:8]
        rstdc = cols[:, 8:9]
        thr_t = cols[:, 9:10]
        m8_t = cols[:, 10:18]
        rrow = am4[0:1, 0:512]
        lrow = st14[0:1, 0:512]
        rw_t = E(nc.sbuf_tensor("rw_t", [128, 16], F32))
        cf_t = E(nc.sbuf_tensor("cf_t", [128, 17], F32))
        ctaps = E(nc.sbuf_tensor("ctaps", [128, 3 * 64], F32))
        nbg2 = E(nc.sbuf_tensor("nbg2", [128, 16], F32))
        normw = E(nc.sbuf_tensor("normw", [128, 1], F32))
        wb_sb = wfg1[:, 0:256]
        gw_sb = wfg1[:, 256:512]
        sgw_sb = wfg1[:, 512:528]
        wf2g = E(nc.sbuf_tensor("wf2g", [128, 512], BF16))
        wg2g = E(nc.sbuf_tensor("wg2g", [128, 512], BF16))
        ident_b = E(nc.sbuf_tensor("ident_b", [128, 128], BF16))
        ident_f = E(nc.sbuf_tensor("ident_f", [128, 128], F32))
        mask01 = E(nc.sbuf_tensor("mask01", [128, 128], BF16))
        Lhalf = E(nc.sbuf_tensor("Lhalf", [128, 128], F32))
        Lfull = E(nc.sbuf_tensor("Lfull", [128, 128], F32))
        Ustrict = E(nc.sbuf_tensor("Ustrict", [128, 128], F32))
        ones_cb = E(nc.sbuf_tensor("ones_cb", [128, 1], BF16))
        ones_cf = E(nc.sbuf_tensor("ones_cf", [128, 1], F32))
        ones_1b = E(nc.sbuf_tensor("ones_1b", [1, 128], BF16))
        cst = E(nc.sbuf_tensor("cst", [128, 4], F32))
        b_lneps = cst[:, 0:1]
        b_l2eps = cst[:, 1:2]
        b_rmse = cst[:, 2:3]
        b_hl128 = cst[:, 3:4]

        psum = [E(nc.psum_tensor(f"ps{i}", [128, 512], F32)) for i in range(8)]
        pool = PsumPool(kb, psum)

        def consts(g):
            for t_, fill, cmp in ((ident_b, 1.0, ALU.not_equal),
                                  (ident_f, 1.0, ALU.not_equal)):
                g.memset(t_[:], 0.0)
                g.drain()
                g.affine_select(out=t_[:], in_=t_[:], compare_op=cmp, fill=fill,
                                base=0, pattern=[[-1, 128]], channel_multiplier=1)
                g.drain()
            for t_, keepval, fillval in ((mask01, 1.0, 0.0), (Lfull, 1.0, 0.0),
                                         (Lhalf, 0.5, -0.5)):
                g.memset(t_[:], keepval)
                g.drain()
                g.affine_select(out=t_[:], in_=t_[:], compare_op=ALU.is_ge,
                                fill=fillval, base=0, pattern=[[1, 128]],
                                channel_multiplier=-1)
                g.drain()
            g.memset(Ustrict[:], 1.0)
            g.drain()
            g.affine_select(out=Ustrict[:], in_=Ustrict[:], compare_op=ALU.is_gt,
                            fill=0.0, base=0, pattern=[[-1, 128]], channel_multiplier=1)
            g.drain()
            g.memset(ones_cb[:], 1.0)
            g.memset(ones_cf[:], 1.0)
            g.memset(ones_1b[:], 1.0)
            g.memset(cst[:, 0:1], LN_EPS)
            g.memset(cst[:, 1:2], L2_EPS)
            g.memset(cst[:, 2:3], 128.0 * RMS_EPS)
            g.memset(cst[:, 3:4], 0.5 * float(np.log(128.0)))
            g.memset(tails[:], 0.0)
            g.memset(cols2[:], 1.0)
            g.memset(elam_col[:], 1.0)
            g.memset(Sk[:], 0.0)
            return g.memset(Sv[:], 0.0)
        g_consts = kb.op("G", consts, inc=True)

        for pi, wn in enumerate(["L0_attn_cq", "L0_attn_ck", "L0_attn_cv"]):
            src = w_d[wn].rearrange("(j p) k -> p j k", p=128)
            dst = ctaps[:, pi * 64:(pi + 1) * 64].rearrange("p (j k) -> p j k", k=4)
            kb.dma("G", dst, src)
        g_bg2 = kb.dma("G", nbg2[:], w_d["L0_attn_bg2"].rearrange("(h p) -> p h", p=128), allow_slow_non_contiguous=True)
        g_nw = kb.dma("G", normw[:], w_d["L0_attn_norm_w"].rearrange("(p one) -> p one", p=128))
        v_nbg = kb.op("V", lambda v: v.tensor_scalar_mul(nbg2[:], nbg2[:], -1.0),
              wait=[g_bg2], inc=True)
        kb.op("S", lambda sc: sc.activation(ebg[:], nbg2[:], AF.Exp),
              wait=[("DV", v_nbg)])

        scrw = {}
        def scratch_write(tag, dst, src):
            if not do_scratch:
                scrw[tag] = ()
                return
            scrw[tag] = kb.dma("G", dst, src)
        for tag, wn in [("wq", "L0_attn_Wq"), ("wk", "L0_attn_Wk"),
                        ("wv", "L0_attn_Wv"), ("ww", "L0_attn_Ww"),
                        ("wf1", "L0_attn_Wf1"), ("wf2", "L0_attn_Wf2"),
                        ("wb", "L0_attn_Wb"), ("wg1", "L0_attn_Wg1"),
                        ("wg2", "L0_attn_Wg2"), ("gatew", "L0_ffn_gate_w"),
                        ("wo", "L0_attn_Wo")]:
            scratch_write(tag, scr[tag], w_d[wn])
        scratch_write("w13_0", scr["w13"][0], w_d["L0_ffn_w13"][0])
        scratch_write("w2_0", scr["w2"][0], w_d["L0_ffn_w2"][0])

        def chunked_rows(ap):
            return ap.rearrange("(j p) c -> p j c", p=128)


        if stop_after == "consts":
            dbg_taps.extend([ident_b, mask01, Lhalf, Lfull, Ustrict])
        for half in range(0 if stop_after == "consts" else n_half):
            th0 = half * TH

            prev_vxn = 0
            for tc in range(NCH):
                xb = x_tok
                yx = kb.dma("Y", xb[:], x_d[th0 + tc * 128: th0 + (tc + 1) * 128, :],
                            wait=[("DV", kb.c("DV")), ("DS", kb.c("DS")), ("DT", kb.c("DT")),
                                  ("QS", kb.c("QS"))] if (tc >= 1 or half > 0) else None)
                kb.waitt("S", yx)
                kb.wait("S", "DV", prev_vxn)
                s_sums = kb.op("S", lambda s, xb=xb: s.activation(
                    xn_tok[:], xb[:], AF.Square, accum_out=sx[:, 1:2]), inc=True)
                kb.waitt("V", yx)
                kb.op("V", lambda v, xb=xb: v.reduce_sum(sx[:, 0:1], xb[:], axis=AX.X))
                kb.op("V", lambda v: v.drain())
                if stop_after == "ln_sums":
                    dbg_taps.append(cols)
                    break
                kb.wait("V", "DS", s_sums)
                kb.op("V", lambda v: v.tensor_scalar_mul(meanc[:], sx[:, 0:1], 1.0 / D))
                kb.op("V", lambda v: v.tensor_scalar_mul(varc[:], sx[:, 1:2], 1.0 / D))
                kb.op("V", lambda v: v.drain())
                kb.op("V", lambda v: v.tensor_tensor(st1[:, 0:1], meanc[:], meanc[:], ALU.mult))
                kb.op("V", lambda v: v.drain())
                v_var = kb.op("V", lambda v: v.tensor_tensor(
                    varc[:], varc[:], st1[:, 0:1], ALU.subtract), inc=True)
                kb.wait("S", "DV", v_var)
                kb.op("S", lambda s: s.activation(rstdc[:], varc[:], AF.Ln, bias=b_lneps))
                s_rstd = kb.op("S", lambda s: s.activation(
                    rstdc[:], rstdc[:], AF.Exp, scale=-0.5), inc=True)
                kb.wait("V", "DS", s_rstd)
                v_xn = kb.op("V", lambda v, xb=xb: v.tensor_scalar(
                    xn_tok[:], xb[:], meanc[:], rstdc[:], ALU.subtract, ALU.mult), inc=True)
                prev_vxn = v_xn
                if stop_after == "ln":
                    dbg_taps.extend([xn_tok, cols])
                    break
                for bq in range(4):
                    b, ps, w8 = pool.acquire()
                    kb.wait("T", "DG", g_consts)
                    kb.wait("T", "DV", v_xn)
                    for jj in range(4):
                        j = bq * 4 + jj
                        src_sl = xn_tok[:, j * 128:(j + 1) * 128]
                        kb.op("T", lambda t, ps=ps, jj=jj, src_sl=src_sl: t.matmul(
                            ps[:, jj * 128:(jj + 1) * 128], src_sl, ident_b[:],
                            start=True, stop=True), wait=w8 if jj == 0 else None)
                    t_tr = kb.op("T", lambda t: t.drain(), inc=True)
                    dst = xnT[:].rearrange("p (j t) -> p j t", j=NDCH)[
                        :, bq * 4:(bq + 1) * 4, tc * 128:(tc + 1) * 128]
                    src = ps[:].rearrange("p (j t) -> p j t", j=4)
                    s_ev = kb.op("S", lambda s, dst=dst, src=src: s.activation(
                        dst, src, AF.Copy), wait=[("DT", t_tr)], inc=True, bank=b)

            if stop_after in ("ln", "ln_sums"):
                break
            xn_done = kb.c("DS")
            if stop_after == "xn":
                dbg_taps.append(xnT)
                break

            y_wb = kb.dma("Y", wb_sb.rearrange("p (j c) -> p j c", j=16),
                          chunked_rows(scr["wb"]),
                          wait=[scrw["wb"], ("DT", kb.c("DT"))])
            y_gw = None
            for j in range(16):
                y_gw = kb.dma("Y", gw_sb[:, j * 16:(j + 1) * 16],
                              scr["gatew"][:, j * 128:(j + 1) * 128],
                              wait=[scrw["gatew"]] if j == 0 else None,
                              transpose=True)
            g_sgw = kb.dma("G", sgw_sb.rearrange("p (one j) -> p one j", one=1),
                           w_d["L0_ffn_sgate_w"].rearrange("one (j p) -> p one j", p=128),
                           allow_slow_non_contiguous=True,
                           wait=[("DT", kb.c("DT"))])
            prev_vct = 0
            for tc in range(NCH):
                b, ps, w8 = pool.acquire()
                kb.wait("T", "DS", xn_done)
                kb.waitt("T", y_wb)
                kb.waitt("T", y_gw)
                kb.waitt("T", g_sgw)
                for j in range(NDCH):
                    xsl = xnT[:, j * TH + tc * 128: j * TH + (tc + 1) * 128]
                    kb.op("T", lambda t, ps=ps, j=j, xsl=xsl: t.matmul(
                        ps[:, 0:16], xsl, wb_sb[:, j * 16:(j + 1) * 16],
                        start=(j == 0), stop=(j == NDCH - 1)), wait=w8 if j == 0 else None)
                for j in range(NDCH):
                    xsl = xnT[:, j * TH + tc * 128: j * TH + (tc + 1) * 128]
                    kb.op("T", lambda t, ps=ps, j=j, xsl=xsl: t.matmul(
                        ps[:, 16:32], xsl, gw_sb[:, j * 16:(j + 1) * 16],
                        start=(j == 0), stop=(j == NDCH - 1)))
                for j in range(NDCH):
                    xsl = xnT[:, j * TH + tc * 128: j * TH + (tc + 1) * 128]
                    kb.op("T", lambda t, ps=ps, j=j, xsl=xsl: t.matmul(
                        ps[:, 32:33], xsl, sgw_sb[:, j:j + 1],
                        start=(j == 0), stop=(j == NDCH - 1)))
                t_b = kb.op("T", lambda t: t.drain(), inc=True)
                kb.wait("S", "DT", t_b)
                kb.wait("S", "DV", prev_vct)
                s_eb = kb.op("S", lambda s, ps=ps: s.activation(
                    rw_t[:], ps[:, 0:16], AF.Exp, scale=-1.0), inc=True, bank=b)
                kb.wait("V", "DS", s_eb)
                kb.op("V", lambda v: v.tensor_scalar_add(rw_t[:], rw_t[:], 1.0))
                kb.op("V", lambda v: v.drain())
                dstb = beta_all[:, tc * 16:(tc + 1) * 16]
                kb.op("V", lambda v, dstb=dstb: v.reciprocal(dstb, rw_t[:]))
                s_er = kb.op("S", lambda s, ps=ps: s.activation(
                    cf_t[:, 0:16], ps[:, 16:32], AF.Exp, accum_out=thr_t[:]), inc=True, bank=b)
                kb.wait("V", "DS", s_er)
                kb.op("V", lambda v: v.reciprocal(smr[:], thr_t[:]))
                kb.op("V", lambda v: v.drain())
                kb.op("V", lambda v: v.tensor_scalar_mul(rw_t[:], cf_t[:, 0:16], smr[:]))
                kb.op("V", lambda v: v.drain())
                kb.op("V", lambda v: v.max(out=m8_t[:], in_=rw_t[:]))
                kb.op("V", lambda v: v.drain())
                kb.op("V", lambda v: v.tensor_single_scalar(thr_t[:], m8_t[:, 3:4], 1.0, ALU.mult))
                kb.op("V", lambda v: v.drain())
                kb.op("V", lambda v: v.tensor_single_scalar(cf_t[:, 0:16], rw_t[:], thr_t[:], ALU.is_ge))
                kb.op("V", lambda v: v.drain())
                kb.op("V", lambda v: v.tensor_tensor(cf_t[:, 0:16], cf_t[:, 0:16], rw_t[:], ALU.mult))
                s_eg = kb.op("S", lambda s, ps=ps: s.activation(
                    st1[:, 0:1], ps[:, 32:33], AF.Exp, scale=-1.0), inc=True, bank=b)
                kb.wait("V", "DS", s_eg)
                kb.op("V", lambda v: v.tensor_scalar_add(st1[:, 0:1], st1[:, 0:1], 1.0))
                kb.op("V", lambda v: v.drain())
                v_sgv = kb.op("V", lambda v: v.reciprocal(cf_t[:, 16:17], st1[:, 0:1]), inc=True)
                b2, ps2, w82 = pool.acquire()
                kb.wait("T", "DV", v_sgv)
                kb.op("T", lambda t, ps2=ps2: t.matmul(
                    ps2[0:17, 0:128], cf_t[:], ident_f[:], start=True, stop=True), wait=w82)
                t_ct = kb.op("T", lambda t: t.drain(), inc=True)
                dstc = coefT[0:17, tc * 128:(tc + 1) * 128]
                v_ct = kb.op("V", lambda v, dstc=dstc, ps2=ps2: v.tensor_copy(
                    dstc, ps2[0:17, 0:128]), wait=[("DT", t_ct)], inc=True, bank=b2)
                prev_vct = v_ct

            for (stag, gdst) in (("wf1", gpre1), ("wg1", gpre2)):
                y_g1 = kb.dma("Y", wfg1[:].rearrange("p (j c) -> p j c", j=16),
                              chunked_rows(scr[stag]),
                              wait=[scrw[stag], ("DT", kb.c("DT"))])
                for tt in range(2):
                    b, ps, w8 = pool.acquire()
                    kb.waitt("T", y_g1)
                    for j in range(NDCH):
                        xsl = xnT[:, j * TH + tt * 512: j * TH + (tt + 1) * 512]
                        kb.op("T", lambda t, ps=ps, j=j, xsl=xsl: t.matmul(
                            ps[:], wfg1[:, j * 128:(j + 1) * 128], xsl,
                            start=(j == 0), stop=(j == NDCH - 1)), wait=w8 if j == 0 else None)
                    t_g = kb.op("T", lambda t: t.drain(), inc=True)
                    dstsl = gdst[:, tt * 512:(tt + 1) * 512]
                    s_ev = kb.op("S", lambda s, dstsl=dstsl, ps=ps: s.activation(
                        dstsl, ps[:], AF.Copy), wait=[("DT", t_g)], inc=True, bank=b)

            phaseB_V = kb.c("DV")
            if half == 0:
                kb.wait("G", "DV", phaseB_V)
                for e_ in range(1, E_MOE):
                    scratch_write(f"w13_{e_}", scr["w13"][e_], w_d["L0_ffn_w13"][e_])
                    scratch_write(f"w2_{e_}", scr["w2"][e_], w_d["L0_ffn_w2"][e_])
                scratch_write("sgu", scr["sgu"], w_d["L0_ffn_shared_gate_up_w"])
                scratch_write("sdw", scr["sdw"], w_d["L0_ffn_shared_down_w"])
            phaseB_S = kb.c("DS")
            if stop_after == "phaseB":
                dbg_taps.extend([beta_all, coefT[0:17, :], gpre1, gpre2])
                break

            for grp in range(NGRP):
                wsel = {0: "wq", 1: "wk", 2: "wv", 3: "ww"}
                dsel = {0: qf, 1: kf, 2: vf, 3: wff}
                for pi in range(4):
                    for hin in range(NG):
                        head = grp * NG + hin
                        bi = (pi * NG + hin) % 2
                        wbuf = wp[bi]
                        stag = wsel[pi]
                        src = scr[stag].rearrange("(j p) c -> p j c", p=128)[
                            :, :, head * 128:(head + 1) * 128]
                        yw = kb.dma("Y", wbuf[:], src,
                                    wait=[scrw[stag], ("DT", wp_free[bi])])
                        tsl = tails[:, (pi * 16 + head) * 3:(pi * 16 + head) * 3 + 3]
                        cbase = {0: 0, 1: 64, 2: 128, 3: 128}[pi]
                        for tt in range(2):
                            b, ps, w8 = pool.acquire()
                            kb.waitt("T", yw)
                            for j in range(NDCH):
                                xsl = xnT[:, j * TH + tt * 512: j * TH + (tt + 1) * 512]
                                kb.op("T", lambda t, ps=ps, wbuf=wbuf, j=j, xsl=xsl: t.matmul(
                                    ps[:], wbuf[:, j * 128:(j + 1) * 128], xsl,
                                    start=(j == 0), stop=(j == NDCH - 1)),
                                    wait=w8 if j == 0 else None)
                            t_p = kb.op("T", lambda t: t.drain(), inc=True)
                            if tt == 1:
                                wp_free[bi] = t_p
                            v_ev = kb.op("V", lambda v, ps=ps: v.tensor_copy(
                                raw[:, 16:528], ps[:]), wait=[("DT", t_p)], inc=True, bank=b)
                            if tt == 0:
                                kb.op("V", lambda v, tsl=tsl: v.tensor_copy(raw[:, 13:16], tsl))
                                kb.op("V", lambda v: v.tensor_copy(tail3[:, 0:3], raw[:, 525:528]))
                            else:
                                kb.op("V", lambda v: v.tensor_copy(raw[:, 13:16], tail3[:, 0:3]))
                                kb.op("V", lambda v, tsl=tsl: v.tensor_copy(tsl, raw[:, 525:528]))
                            tap0 = ctaps[:, cbase + head * 4: cbase + head * 4 + 1]
                            kb.op("V", lambda v, tap0=tap0: v.tensor_scalar_mul(
                                conv_acc[:], raw[:, 13:525], tap0))
                            for k in range(1, 4):
                                tapk = ctaps[:, cbase + head * 4 + k: cbase + head * 4 + k + 1]
                                rsl = raw[:, 13 + k:525 + k]
                                kb.op("V", lambda v, tapk=tapk, rsl=rsl: v.scalar_tensor_tensor(
                                    conv_acc[:], rsl, tapk, conv_acc[:], ALU.mult, ALU.add))
                            v_cv = kb.op("V", lambda v: v.engine_nop(), inc=True)
                            dproj = dsel[pi][:, hin * TH + tt * 512: hin * TH + (tt + 1) * 512]
                            s_si = kb.op("S", lambda s, dproj=dproj: s.activation(
                                dproj, conv_acc[:], AF.Silu), wait=[("DV", v_cv)], inc=True)
                            kb.wait("V", "DS", s_si)

                c1_done_s = kb.c("DS")
                if stop_after == "proj" and grp == 0:
                    dbg_taps.extend([qf, kf, vf, wff])
                    break

                y_f2 = kb.dma("Y", wf2g[:], scr["wf2"][:, grp * 512:(grp + 1) * 512],
                              wait=[scrw["wf2"], ("DT", kb.c("DT"))])
                y_g2 = kb.dma("Y", wg2g[:], scr["wg2"][:, grp * 512:(grp + 1) * 512],
                              wait=[scrw["wg2"], ("DT", kb.c("DT"))])
                kb.waitt("T", y_f2)
                kb.waitt("T", y_g2)
                kb.wait("T", "DS", c1_done_s)
                for cc in range(NCH):
                    tc0 = cc * 128
                    kb.wait("S", "DV", kb.c("DV"))
                    gp_sl = gpre1[:, tc0:tc0 + 128]
                    wf2_sl = wf2g[:]
                    b_p, ps_p, w8 = pool.acquire()
                    kb.op("T", lambda t, ps_p=ps_p, gp_sl=gp_sl, wf2_sl=wf2_sl: t.matmul(
                        ps_p[:], gp_sl, wf2_sl, start=True, stop=True), wait=w8)
                    t_pp = kb.op("T", lambda t: t.drain(), inc=True)
                    kb.wait("S", "DT", t_pp)
                    s_pe = kb.op("S", lambda s, ps_p=ps_p: s.activation(
                        p_sb[:], ps_p[:], AF.Exp, scale=-1.0), inc=True, bank=b_p)
                    s_p = kb.op("S", lambda s: s.activation(
                        p_sb[:], p_sb[:], AF.Ln, bias=1.0), inc=True)
                    b_ch, ps_ch, w1 = pool.acquire()
                    b_cf, ps_cf, w2_ = pool.acquire()
                    b_rc, ps_rc, w3 = pool.acquire()
                    b_sm, ps_sm, w4 = pool.acquire()
                    b_lc, ps_lc, w5 = pool.acquire()
                    kb.wait("T", "DS", s_p)
                    kb.op("T", lambda t, ps_ch=ps_ch: t.matmul(
                        ps_ch[:], Lhalf[:], p_sb[:], start=True, stop=True), wait=w1)
                    kb.op("T", lambda t, ps_cf=ps_cf: t.matmul(
                        ps_cf[:], Lfull[:], p_sb[:], start=True, stop=True), wait=w2_)
                    kb.op("T", lambda t, ps_rc=ps_rc: t.matmul(
                        ps_rc[:], Ustrict[:], p_sb[:], start=True, stop=True), wait=w3)
                    kb.op("T", lambda t, ps_sm=ps_sm: t.matmul(
                        ps_sm[0:1, :], ones_cf[:], p_sb[:], start=True, stop=True), wait=w4)
                    for hin in range(NG):
                        psl = p_sb[:, hin * 128:(hin + 1) * 128]
                        kb.op("T", lambda t, ps_lc=ps_lc, hin=hin, psl=psl: t.matmul(
                            ps_lc[:, hin:hin + 1], psl, ones_cf[:],
                            start=True, stop=True), wait=w5 if hin == 0 else None)
                    t_cum = kb.op("T", lambda t: t.drain(), inc=True)
                    kb.wait("S", "DT", t_cum)
                    kb.op("S", lambda s, ps_ch=ps_ch: s.activation(lam_c[:], ps_ch[:], AF.Exp, scale=-1.0), inc=True, bank=b_ch)
                    s_ch = kb.op("S", lambda s, ps_ch=ps_ch: s.activation(eu[:], ps_ch[:], AF.Exp, scale=1.0), inc=True, bank=b_ch)
                    s_cf = kb.op("S", lambda s, ps_cf=ps_cf: s.activation(lam_f[:], ps_cf[:], AF.Exp, scale=-1.0), inc=True, bank=b_cf)
                    s_rc = kb.op("S", lambda s, ps_rc=ps_rc: s.activation(rfac[:], ps_rc[:], AF.Exp, scale=-1.0), inc=True, bank=b_rc)
                    s_sm = kb.op("S", lambda s, ps_sm=ps_sm: s.activation(
                        elam_row[:], ps_sm[0:1, 0:512], AF.Exp, scale=-1.0), inc=True, bank=b_sm)
                    ec_out = elam_col[:].rearrange("p (h s) -> p h s", s=4)[:, :, 0:1]
                    s_cum = kb.op("S", lambda s, ps_lc=ps_lc, ec_out=ec_out: s.activation(
                        ec_out, ps_lc[:, 0:4].rearrange("p (h s) -> p h s", s=1),
                        AF.Exp, scale=-1.0), inc=True, bank=b_lc)

                    sk_g = Sk[:, grp * 512:(grp + 1) * 512]
                    sv_g = Sv[:, grp * 512:(grp + 1) * 512]
                    b_tk, ps_tk, w1 = pool.acquire()
                    b_tv, ps_tv, w2_ = pool.acquire()
                    b_tw, ps_tw, w3 = pool.acquire()
                    for hin in range(NG):
                        ksl = kf[:, hin * TH + tc0: hin * TH + tc0 + 128]
                        vsl = vf[:, hin * TH + tc0: hin * TH + tc0 + 128]
                        wsl = wff[:, hin * TH + tc0: hin * TH + tc0 + 128]
                        kb.op("T", lambda t, ps_tk=ps_tk, hin=hin, ksl=ksl: t.matmul(
                            ps_tk[:, hin * 128:(hin + 1) * 128], ksl, ident_b[:],
                            start=True, stop=True), wait=w1 if hin == 0 else None)
                        kb.op("T", lambda t, ps_tv=ps_tv, hin=hin, vsl=vsl: t.matmul(
                            ps_tv[:, hin * 128:(hin + 1) * 128], vsl, ident_b[:],
                            start=True, stop=True), wait=w2_ if hin == 0 else None)
                        kb.op("T", lambda t, ps_tw=ps_tw, hin=hin, wsl=wsl: t.matmul(
                            ps_tw[:, hin * 128:(hin + 1) * 128], wsl, ident_b[:],
                            start=True, stop=True), wait=w3 if hin == 0 else None)
                    t_tr = kb.op("T", lambda t: t.drain(), inc=True)
                    kb.wait("V", "DT", t_tr)
                    v_k = kb.op("V", lambda v, ps_tk=ps_tk: v.tensor_copy(ktok4[:], ps_tk[:]), inc=True, bank=b_tk)
                    v_v = kb.op("V", lambda v, ps_tv=ps_tv: v.tensor_copy(vtok4[:], ps_tv[:]), inc=True, bank=b_tv)
                    v_tr = kb.op("V", lambda v, ps_tw=ps_tw: v.tensor_copy(wtok4[:], ps_tw[:]), inc=True, bank=b_tw)
                    kb.wait("S", "DV", v_tr)
                    for hin in range(NG):
                        wdst = o24[:, hin * 128:(hin + 1) * 128]
                        wsrc = wtok4[:, hin * 128:(hin + 1) * 128]
                        acc_sl = wsum4[:, hin * 4: hin * 4 + 1]
                        kb.op("S", lambda sc, wdst=wdst, wsrc=wsrc, acc_sl=acc_sl: sc.activation(
                            wdst, wsrc, AF.Square, accum_out=acc_sl))
                    kb.op("S", lambda sc: sc.activation(wrs4[:], wsum4[:], AF.Ln, bias=b_l2eps))
                    s_wn = kb.op("S", lambda sc: sc.activation(wrs4[:], wrs4[:], AF.Exp, scale=-0.5), inc=True)
                    kb.wait("V", "DS", max(s_wn, s_cum))
                    wrs_bc = wrs4.rearrange("p (h s) -> p h s", s=4)[:, :, 0:1].broadcast_to([128, NG, 128])
                    kb.op("V", lambda v, wrs_bc=wrs_bc: v.tensor_tensor(
                        wn4[:].rearrange("p (h m) -> p h m", h=NG), 
                        wtok4[:].rearrange("p (h m) -> p h m", h=NG), wrs_bc, ALU.mult))
                    bsl = beta_all[:, cc * 16 + grp * 4: cc * 16 + grp * 4 + 4]
                    beta_bc = bsl.rearrange("p (h s) -> p h s", s=1).broadcast_to([128, NG, 128])
                    kb.op("V", lambda v, beta_bc=beta_bc: v.tensor_tensor(
                        wn4[:].rearrange("p (h m) -> p h m", h=NG),
                        wn4[:].rearrange("p (h m) -> p h m", h=NG), beta_bc, ALU.mult))
                    kb.op("V", lambda v: v.tensor_tensor(u4[:], wn4[:], eu[:], ALU.mult))
                    v_ur = kb.op("V", lambda v: v.tensor_tensor(r4[:], wn4[:], rfac[:], ALU.mult), inc=True)
                    b_a, ps_a, w8 = pool.acquire()
                    for hin in range(NG):
                        ksl = kf[:, hin * TH + tc0: hin * TH + tc0 + 128]
                        qsl = qf[:, hin * TH + tc0: hin * TH + tc0 + 128]
                        kb.op("T", lambda t, ps_a=ps_a, hin=hin, ksl=ksl, qsl=qsl: t.matmul(
                            ps_a[:, hin * 128:(hin + 1) * 128], ksl, qsl,
                            start=True, stop=True), wait=w8 if hin == 0 else None)
                    t_a = kb.op("T", lambda t: t.drain(), inc=True)
                    kb.wait("V", "DT", t_a)
                    v_am = kb.op("V", lambda v, ps_a=ps_a: v.tensor_tensor(
                        am4[:].rearrange("p (h t) -> p h t", h=NG),
                        ps_a[:].rearrange("p (h t) -> p h t", h=NG),
                        mask01[:].unsqueeze(1).broadcast_to([128, NG, 128]), ALU.mult), inc=True, bank=b_a)
                    b_i, ps_i, w8i = pool.acquire()
                    b_j, ps_j, w8j = pool.acquire()
                    kb.wait("T", "DV", max(v_am, v_ur))
                    for hin in range(NG):
                        hs_ = slice(hin * 128, (hin + 1) * 128)
                        qsl = qf[:, hin * TH + tc0: hin * TH + tc0 + 128]
                        sk_sl = Sk[:, (grp * NG + hin) * 128:(grp * NG + hin) * 128 + 128]
                        kb.op("T", lambda t, ps_i=ps_i, hs_=hs_: t.matmul(
                            ps_i[:, hs_], am4[:, hs_], u4[:, hs_],
                            start=True, stop=True), wait=w8i if hin == 0 else None)
                        kb.op("T", lambda t, ps_j=ps_j, hs_=hs_, qsl=qsl, sk_sl=sk_sl: t.matmul(
                            ps_j[:, hs_], qsl, sk_sl,
                            start=True, stop=True), wait=w8j if hin == 0 else None)
                    t_sc = kb.op("T", lambda t: t.drain(), inc=True)
                    kb.wait("V", "DT", t_sc)
                    v_j = kb.op("V", lambda v, ps_j=ps_j: v.tensor_tensor(st14[:], lam_f[:], ps_j[:], ALU.mult), inc=True, bank=b_j)
                    v_i = kb.op("V", lambda v, ps_i=ps_i: v.scalar_tensor_tensor(
                        st24[:], ps_i[:], SCALE, lam_c[:], ALU.mult, ALU.mult), inc=True, bank=b_i)
                    v_sc = kb.op("V", lambda v: v.scalar_tensor_tensor(
                        st24[:], st14[:], SCALE, st24[:], ALU.mult, ALU.add), inc=True)
                    st3d = st24[:].rearrange("p (h m) -> p h m", h=NG)
                    nmx_sl = nmx4.rearrange("p (h s) -> p h s", s=4)[:, :, 0:1]
                    kb.op("V", lambda v, st3d=st3d, nmx_sl=nmx_sl: v.reduce_max(
                        nmx_sl, st3d, axis=AX.X, negate=True))
                    kb.op("V", lambda v: v.drain())
                    nmx_bc = nmx4.rearrange("p (h s) -> p h s", s=4)[:, :, 0:1].broadcast_to([128, NG, 128])
                    v_nm = kb.op("V", lambda v, st3d=st3d, nmx_bc=nmx_bc: v.tensor_tensor(
                        st3d, st3d, nmx_bc, ALU.add), inc=True)
                    kb.wait("S", "DV", v_nm)
                    s_es = kb.op("S", lambda sc: sc.activation(st24[:], st24[:], AF.Exp), inc=True)
                    kb.wait("V", "DS", s_es)
                    smr_sl = smr4.rearrange("p (h s) -> p h s", s=4)[:, :, 0:1]
                    kb.op("V", lambda v, st3d=st3d, smr_sl=smr_sl: v.reduce_sum(smr_sl, st3d, axis=AX.X))
                    kb.op("V", lambda v: v.drain())
                    kb.op("V", lambda v: v.reciprocal(smr4[:], smr4[:]))
                    kb.op("V", lambda v: v.drain())
                    smr_bc = smr4.rearrange("p (h s) -> p h s", s=4)[:, :, 0:1].broadcast_to([128, NG, 128])
                    kb.op("V", lambda v, st3d=st3d, smr_bc=smr_bc: v.tensor_tensor(
                        st14[:].rearrange("p (h m) -> p h m", h=NG), st3d, smr_bc, ALU.mult))
                    kb.op("V", lambda v: v.tensor_tensor(ac4[:], st14[:], lam_c[:], ALU.mult))
                    v_a2 = kb.op("V", lambda v: v.tensor_tensor(af4[:], st14[:], lam_f[:], ALU.mult), inc=True)
                    b_t2, ps_t2, w8 = pool.acquire()
                    b_t3, ps_t3, w83 = pool.acquire()
                    b_t4, ps_t4, w84 = pool.acquire()
                    kb.wait("T", "DV", v_a2)
                    for hin in range(NG):
                        hs_ = slice(hin * 128, (hin + 1) * 128)
                        kb.op("T", lambda t, ps_t2=ps_t2, hs_=hs_: t.matmul(
                            ps_t2[:, hs_], ac4[:, hs_], ident_b[:], start=True, stop=True),
                            wait=w8 if hin == 0 else None)
                        kb.op("T", lambda t, ps_t3=ps_t3, hs_=hs_: t.matmul(
                            ps_t3[:, hs_], af4[:, hs_], ident_b[:], start=True, stop=True),
                            wait=w83 if hin == 0 else None)
                        kb.op("T", lambda t, ps_t4=ps_t4, hs_=hs_: t.matmul(
                            ps_t4[:, hs_], u4[:, hs_], ident_b[:], start=True, stop=True),
                            wait=w84 if hin == 0 else None)
                    t_t2 = kb.op("T", lambda t: t.drain(), inc=True)
                    kb.wait("V", "DT", t_t2)
                    v_c1 = kb.op("V", lambda v, ps_t2=ps_t2: v.tensor_copy(acT4[:], ps_t2[:]), inc=True, bank=b_t2)
                    v_c2 = kb.op("V", lambda v, ps_t3=ps_t3: v.tensor_copy(afT4[:], ps_t3[:]), inc=True, bank=b_t3)
                    v_t2 = kb.op("V", lambda v, ps_t4=ps_t4: v.tensor_copy(uT4[:], ps_t4[:]), inc=True, bank=b_t4)
                    b_pt, ps_pt, w8 = pool.acquire()
                    kb.wait("T", "DV", v_t2)
                    for hin in range(NG):
                        hs_ = slice(hin * 128, (hin + 1) * 128)
                        kb.op("T", lambda t, ps_pt=ps_pt, hs_=hs_: t.matmul(
                            ps_pt[:, hs_], uT4[:, hs_], acT4[:, hs_], start=True, stop=True),
                            wait=w8 if hin == 0 else None)
                    t_pt = kb.op("T", lambda t: t.drain(), inc=True)
                    kb.wait("V", "DT", t_pt)
                    v_pm = kb.op("V", lambda v, ps_pt=ps_pt: v.tensor_tensor(
                        pm4[:].rearrange("p (h t) -> p h t", h=NG),
                        ps_pt[:].rearrange("p (h t) -> p h t", h=NG),
                        mask01[:].unsqueeze(1).broadcast_to([128, NG, 128]), ALU.mult), inc=True, bank=b_pt)
                    b_o, ps_o, w8 = pool.acquire()
                    kb.wait("T", "DV", v_pm)
                    for hin in range(NG):
                        hs_ = slice(hin * 128, (hin + 1) * 128)
                        sv_sl = Sv[:, (grp * NG + hin) * 128:(grp * NG + hin) * 128 + 128]
                        kb.op("T", lambda t, ps_o=ps_o, hs_=hs_: t.matmul(
                            ps_o[:, hs_], vtok4[:, hs_], pm4[:, hs_], start=True, stop=False),
                            wait=w8 if hin == 0 else None)
                        kb.op("T", lambda t, ps_o=ps_o, hs_=hs_, sv_sl=sv_sl: t.matmul(
                            ps_o[:, hs_], sv_sl, afT4[:, hs_], start=False, stop=True))
                    t_o = kb.op("T", lambda t: t.drain(), inc=True)
                    b_su, ps_su, w8 = pool.acquire()
                    b_sb_, ps_sb_, w8b = pool.acquire()
                    b_sv, ps_sv, w8v = pool.acquire()
                    for hin in range(NG):
                        hs_ = slice(hin * 128, (hin + 1) * 128)
                        rowsl = elam_row[0:1, hin * 128:(hin + 1) * 128]
                        kb.op("T", lambda t, ps_su=ps_su, hs_=hs_: t.matmul(
                            ps_su[:, hs_], ktok4[:, hs_], r4[:, hs_], start=True, stop=True),
                            wait=w8 if hin == 0 else None)
                        kb.op("T", lambda t, ps_sb_=ps_sb_, hs_=hs_, rowsl=rowsl: t.matmul(
                            ps_sb_[:, hs_], ones_1b[:], rowsl, start=True, stop=True),
                            wait=w8b if hin == 0 else None)
                        kb.op("T", lambda t, ps_sv=ps_sv, hs_=hs_: t.matmul(
                            ps_sv[:, hs_], r4[:, hs_], vtok4[:, hs_], start=True, stop=True),
                            wait=w8v if hin == 0 else None)
                    t_su = kb.op("T", lambda t: t.drain(), inc=True)
                    kb.wait("V", "DT", t_su)
                    v_s1 = kb.op("V", lambda v, ps_sb_=ps_sb_, sk_g=sk_g: v.tensor_tensor(sktmp4[:], sk_g, ps_sb_[:], ALU.mult), inc=True, bank=b_sb_)
                    v_s2 = kb.op("V", lambda v, ps_su=ps_su, sk_g=sk_g: v.tensor_tensor(sk_g, sktmp4[:], ps_su[:], ALU.add), inc=True, bank=b_su)
                    elam_bc = elam_col[:].rearrange("p (h s) -> p h s", s=4)[:, :, 0:1].broadcast_to([128, NG, 128])
                    kb.op("V", lambda v, elam_bc=elam_bc, sv_g=sv_g: v.tensor_tensor(
                        sktmp4[:].rearrange("p (h m) -> p h m", h=NG),
                        sv_g.rearrange("p (h m) -> p h m", h=NG), elam_bc, ALU.mult))
                    v_su = kb.op("V", lambda v, ps_sv=ps_sv, sv_g=sv_g: v.tensor_tensor(
                        sv_g, sktmp4[:], ps_sv[:], ALU.add), inc=True, bank=b_sv)
                    kb.wait("V", "DT", t_o)
                    v_ocp = kb.op("V", lambda v, ps_o=ps_o: v.tensor_copy(o4[:], ps_o[:]), inc=True, bank=b_o)
                    kb.wait("S", "DV", v_ocp)
                    s_o2 = kb.op("S", lambda sc: sc.activation(o24[:], o4[:], AF.Square), inc=True)
                    b_r, ps_r, w8 = pool.acquire()
                    b_g, ps_g, w8g = pool.acquire()
                    kb.wait("T", "DS", s_o2)
                    kb.op("T", lambda t, ps_r=ps_r: t.matmul(
                        ps_r[0:1, :], ones_cb[:], o24[:], start=True, stop=True), wait=w8)
                    gsl = gpre2[:, tc0:tc0 + 128]
                    for hin in range(NG):
                        hs_ = slice(hin * 128, (hin + 1) * 128)
                        kb.op("T", lambda t, ps_g=ps_g, hs_=hs_, gsl=gsl: t.matmul(
                            ps_g[:, hs_], wg2g[:, hs_], gsl, start=True, stop=True),
                            wait=w8g if hin == 0 else None)
                    t_r = kb.op("T", lambda t: t.drain(), inc=True)
                    kb.wait("S", "DT", t_r)
                    s_ln = kb.op("S", lambda sc, ps_r=ps_r: sc.activation(
                        lrow, ps_r[0:1, :], AF.Ln, bias=b_rmse[0:1, :]), inc=True, bank=b_r)
                    kb.op("S", lambda sc: sc.activation(
                        rrow, lrow, AF.Exp, scale=-0.5, bias=b_hl128[0:1, :]))
                    s_eg2 = kb.op("S", lambda sc, ps_g=ps_g: sc.activation(
                        eg4[:], ps_g[:], AF.Exp, scale=-1.0), inc=True, bank=b_g)
                    b_rb, ps_rb, w8 = pool.acquire()
                    kb.wait("T", "DS", s_eg2)
                    kb.op("T", lambda t, ps_rb=ps_rb: t.matmul(
                        ps_rb[:], ones_1b[:], rrow, start=True, stop=True), wait=w8)
                    t_rb = kb.op("T", lambda t: t.drain(), inc=True)
                    kb.wait("V", "DS", s_eg2)
                    ebg_sl = ebg[:, grp * 4: grp * 4 + 4]
                    ebg_b = ebg_sl.rearrange("p (h s) -> p h s", s=1).broadcast_to([128, NG, 128])
                    kb.op("V", lambda v, ebg_b=ebg_b: v.tensor_tensor(
                        eg4[:].rearrange("p (h t) -> p h t", h=NG),
                        eg4[:].rearrange("p (h t) -> p h t", h=NG), ebg_b, ALU.mult))
                    kb.op("V", lambda v: v.tensor_scalar_add(st24[:], eg4[:], 1.0))
                    kb.op("V", lambda v: v.drain())
                    kb.op("V", lambda v: v.reciprocal(st24[:], st24[:]))
                    kb.wait("V", "DT", t_rb)
                    v_rb = kb.op("V", lambda v, ps_rb=ps_rb: v.tensor_tensor(o4[:], o4[:], ps_rb[:], ALU.mult), inc=True, bank=b_rb)
                    dsto = o_grp[:].rearrange("p (h t) -> p h t", h=NG)[:, :, tc0:tc0 + 128]
                    v_of = kb.op("V", lambda v, dsto=dsto: v.scalar_tensor_tensor(
                        dsto, o4[:].rearrange("p (h t) -> p h t", h=NG), normw[:],
                        st24[:].rearrange("p (h t) -> p h t", h=NG), ALU.mult, ALU.mult), inc=True)

                if stop_after == "scan" and grp == 0:
                    dbg_taps.extend([o_grp, Sk, Sv])
                    break

                scan_done_v = kb.c("DV")
                for dj in range(NDCH):
                    bi = dj % 2
                    wbuf = wp[bi]
                    src = scr["wo"].rearrange("(j p) c -> p j c", p=128)[
                        :, grp * NG:(grp + 1) * NG, dj * 128:(dj + 1) * 128]
                    yw = kb.dma("Y", wbuf[:, 0:512], src,
                                wait=[scrw["wo"], ("DT", wp_free[bi])])
                    for tt in range(2):
                        b, ps, w8 = pool.acquire()
                        kb.waitt("T", yw)
                        kb.wait("T", "DV", scan_done_v)
                        for hin in range(NG):
                            osl = o_grp[:, hin * TH + tt * 512: hin * TH + (tt + 1) * 512]
                            kb.op("T", lambda t, ps=ps, wbuf=wbuf, hin=hin, osl=osl: t.matmul(
                                ps[:], wbuf[:, hin * 128:(hin + 1) * 128], osl,
                                start=(hin == 0), stop=(hin == NG - 1)),
                                wait=w8 if hin == 0 else None)
                        t_wo = kb.op("T", lambda t: t.drain(), inc=True)
                        if tt == 1:
                            wp_free[bi] = t_wo
                        asl = acc[:, dj * TH + tt * 512: dj * TH + (tt + 1) * 512]
                        kb.wait("V", "DT", t_wo)
                        if grp == 0:
                            v_ev = kb.op("V", lambda v, asl=asl, ps=ps: v.tensor_copy(asl, ps[:]), inc=True, bank=b)
                        else:
                            v_ev = kb.op("V", lambda v, asl=asl, ps=ps: v.tensor_tensor(
                                asl, asl, ps[:], ALU.add), inc=True, bank=b)

            if stop_after in ("proj", "scan"):
                break
            if stop_after == "wo":
                dbg_taps.extend([acc, Sk, Sv, o_grp])
                break

            last_vhs = 0
            ffn_t0 = kb.c("DT")
            nat_last_tr = [ffn_t0, ffn_t0]
            natb = [big[:, 13312:16128], big[:, 16128:18944]]

            units = []
            groups = []
            for ex in range(NEXP):
                if ex < E_MOE:
                    w13src, w2src = scr["w13"][ex], scr["w2"][ex]
                    gup_off, up_off = 0, I_MOE
                    dn_col0 = 0
                    wtag13, wtag2 = f"w13_{ex}", f"w2_{ex}"
                else:
                    qq = ex - E_MOE
                    w13src, w2src = scr["sgu"], scr["sdw"]
                    gup_off, up_off = qq * I_MOE, I_SH + qq * I_MOE
                    dn_col0 = qq * I_MOE
                    wtag13, wtag2 = "sgu", "sdw"
                for sup in range(6):
                    ic0 = sup * 2
                    nch_s = 2 if sup < 5 else 1
                    for gu in range(2):
                        roff = (gup_off if gu == 0 else up_off) + ic0 * 128
                        u0 = len(units)
                        sbi = len(groups) % 2
                        if nch_s == 2:
                            for h in range(2):
                                banks = []
                                for a in range(4):
                                    pieces = [2 * a * 128, 1024 + 2 * a * 128,
                                              (2 * a + 1) * 128,
                                              1024 + (2 * a + 1) * 128]
                                    banks.append((512, (h * 8 + 2 * a) * 256, pieces))
                                units.append(dict(
                                    src=w13src[roff: roff + 256,
                                               h * 1024:(h + 1) * 1024]
                                    .rearrange("(q p) d -> p q d", p=128),
                                    ncols=2048, qw=1024, tag=wtag13,
                                    banks=banks, slab_t=upslab, sbi=sbi))
                        else:
                            banks = [(512, a * 512,
                                      [(4 * a + m) * 128 for m in range(4)])
                                     for a in range(4)]
                            units.append(dict(
                                src=w13src[roff: roff + 128, :],
                                ncols=2048, qw=0, tag=wtag13,
                                banks=banks, slab_t=upslab, sbi=sbi))
                        groups.append(dict(kind="up", ex=ex, sup=sup, gu=gu,
                                           ic0=ic0, nch_s=nch_s,
                                           u0=u0, u1=len(units)))
                for hd in range(8):
                    u0 = len(units)
                    sbi = len(groups) % 2
                    banks = []
                    for icc in range(5):
                        pieces = [2 * icc * 128, 1408 + 2 * icc * 128,
                                  (2 * icc + 1) * 128, 1408 + (2 * icc + 1) * 128]
                        banks.append((512, icc * 512, pieces))
                    banks.append((256, 2560, [10 * 128, 1408 + 10 * 128]))
                    units.append(dict(
                        src=w2src[hd * 256: hd * 256 + 256,
                                  dn_col0: dn_col0 + 1408]
                        .rearrange("(q p) c -> p q c", p=128),
                        ncols=2816, qw=1408, tag=wtag2,
                        banks=banks, slab_t=dnslab, sbi=sbi))
                    groups.append(dict(kind="dn", ex=ex, hd0=hd * 2,
                                       u0=u0, u1=len(units)))
            for i, u in enumerate(units):
                u["idx"] = i

            def emit_unit_dma(u):
                np_ = u["idx"] % 2
                dst = natb[np_][:, 0:u["ncols"]]
                if u["qw"]:
                    dst = dst.rearrange("p (q d) -> p q d", d=u["qw"])
                u["dma_tok"] = kb.dma1("Y", dst, u["src"],
                                       wait=[scrw[u["tag"]],
                                             ("DT", nat_last_tr[np_])])

            def emit_unit_tr(u):
                np_ = u["idx"] % 2
                slabv = u["slab_t"][u["sbi"]]
                binfo = []
                for bidx, (w, soff, pieces) in enumerate(u["banks"]):
                    b, ps, w8 = pool.acquire()
                    for k, noff in enumerate(pieces):
                        nsl = natb[np_][:, noff: noff + 128]
                        wl = list(w8) if k == 0 else []
                        if bidx == 0 and k == 0:
                            wl = wl + [u["dma_tok"]]
                        kb.op("T", lambda t, ps=ps, k=k, nsl=nsl: t.matmul(
                            ps[:, k * 128:(k + 1) * 128], nsl, ident_b[:],
                            start=True, stop=True), wait=wl or None)
                    binfo.append((b, ps, w, soff))
                trd = kb.op("T", lambda t: t.drain(), inc=True)
                nat_last_tr[np_] = trd
                ev = 0
                for (b, ps, w, soff) in binfo:
                    dstv = slabv[:, soff: soff + w]
                    ev = kb.op("S", lambda s, dstv=dstv, ps=ps, w=w: s.activation(
                        dstv, ps[:, 0:w], AF.Copy),
                        wait=[("DT", trd)], inc=True, bank=b)
                u["ev"] = ev

            ud = ut = 0
            cb_tr_prev = 0
            v_cb = 0
            t_cb = 0
            silu_ds = {}
            up_done_v = 0
            cur_ex = -1
            ng = len(groups)
            nu = len(units)
            for gi, g in enumerate(groups):
                tgt_t = groups[min(gi + 1, ng - 1)]["u1"]
                while ut < tgt_t:
                    while ud < min(ut + 2, nu):
                        emit_unit_dma(units[ud])
                        ud += 1
                    emit_unit_tr(units[ut])
                    ut += 1
                tgt_d = min(groups[min(gi + 2, ng - 1)]["u1"], ut + 2, nu)
                while ud < tgt_d:
                    emit_unit_dma(units[ud])
                    ud += 1
                if g["ex"] != cur_ex:
                    cur_ex = g["ex"]
                    cf_row = cur_ex if cur_ex < E_MOE else 16
                    if cur_ex == 0:
                        wl = [("DV", phaseB_V), ("DT", kb.c("DT"))]
                    else:
                        wl = [("DT", cb_tr_prev)]
                    y_cr = kb.dma("Y", coef_row[:], coefT[cf_row:cf_row + 1, :],
                                  wait=wl)
                    for tt in range(2):
                        b_cb, ps_cb, w8 = pool.acquire()
                        kb.waitt("T", y_cr)
                        rsl = coef_row[0:1, tt * 512:(tt + 1) * 512]
                        kb.op("T", lambda t, ps_cb=ps_cb, rsl=rsl: t.matmul(
                            ps_cb[:], ones_1b[:], rsl, start=True, stop=True), wait=w8)
                        t_cb = kb.op("T", lambda t: t.drain(), inc=True)
                        dstcb = coef_bc[:, tt * 512:(tt + 1) * 512]
                        v_cb = kb.op("V", lambda v, dstcb=dstcb, ps_cb=ps_cb: v.tensor_copy(
                            dstcb, ps_cb[:]), wait=[("DT", t_cb)], inc=True, bank=b_cb)
                    cb_tr_prev = t_cb
                    silu_ds = {}
                sbi = gi % 2
                kb.wait("T", "DS", units[g["u1"] - 1]["ev"])
                if g["kind"] == "up":
                    slabv = upslab[sbi]
                    nch_s, ic0, gu, sup = g["nch_s"], g["ic0"], g["gu"], g["sup"]
                    pst = 256 if nch_s == 2 else 128
                    t_last = 0
                    for icx in range(nch_s):
                        for tt in range(2):
                            b, ps, w8 = pool.acquire()
                            for j in range(NDCH):
                                xsl = xnT[:, j * TH + tt * 512: j * TH + (tt + 1) * 512]
                                wslab = slabv[:, j * pst + icx * 128: j * pst + icx * 128 + 128]
                                kb.op("T", lambda t, ps=ps, wslab=wslab, xsl=xsl, j=j: t.matmul(
                                    ps[:], wslab, xsl, start=(j == 0), stop=(j == NDCH - 1)),
                                    wait=w8 if j == 0 else None)
                            t_up = kb.op("T", lambda t: t.drain(), inc=True)
                            t_last = t_up
                            if gu == 0:
                                dst = sgtmp[:, icx * TH + tt * 512: icx * TH + (tt + 1) * 512]
                                s_ev = kb.op("S", lambda s, dst=dst, ps=ps: s.activation(
                                    dst, ps[:], AF.Silu),
                                    wait=[("DT", t_up), ("DV", last_vhs)], inc=True, bank=b)
                                silu_ds[(sup, icx, tt)] = s_ev
                            else:
                                hsl = hs[:, (ic0 + icx) * TH + tt * 512: (ic0 + icx) * TH + (tt + 1) * 512]
                                ssl = sgtmp[:, icx * TH + tt * 512: icx * TH + (tt + 1) * 512]
                                cbsl = coef_bc[:, tt * 512:(tt + 1) * 512]
                                kb.wait("V", "DT", t_up)
                                kb.wait("V", "DV", v_cb)
                                kb.wait("V", "DS", silu_ds[(sup, icx, tt)])
                                v_h1 = kb.op("V", lambda v, hsl=hsl, ssl=ssl, ps=ps: v.tensor_tensor(
                                    hsl, ssl, ps[:], ALU.mult), inc=True, bank=b)
                                v_hs = kb.op("V", lambda v, hsl=hsl, cbsl=cbsl: v.tensor_tensor(
                                    hsl, hsl, cbsl, ALU.mult), inc=True)
                                last_vhs = v_hs
                    slab_free[sbi] = t_last
                else:
                    slabv = dnslab[sbi]
                    if g["hd0"] == 0:
                        up_done_v = kb.c("DV")
                    hd0 = g["hd0"]
                    t_last = 0
                    for hdc in range(2):
                        for tt in range(2):
                            b, ps, w8 = pool.acquire()
                            kb.wait("T", "DV", up_done_v)
                            for ic in range(11):
                                hsl = hs[:, ic * TH + tt * 512: ic * TH + (tt + 1) * 512]
                                wslab = slabv[:, ic * 256 + hdc * 128: ic * 256 + hdc * 128 + 128]
                                kb.op("T", lambda t, ps=ps, wslab=wslab, hsl=hsl, ic=ic: t.matmul(
                                    ps[:], wslab, hsl, start=(ic == 0), stop=(ic == 10)),
                                    wait=w8 if ic == 0 else None)
                            t_dn = kb.op("T", lambda t: t.drain(), inc=True)
                            t_last = t_dn
                            asl = acc[:, (hd0 + hdc) * TH + tt * 512: (hd0 + hdc) * TH + (tt + 1) * 512]
                            kb.wait("V", "DT", t_dn)
                            v_dn = kb.op("V", lambda v, asl=asl, ps=ps: v.tensor_tensor(
                                asl, asl, ps[:], ALU.add), inc=True, bank=b)
                    slab_free[sbi] = t_last

            if stop_after == "ffn":
                dbg_taps.append(acc)
                break

            ffn_done_v = kb.c("DV")
            for tc in range(NCH):
                xb = x_tok_e
                yx = kb.dma("Y", xb[:], x_d[th0 + tc * 128: th0 + (tc + 1) * 128, :],
                            wait=[("QS", kb.c("QS")), ("DT", kb.c("DT"))] if tc == 0 else [("QS", kb.c("QS"))])
                for bq in range(4):
                    b, ps, w8 = pool.acquire()
                    kb.wait("T", "DV", ffn_done_v)
                    for jj in range(4):
                        dj = bq * 4 + jj
                        asl = acc[:, dj * TH + tc * 128: dj * TH + (tc + 1) * 128]
                        kb.op("T", lambda t, ps=ps, jj=jj, asl=asl: t.matmul(
                            ps[:, jj * 128:(jj + 1) * 128], asl, ident_f[:],
                            start=True, stop=True), wait=w8 if jj == 0 else None)
                    t_tr = kb.op("T", lambda t: t.drain(), inc=True)
                    kb.wait("V", "DT", t_tr)
                    kb.waitt("V", yx)
                    xsl = xb[:, bq * 512:(bq + 1) * 512]
                    v_ev = kb.op("V", lambda v, xsl=xsl, ps=ps: v.tensor_tensor(
                        xsl, xsl, ps[:], ALU.add), inc=True, bank=b)
                kb.dma("S", out_d[th0 + tc * 128: th0 + (tc + 1) * 128, :], xb[:],
                       wait=[("DV", kb.c("DV"))])

        if stop_after is not None:
            kb.wait("G", "DV", kb.c("DV"))
            kb.wait("G", "DS", kb.c("DS"))
            kb.wait("G", "DT", kb.c("DT"))
            row = 0
            for tens in dbg_taps:
                ap = tens if isinstance(tens, bass.AP) else tens[:]
                P, FREE = ap.shape
                npieces = (FREE + D - 1) // D
                for pc in range(npieces):
                    w = min(D, FREE - pc * D)
                    if row + P <= T:
                        kb.dma("G", out_d[row:row + P, 0:w], ap[:, pc * D:pc * D + w])
                    row += 128

        kb.wait("G", "QS", kb.c("QS"))
        for i in range(KB.NYS):
            kb.wait("G", f"QY{i}", kb.c(f"QY{i}"))
        kb.wait("G", "QG", kb.c("QG"))

        build_program.last_kb = kb
        block = E(nc.Block())
        block.sync(lambda e: [f(e) for f in kb.ops["Y"]])
        block.gpsimd(lambda e: [f(e) for f in kb.ops["G"]])
        block.vector(lambda e: [f(e) for f in kb.ops["V"]])
        block.scalar(lambda e: [f(e) for f in kb.ops["S"]])
        block.tensor(lambda e: [f(e) for f in kb.ops["T"]])

    return nc


def kernel(**inputs):
    x = np.ascontiguousarray(np.asarray(inputs["L0_attn_hidden_states"], np.float32))
    nc = build_program()
    in_maps = []
    for b in range(8):
        m = {"x": np.ascontiguousarray(x[b])}
        for n in W_NAMES:
            m[n] = np.ascontiguousarray(np.asarray(inputs[n], np.float32))
        in_maps.append(m)
    res = run_bass_kernel_spmd(nc, in_maps, core_ids=list(range(8)))
    out = np.stack([np.asarray(r["out"], np.float32) for r in res.results], axis=0)
    return out
